# version 11
# baseline (speedup 1.0000x reference)
import numpy as np
import ml_dtypes
import concourse.bass as bass
import concourse.mybir as mybir
from concourse.bass_utils import run_bass_kernel_spmd
from concourse import library_config
from contextlib import ExitStack

F32 = mybir.dt.float32
BF16 = mybir.dt.bfloat16
I16 = mybir.dt.int16
U32 = mybir.dt.uint32
AF = mybir.ActivationFunctionType
ALU = mybir.AluOpType
NPBF = ml_dtypes.bfloat16
X_AX = mybir.AxisListType.X

NC, B, T, D, H, HD = 8, 2, 16384, 1024, 16, 64
TL, CH = 4096, 128
NCH = TL // CH
NE, CAP, MID = 8, 4096, 4096
NTOK = B * T
LN_EPS = 1e-5
TTILE = 512
NTILE = TL // TTILE
SUMROWS = H * (HD + 1) + 1
ENGS = ("sync", "tensor", "vector", "scalar", "gpsimd")
GINC = 16


class Prog:
    def __init__(self):
        self.q = {e: [] for e in ENGS}
        self.semcnt = {}

    def emit(self, eng, fn, sem=None, dn=0):
        if sem is not None:
            v = self.inc(sem, dn)
            self.q[eng].append(lambda e, fn=fn, sem=sem, dn=dn: fn(e).then_inc(sem, dn))
            return v
        self.q[eng].append(fn)
        return None

    def inc(self, sem, n):
        self.semcnt[sem.name] = self.semcnt.get(sem.name, 0) + n
        return self.semcnt[sem.name]

    def cur(self, sem):
        return self.semcnt.get(sem.name, 0)

    def wait(self, eng, sem, v=None):
        if v is None:
            v = self.cur(sem)
        self.q[eng].append(lambda e, sem=sem, v=v: e.wait_ge(sem, v))
        return v


def build_program(dbg=()):
    nc = bass.Bass("TRN2", target_bir_lowering=False)
    p = Prog()
    ctx = ExitStack()
    sb = lambda name, shape, dt: ctx.enter_context(nc.sbuf_tensor(name, shape, dt))
    ps_ = lambda name, shape: ctx.enter_context(nc.psum_tensor(name, shape, F32))
    mksem = lambda name: ctx.enter_context(nc.semaphore(name))

    def dram(name, shape, dt, kind=None):
        if kind:
            return nc.dram_tensor(name, shape, dt, kind=kind).ap()
        return nc.dram_tensor(name, shape, dt).ap()

    class DS:

        def __init__(self, name, nsems=1):
            self.sems = [mksem(f"{name}{k}") for k in range(nsems)]

        def issue(self, eng, fn, idx=0, dn=16):
            s = self.sems[idx % len(self.sems)]
            p.q[eng].append(lambda e, fn=fn, s=s: fn(e).then_inc(s, 16))
            v = p.inc(s, dn)
            return (s, v)

        def tok(self, idx=0):
            s = self.sems[idx % len(self.sems)]
            return (s, p.cur(s))

    def dwait(eng, token):
        p.wait(eng, token[0], token[1])

    x_h = dram("x", [TL, D], F32, "ExternalInput")
    wq_h = dram("wq", [D, D], F32, "ExternalInput")
    wk_h = dram("wk", [D, D], F32, "ExternalInput")
    wv_h = dram("wv", [D, D], F32, "ExternalInput")
    wo_h = dram("wo", [D, D], F32, "ExternalInput")
    bo_h = dram("bo", [1, D], F32, "ExternalInput")
    wif_h = dram("wif", [D, 2 * H], F32, "ExternalInput")
    bif_h = dram("bif", [2 * H, 1], F32, "ExternalInput")
    rwt_h = dram("rwt", [D, NE], F32, "ExternalInput")
    rb_h = dram("rb", [NE, 1], F32, "ExternalInput")
    gam_h = dram("gam", [1, D], F32, "ExternalInput")
    bet_h = dram("bet", [1, D], F32, "ExternalInput")
    w1_h = dram("w1", [D, MID], F32, "ExternalInput")
    w2_h = dram("w2", [MID, D], F32, "ExternalInput")
    ident_h = dram("ident", [128, 128], F32, "ExternalInput")
    tristb_h = dram("tristb", [128, 128], BF16, "ExternalInput")
    tri01_h = dram("tri01", [128, H * CH], BF16, "ExternalInput")
    bdiag16_h = dram("bdiag16", [H, H * CH], BF16, "ExternalInput")
    negoh16_h = dram("negoh16", [H, H * CH], BF16, "ExternalInput")
    bdiag32_h = dram("bdiag32", [32, 32 * H], F32, "ExternalInput")
    prevmask16_h = dram("prevmask16", [H, NC], F32, "ExternalInput")
    ohrep_h = dram("ohrep", [128, NE], F32, "ExternalInput")
    valid_h = dram("valid", [128, 1], F32, "ExternalInput")
    prevmask_h = dram("prevmask", [NC, 1], F32, "ExternalInput")
    prevall_h = dram("prevall", [NC, 1], F32, "ExternalInput")
    iotaj_h = dram("iotaj", [1, 256], F32, "ExternalInput")
    pcol1_h = dram("pcol1", [128, 1], F32, "ExternalInput")

    ag1a_in = dram("ag1a_in", [TL, D], BF16)
    ag1a_out = dram("ag1a_out", [NTOK, D], BF16)
    ag1b_in = dram("ag1b_in", [SUMROWS, HD], F32)
    ag1b_out = dram("ag1b_out", [NC * SUMROWS, HD], F32)
    ag1c_in = dram("ag1c_in", [TL, NE], F32)
    ag1c_out = dram("ag1c_out", [NTOK, NE], F32)
    lmrow_h = dram("lmrow", [8, 512], F32)
    ag2_in = dram("ag2_in", [CAP, D], BF16)
    ag2_out = dram("ag2_out", [NC * CAP, D], BF16)
    xnlo_hbm = dram("xnlo", [TL, D], BF16)
    qf_hbm = dram("qf", [D, TL], BF16)
    kf_hbm = dram("kf", [D, TL], BF16)
    vf_hbm = dram("vf", [D, TL], BF16)
    otm_hbm = dram("otm", [TL, D], BF16)
    hat_hbm = dram("hat", [TL, D], BF16)
    out_h = dram("out", [TL, D], F32, "ExternalOutput")
    scratch_map = dict(ag1a=ag1a_in, xnlo=xnlo_hbm, qf=qf_hbm, kf=kf_hbm,
                       vf=vf_hbm, otm=otm_hbm, ag1c=ag1c_in, ag1b=ag1b_in,
                       ag2in=ag2_in, ag1ao=ag1a_out, ag1bo=ag1b_out,
                       ag1co=ag1c_out, ag2o=ag2_out, hat=hat_hbm)
    dbg_h = {}

    cc = mksem("cc")
    es = {e: mksem("es_" + e) for e in ENGS}

    def mark(eng):
        if eng == "sync":
            return p.emit(eng, lambda x: x.nop(nofuse=True), es[eng], 1)
        return p.emit(eng, lambda x: x.drain(), es[eng], 1)

    def barrier(engines=ENGS, producers=ENGS):
        vals = {}
        for e in producers:
            vals[e] = mark(e)
        for e in engines:
            for pr in producers:
                if pr != e:
                    p.wait(e, es[pr], vals[pr])

    def dbg_dump(name, ap):
        if name not in dbg:
            return
        t = dram("dbg_" + name, list(ap.shape), ap.tensor.dtype, "ExternalOutput")
        dbg_h[name] = t
        vals = {e2: mark(e2) for e2 in ("vector", "scalar", "tensor", "gpsimd")}
        for e2, v in vals.items():
            p.wait("sync", es[e2], v)
        tk = st_dbg.issue("sync", lambda e, t=t, ap=ap: e.dma_start(out=t, in_=ap))
        for e2 in ("vector", "scalar", "tensor", "gpsimd"):
            dwait(e2, tk)

    def bc(main_ap, small_ap):
        a1, a2 = bass.broadcast_tensor_aps(main_ap, small_ap)
        return a2

    def vdrain():
        p.emit("vector", lambda e: e.drain())

    def sdrain():
        p.emit("scalar", lambda e: e.drain())

    rwh_sb = sb("rwh_sb", [128, 8, NE], BF16)
    rwl_sb = sb("rwl_sb", [128, 8, NE], BF16)
    rwf_sb = sb("rwf_sb", [128, 8, NE], F32)
    bo_sb = sb("bo_sb", [1, D], BF16)
    bif_sb = sb("bif_sb", [2 * H, 1], F32)
    bif_sb2 = sb("bif_sb2", [H, 1], F32)
    rb_sb = sb("rb_sb", [NE, 1], F32)
    ident_sb = sb("ident_sb", [128, 128], F32)
    trist_sb = sb("trist_sb", [128, 128], BF16)
    ohrep_sb = sb("ohrep_sb", [128, NE], F32)
    valid_sb = sb("valid_sb", [128, 1], F32)
    prevmask_sb = sb("prevmask_sb", [NC, 1], F32)
    prevall_sb = sb("prevall_sb", [NC, 1], F32)
    ones_sb = sb("ones_sb", [128, 128], F32)
    onesb_sb = sb("onesb_sb", [128, 128], BF16)
    lownT_sb = sb("lownT_sb", [128, NCH, NE], F32)
    ownsel_sb = sb("ownsel_sb", [128, NCH * NE], F32)
    sqdump = sb("sqdump", [128, D], BF16)
    eps_col = sb("eps_col", [128, 1], F32)
    scope_ab = ExitStack()
    sbab = lambda name, shape, dt: scope_ab.enter_context(nc.sbuf_tensor(name, shape, dt))
    gpool = sbab("gpool", [16, 4, TL], F32)
    gi_i = gpool[:, 0, :]
    omg_sb = gpool[:, 0, :]
    dec_sb = gpool[:, 0, :]
    gi_f = gpool[:, 1, :]
    aloc_sb = gpool[:, 1, :]
    denomI_sb = gpool[:, 1, :]
    logf_sb = gpool[0:8, 2, :]
    Fc_sb = gpool[:, 2, :]
    A_sb = gpool[:, 2, :]
    g_sb = gpool[:, 3, :]
    gpool2 = sbab("gpool2", [16, 512], F32)
    Lc_sb = gpool2[:, 0:NCH]
    Gc_sb = gpool2[:, 32:32 + NCH]
    prefL_sb = gpool2[:, 64:64 + NCH]
    Ghat_sb = gpool2[:, 96:97]
    Ftot_sb = gpool2[:, 97:98]
    GG_sb = gpool2[:, 98:130]
    po_sb = gpool2[:, 130:162]
    mu_sb = gpool2[:, 162:194]
    Mi_sb = gpool2[:, 194:195]
    Mx_sb = gpool2[:, 200:233]
    aarg_sb = gpool2[:, 240:272]
    barg_sb = gpool2[:, 272:304]
    bdiag16_sb = sbab("bdiag16_sb", [H, H * CH], BF16)
    negoh16_sb = sbab("negoh16_sb", [H, H * CH], BF16)
    bdiag32_sb = sbab("bdiag32_sb", [32, 32 * H], F32)
    prevmask16_sb = sbab("prevmask16_sb", [H, NC], F32)
    tri01_sb = sbab("tri01_sb", [128, H * CH], BF16)
    gT_sb = sbab("gT_sb", [128, NCH, H], F32)

    psA = ps_("psA", [128, 512])
    psB = ps_("psB", [128, 512])
    psC = ps_("psC", [128, 512])
    psD = ps_("psD", [128, 512])
    psE = ps_("psE", [128, 512])
    psF = ps_("psF", [128, 512])
    psG = ps_("psG", [128, 512])
    psH = ps_("psH", [128, 512])
    bank_guard = {}

    def guard(eng, bank):
        gv = bank_guard.get(id(bank))
        if gv is not None:
            p.wait(eng, gv[0], gv[1])

    def consumed(bank, semx, v=None):
        if v is None:
            v = p.cur(semx)
        bank_guard[id(bank)] = (semx, v)

    st_init = DS("sIni")
    st_initg = DS("sInG")
    st_a0x = DS("sA0x", 2)
    st_a0s = DS("sA0s", 2)
    st_ag1a = DS("sAGa")
    st_a1x = DS("sA1x", 2)
    st_qkv = DS("sQkv")
    st_o = DS("sO", 2)
    st_ag1c = DS("sAGc")
    st_a2kv = DS("sA2k", 2)
    st_scal = DS("sScl")
    st_ag1b = DS("sAGb")
    st_dbg = DS("sDbg")
    st_bp = DS("sBp")
    st_bp2 = DS("sBp2")
    st_bq = DS("sBq", 2)
    st_bk = DS("sBk", 2)
    st_bo = DS("sBo", 2)
    st_bh = DS("sBh", 2)
    st_dbgg = DS("sDbgG")
    st_cw = DS("sCw")
    st_cr = DS("sCr")
    st_cl = DS("sCl")
    st_cg = DS("sCg")
    st_cs = DS("sCs", 2)
    st_dc = DS("sDc")
    st_dld = DS("sDld", 2)
    st_dg = DS("sDg", 2)
    st_out = DS("sOut", 2)

    scope1 = ExitStack()
    sb1 = lambda name, shape, dt: scope1.enter_context(nc.sbuf_tensor(name, shape, dt))
    wq_sb = sb1("wq_sb", [128, 8, D], BF16)
    wk_sb = sb1("wk_sb", [128, 8, D], BF16)
    wv_sb = sb1("wv_sb", [128, 8, D], BF16)
    wo_sb = sb1("wo_sb", [128, 8, D], BF16)
    wif_sb = sb1("wif_sb", [128, 8, 2 * H], BF16)

    def f_init():
        for w_sb_, w_h_ in ((wq_sb, wq_h), (wk_sb, wk_h), (wv_sb, wv_h), (wo_sb, wo_h),
                            (wif_sb, wif_h)):
            for k in range(8):
                st_initg.issue("gpsimd", lambda e, o=w_sb_[:, k, :],
                               i=w_h_[128 * k:128 * (k + 1), :]: e.dma_start(out=o, in_=i))
        st_initg.issue("gpsimd", lambda e: e.dma_start(out=bo_sb[:], in_=bo_h))
        for k in range(8):
            st_init.issue("sync", lambda e, k=k: e.dma_start(
                out=rwf_sb[:, k, :], in_=rwt_h[128 * k:128 * (k + 1), :]))
        st_init.issue("sync", lambda e: e.dma_start(out=bif_sb2[:], in_=bif_h[H:2 * H, :]))
        for o, i in ((bif_sb[:], bif_h), (rb_sb[:], rb_h), (ident_sb[:], ident_h),
                     (trist_sb[:], tristb_h), (tri01_sb[:], tri01_h),
                     (bdiag16_sb[:], bdiag16_h), (negoh16_sb[:], negoh16_h),
                     (ohrep_sb[:], ohrep_h), (valid_sb[:], valid_h),
                     (bdiag32_sb[:], bdiag32_h), (prevmask16_sb[:], prevmask16_h),
                     (prevmask_sb[:], prevmask_h), (prevall_sb[:], prevall_h)):
            st_init.issue("sync", lambda e, o=o, i=i: e.dma_start(out=o, in_=i))
        dwait("vector", st_init.tok())
        dwait("vector", st_initg.tok())
        p.emit("vector", lambda e: e.memset(ones_sb[:], 1.0))
        p.emit("vector", lambda e: e.memset(onesb_sb[:], 1.0))
        p.emit("vector", lambda e: e.memset(eps_col[:], LN_EPS))
        for k in range(8):
            p.emit("vector", lambda e, k=k: e.tensor_copy(rwh_sb[:, k, :], rwf_sb[:, k, :]))
        vdrain()
        for k in range(8):
            p.emit("vector", lambda e, k=k: e.tensor_tensor(
                rwl_sb[:, k, :], rwf_sb[:, k, :], rwh_sb[:, k, :], ALU.subtract))
        mark("vector")

    def f_a0(scope):
        RT = 2
        T0T = RT * 128
        sbA = lambda name, shape, dt: scope.enter_context(nc.sbuf_tensor(name, shape, dt))
        xa = sbA("a0_x", [128, 2, RT, D], F32)
        xn32 = sbA("a0_xn", [128, RT, D], F32)
        xnh = sbA("a0_xnh", [128, RT, D], BF16)
        xnl = sbA("a0_xnl", [128, RT, D], BF16)
        mcol = sbA("a0_m", [128, RT], F32)
        vcol = sbA("a0_v", [128, RT], F32)
        rcol = sbA("a0_r", [128, RT], F32)
        ntl = TL // T0T
        stA_tok = {}
        for i in range(ntl):
            bufi = i % 2
            t0 = i * T0T
            if i >= 2:
                dwait("sync", stA_tok[i - 2])
            ldt = st_a0x.issue("sync", lambda e, t0=t0, bufi=bufi: e.dma_start(
                out=xa[:, bufi, :, :],
                in_=x_h[t0:t0 + T0T, :].rearrange("(r p) d -> p r d", p=128)), i)
            dwait("vector", ldt)
            for r in range(RT):
                p.emit("vector", lambda e, bufi=bufi, r=r: e.reduce_sum(
                    mcol[:, r:r + 1], xa[:, bufi, r, :], axis=X_AX))
            vdrain()
            p.emit("vector", lambda e: e.tensor_scalar(
                mcol[:], mcol[:], 1.0 / D, None, ALU.mult))
            vdrain()
            for r in range(RT):
                p.emit("vector", lambda e, bufi=bufi, r=r: e.tensor_scalar(
                    xn32[:, r, :], xa[:, bufi, r, :], mcol[:, r:r + 1], None,
                    ALU.subtract))
            mark("vector")
            p.wait("scalar", es["vector"])
            for r in range(RT):
                p.emit("scalar", lambda e, r=r: e.activation(
                    sqdump[:, :], xn32[:, r, :], AF.Square, accum_out=vcol[:, r:r + 1]))
                sdrain()
            p.emit("scalar", lambda e: e.activation(
                rcol[:], vcol[:], AF.Ln, scale=1.0 / D, bias=eps_col[:]))
            sdrain()
            p.emit("scalar", lambda e: e.activation(
                rcol[:], rcol[:], AF.Exp, scale=-0.5), es["scalar"], 1)
            p.wait("vector", es["scalar"])
            if i >= 1:
                dwait("vector", stA_tok[i - 1])
                dwait("vector", st_ag1a.tok())
            for r in range(RT):
                p.emit("vector", lambda e, r=r: e.tensor_scalar(
                    xnh[:, r, :], xn32[:, r, :], rcol[:, r:r + 1], None, ALU.mult))
            vdrain()
            for r in range(RT):
                p.emit("vector", lambda e, r=r: e.scalar_tensor_tensor(
                    xnl[:, r, :], xn32[:, r, :], rcol[:, r:r + 1], xnh[:, r, :],
                    ALU.mult, ALU.subtract))
            stv = mark("vector")
            p.wait("sync", es["vector"], stv)
            st_ag1a.issue("sync", lambda e, t0=t0: e.dma_start(
                out=ag1a_in[t0:t0 + T0T, :].rearrange("(r p) d -> p r d", p=128),
                in_=xnh[:, :, :]))
            stA_tok[i] = st_a0s.issue("sync", lambda e, t0=t0: e.dma_start(
                out=xnlo_hbm[t0:t0 + T0T, :].rearrange("(r p) d -> p r d", p=128),
                in_=xnl[:, :, :]), i)
        dwait("sync", st_ag1a.tok())
        dwait("sync", st_a0s.tok(0))
        dwait("sync", st_a0s.tok(1))
        return st_ag1a.tok()

    def f_a1(scope):
        sbA = lambda name, shape, dt: scope.enter_context(nc.sbuf_tensor(name, shape, dt))
        xh = sbA("a1_xh", [128, 2, 8, TTILE], BF16)
        xl = sbA("a1_xl", [128, 2, 8, TTILE], BF16)
        qkv_sb = sbA("a1_qkv", [128, 8, TTILE], BF16)
        osig_sb = sbA("a1_osig", [128, 2, D], BF16)
        qkv_tok = [None]
        osig_tok = {0: None, 1: None}
        tile_done = {}
        for i in range(NTILE):
            bufi = i % 2
            t0 = i * TTILE
            if i >= 2:
                p.wait("sync", es["tensor"], tile_done[i - 2])
            for k in range(8):
                st_a1x.issue("sync", lambda e, k=k, bufi=bufi, t0=t0: e.dma_start(
                    out=xh[:, bufi, k, :],
                    in_=ag1a_in[t0:t0 + TTILE, 128 * k:128 * (k + 1)], transpose=True), i)
                st_a1x.issue("sync", lambda e, k=k, bufi=bufi, t0=t0: e.dma_start(
                    out=xl[:, bufi, k, :],
                    in_=xnlo_hbm[t0:t0 + TTILE, 128 * k:128 * (k + 1)], transpose=True), i)
            ldt = st_a1x.tok(i)
            dwait("tensor", ldt)
            if i == 0:
                dwait("tensor", st_initg.tok())
                p.wait("tensor", es["vector"])
            for w_sb_, hbm in ((wq_sb, qf_hbm), (wk_sb, kf_hbm), (wv_sb, vf_hbm)):
                if qkv_tok[0] is not None:
                    dwait("vector", qkv_tok[0])
                for hc in range(8):
                    bank = (psA, psB)[hc % 2]
                    guard("tensor", bank)
                    for k in range(8):
                        p.emit("tensor", lambda e, w=w_sb_, hc=hc, k=k, bufi=bufi,
                               bank=bank: e.matmul(
                                   bank[:, :], w[:, k, 128 * hc:128 * (hc + 1)],
                                   xh[:, bufi, k, :], start=(k == 0), stop=(k == 7)),
                               es["tensor"] if k == 7 else None, 1)
                    p.wait("vector", es["tensor"])
                    scale = 0.125 if hbm is qf_hbm else 1.0
                    p.emit("vector", lambda e, hc=hc, bank=bank, scale=scale:
                           e.tensor_scalar(qkv_sb[:, hc, :], bank[:, :], scale, None,
                                           ALU.mult), es["vector"], 1)
                    consumed(bank, es["vector"])
                stv = p.cur(es["vector"])
                p.wait("sync", es["vector"], stv)
                for hc in range(8):
                    qkv_tok[0] = st_qkv.issue("sync", lambda e, hc=hc, t0=t0, hbm=hbm:
                                              e.dma_start(
                        out=hbm[128 * hc:128 * (hc + 1), t0:t0 + TTILE],
                        in_=qkv_sb[:, hc, :]))
            guard("tensor", psC)
            guard("tensor", psH)
            for half, bank in ((0, psC), (1, psH)):
                for k in range(8):
                    p.emit("tensor", lambda e, k=k, bufi=bufi, half=half, bank=bank:
                           e.matmul(bank[0:H, :], wif_sb[:, k, H * half:H * (half + 1)],
                                    xh[:, bufi, k, :], start=(k == 0), stop=(k == 7)),
                           es["tensor"] if (k == 7 and half == 1) else None, 1)
            p.wait("vector", es["tensor"])
            p.emit("vector", lambda e, t0=t0: e.tensor_scalar(
                gi_i[:, t0:t0 + TTILE], psC[0:H, :], bif_sb[0:H, :], None, ALU.add))
            p.emit("vector", lambda e, t0=t0: e.tensor_scalar(
                gi_f[:, t0:t0 + TTILE], psH[0:H, :], bif_sb2[:], None, ALU.add),
                es["vector"], 1)
            consumed(psC, es["vector"])
            consumed(psH, es["vector"])
            guard("tensor", psD)
            for term, (wl, xr) in enumerate(((rwh_sb, xh), (rwl_sb, xh), (rwh_sb, xl))):
                for k in range(8):
                    last = (term == 2 and k == 7)
                    p.emit("tensor", lambda e, k=k, bufi=bufi, wl=wl, xr=xr,
                           stt=(term == 0 and k == 0), spp=last: e.matmul(
                               psD[0:NE, :], wl[:, k, :], xr[:, bufi, k, :],
                               start=stt, stop=spp),
                           es["tensor"] if last else None, 1)
            p.wait("vector", es["tensor"])
            p.emit("vector", lambda e, t0=t0: e.tensor_scalar(
                logf_sb[:, t0:t0 + TTILE], psD[0:NE, :], rb_sb[:], None, ALU.add),
                es["vector"], 1)
            consumed(psD, es["vector"])
            for tt in range(4):
                obufi = tt % 2
                for seg in range(2):
                    bank = (psE, psF)[seg]
                    guard("tensor", bank)
                    for k in range(8):
                        p.emit("tensor", lambda e, k=k, tt=tt, seg=seg, bufi=bufi,
                               bank=bank: e.matmul(
                                   bank[:, :], xh[:, bufi, k, 128 * tt:128 * (tt + 1)],
                                   wo_sb[:, k, 512 * seg:512 * (seg + 1)],
                                   start=(k == 0), stop=False))
                    p.emit("tensor", lambda e, seg=seg, bank=bank: e.matmul(
                        bank[:, :], onesb_sb[0:1, :],
                        bo_sb[0:1, 512 * seg:512 * (seg + 1)],
                        start=False, stop=True), es["tensor"] if seg == 1 else None, 1)
                p.wait("scalar", es["tensor"])
                if osig_tok[obufi] is not None:
                    dwait("scalar", osig_tok[obufi])
                for seg in range(2):
                    p.emit("scalar", lambda e, seg=seg, obufi=obufi: e.activation(
                        osig_sb[:, obufi, 512 * seg:512 * (seg + 1)],
                        (psE, psF)[seg][:, :], AF.Sigmoid),
                        es["scalar"] if seg == 1 else None, 1)
                consumed(psE, es["scalar"])
                consumed(psF, es["scalar"])
                p.wait("sync", es["scalar"])
                osig_tok[obufi] = st_o.issue("sync", lambda e, t0=t0, tt=tt,
                                             obufi=obufi: e.dma_start(
                    out=otm_hbm[t0 + 128 * tt:t0 + 128 * (tt + 1), :],
                    in_=osig_sb[:, obufi, :]), obufi)
            tile_done[i] = p.cur(es["tensor"])
        for j in range(NCH):
            guard("tensor", psG)
            p.emit("tensor", lambda e, j=j: e.transpose(
                psG[:, 0:NE], logf_sb[:, CH * j:CH * (j + 1)], ident_sb[0:NE, 0:NE]),
                es["tensor"], 1)
            p.wait("vector", es["tensor"])
            p.emit("vector", lambda e, j=j: e.tensor_copy(
                lownT_sb[:, j, :], psG[:, 0:NE]), es["vector"], 1)
            consumed(psG, es["vector"])
        lv = mark("vector")
        p.wait("sync", es["vector"], lv)
        st_ag1c.issue("sync", lambda e: e.dma_start(
            out=ag1c_in.rearrange("(j p) x -> p j x", p=128), in_=lownT_sb[:, :, :]))
        dwait("sync", st_qkv.tok())
        dwait("sync", st_o.tok(0))
        dwait("sync", st_o.tok(1))
        dwait("sync", st_a1x.tok(0))
        dwait("sync", st_a1x.tok(1))
        return st_ag1c.tok()

    def f_a2(scope):
        sbA = lambda name, shape, dt: scope.enter_context(nc.sbuf_tensor(name, shape, dt))
        neg_col = sbA("neg_col", [H, 1], F32)
        ktm2 = sbA("ktm2", [128, 2, D], BF16)
        vtm2 = sbA("vtm2", [128, 2, D], BF16)
        vw_sb = sbA("vw_sb", [128, H * (HD + 1)], BF16)
        wT_sb = sbA("wT_sb", [128, H], F32)
        sum_sb = sbA("sum_sb", [65, 1024], F32)
        scal_sb = sbA("scal_sb", [1, HD], F32)
        cnt_sb = sbA("cnt_sb", [128, NCH * NE], F32)
        tmp8_sb = sbA("tmp8_sb", [128, NCH * NE], F32)
        cnt8_sb = sbA("cnt8_sb", [128, NE], F32)
        p.emit("vector", lambda e: e.memset(neg_col[:], -1e30))
        p.emit("vector", lambda e: e.memset(scal_sb[:], 0.0))
        vdrain()
        for j in range(NCH):
            js = slice(CH * j, CH * (j + 1))
            p.emit("vector", lambda e, js=js: e.tensor_tensor_scan(
                Fc_sb[:, js], gi_f[:, js], gi_f[:, js], 0.0, ALU.add, ALU.bypass))
        vdrain()
        p.emit("vector", lambda e: e.tensor_tensor(
            g_sb[:, :], gi_i[:, :], Fc_sb[:, :], ALU.subtract))
        vdrain()
        for j in range(NCH):
            js = slice(CH * j, CH * (j + 1))
            p.emit("vector", lambda e, js=js: e.tensor_tensor_scan(
                aloc_sb[:, js], g_sb[:, js], g_sb[:, js], neg_col[:], ALU.max,
                ALU.bypass))
        vdrain()
        p.emit("vector", lambda e: e.tensor_copy(Lc_sb[:, :], Fc_sb[:, 127::128]))
        p.emit("vector", lambda e: e.tensor_copy(Gc_sb[:, :], aloc_sb[:, 127::128]))
        vdrain()
        p.emit("vector", lambda e: e.tensor_tensor_scan(
            GG_sb[:, :], Lc_sb[:, :], Lc_sb[:, :], 0.0, ALU.add, ALU.bypass))
        vdrain()
        p.emit("vector", lambda e: e.memset(prefL_sb[:, 0:1], 0.0))
        p.emit("vector", lambda e: e.tensor_copy(prefL_sb[:, 1:NCH], GG_sb[:, 0:NCH - 1]))
        p.emit("vector", lambda e: e.tensor_copy(Ftot_sb[:, :], GG_sb[:, NCH - 1:NCH]))
        vdrain()
        p.emit("vector", lambda e: e.tensor_tensor(
            GG_sb[:, :], Gc_sb[:, :], prefL_sb[:, :], ALU.subtract))
        vdrain()
        p.emit("vector", lambda e: e.reduce_max(Ghat_sb[:, :], GG_sb[:, :], axis=X_AX))
        vdrain()
        p.emit("vector", lambda e: e.tensor_scalar(
            po_sb[:, :], prefL_sb[:, :], Ghat_sb[:, :], None, ALU.add))
        vdrain()
        g3 = g_sb[:, :].rearrange("h (j t) -> h j t", t=CH)
        p.emit("vector", lambda e: e.tensor_tensor(
            Fc_sb[:, :].rearrange("h (j t) -> h j t", t=CH), g3,
            bc(g3, po_sb[:, :].rearrange("h (j o) -> h j o", o=1)), ALU.subtract),
            es["vector"], 1)
        p.wait("scalar", es["vector"])
        p.emit("scalar", lambda e: e.activation(omg_sb[:, :], Fc_sb[:, :], AF.Exp),
               es["scalar"], 1)
        p.emit("vector", lambda e: e.memset(cnt_sb[:], 0.0))
        vdrain()
        l3 = lownT_sb[:, :, :]
        for ep in range(NE):
            p.emit("vector", lambda e, ep=ep: e.tensor_tensor(
                tmp8_sb[:].rearrange("p (j x) -> p j x", x=NE),
                bc(l3, lownT_sb[:, :, ep:ep + 1]), l3, ALU.is_gt))
            vdrain()
            p.emit("vector", lambda e: e.tensor_tensor(
                cnt_sb[:], cnt_sb[:], tmp8_sb[:], ALU.add))
            vdrain()
        p.emit("vector", lambda e: e.tensor_scalar(
            ownsel_sb[:], cnt_sb[:], 1.5, None, ALU.is_lt))
        vdrain()
        p.emit("vector", lambda e: e.reduce_sum(
            cnt8_sb[:, :], ownsel_sb[:].rearrange("p (j x) -> p x j", x=NE), axis=X_AX),
            es["vector"], 1)
        p.wait("tensor", es["vector"])
        guard("tensor", psD)
        p.emit("tensor", lambda e: e.matmul(psD[0:1, 0:NE], ones_sb[:, 0:1],
                                            cnt8_sb[:, :], start=True, stop=True),
               es["tensor"], 1)
        p.wait("vector", es["tensor"])
        p.emit("vector", lambda e: e.tensor_copy(scal_sb[0:1, 32:32 + NE],
                                                 psD[0:1, 0:NE]), es["vector"], 1)
        consumed(psD, es["vector"])
        scv = mark("vector")
        p.wait("sync", es["vector"], scv)
        st_scal.issue("sync", lambda e: e.dma_start(out=scal_sb[0:1, 0:H],
                                                    in_=Ghat_sb[:, :]))
        st_scal.issue("sync", lambda e: e.dma_start(out=scal_sb[0:1, H:2 * H],
                                                    in_=Ftot_sb[:, :]))
        ktm_marks = {}
        for j in range(NCH):
            bufi = j % 2
            js = slice(CH * j, CH * (j + 1))
            if j >= 2:
                p.wait("sync", es["tensor"], ktm_marks[j - 2])
            st_a2kv.issue("sync", lambda e, j=j, bufi=bufi: e.dma_start(
                out=ktm2[:, bufi, :], in_=kf_hbm[:, CH * j:CH * (j + 1)],
                transpose=True), j)
            st_a2kv.issue("sync", lambda e, j=j, bufi=bufi: e.dma_start(
                out=vtm2[:, bufi, :], in_=vf_hbm[:, CH * j:CH * (j + 1)],
                transpose=True), j)
            ldt = st_a2kv.tok(j)
            guard("tensor", psG)
            if j == 0:
                p.wait("tensor", es["scalar"])
            p.emit("tensor", lambda e, js=js: e.transpose(
                psG[:, 0:H], omg_sb[:, js], ident_sb[0:H, 0:H]), es["tensor"], 1)
            p.emit("tensor", lambda e, js=js: e.transpose(
                psG[:, H:2 * H], g_sb[:, js], ident_sb[0:H, 0:H]), es["tensor"], 1)
            p.wait("vector", es["tensor"])
            if j >= 1:
                p.wait("vector", es["tensor"], ktm_marks[j - 1])
            p.emit("vector", lambda e: e.tensor_copy(wT_sb[:, :], psG[:, 0:H]))
            p.emit("vector", lambda e, j=j: e.tensor_copy(gT_sb[:, j, :],
                                                          psG[:, H:2 * H]),
                   es["vector"], 1)
            consumed(psG, es["vector"])
            vdrain()
            dwait("vector", ldt)
            v3 = vtm2[:, bufi, :].rearrange("p (h x) -> p h x", x=HD)
            p.emit("vector", lambda e, v3=v3: e.tensor_tensor(
                vw_sb[:].rearrange("p (h x) -> p h x", x=HD + 1)[:, :, 0:HD],
                v3, bc(v3, wT_sb[:, :].rearrange("p (h o) -> p h o", o=1)), ALU.mult))
            p.emit("vector", lambda e: e.tensor_copy(
                vw_sb[:].rearrange("p (h x) -> p h x", x=HD + 1)[:, :, HD:HD + 1],
                wT_sb[:, :].rearrange("p (h o) -> p h o", o=1)), es["vector"], 1)
            p.wait("tensor", es["vector"])
            for h in range(H):
                bank = psE if h < 8 else psF
                if j == 0 and h % 8 == 0:
                    guard("tensor", bank)
                p.emit("tensor", lambda e, h=h, j=j, bufi=bufi, bank=bank: e.matmul(
                    bank[0:HD + 1, HD * (h % 8):HD * (h % 8 + 1)],
                    vw_sb[:, (HD + 1) * h:(HD + 1) * (h + 1)],
                    ktm2[:, bufi, HD * h:HD * (h + 1)],
                    start=(j == 0 and h % 8 == 0), stop=(j == NCH - 1 and h % 8 == 7)),
                    es["tensor"] if h == H - 1 else None, 1)
            ktm_marks[j] = p.cur(es["tensor"])
        p.wait("vector", es["tensor"])
        p.emit("vector", lambda e: e.tensor_copy(sum_sb[0:HD + 1, 0:512],
                                                 psE[0:HD + 1, :]))
        p.emit("vector", lambda e: e.tensor_copy(sum_sb[0:HD + 1, 512:1024],
                                                 psF[0:HD + 1, :]), es["vector"], 1)
        consumed(psE, es["vector"])
        consumed(psF, es["vector"])
        sv2 = mark("vector")
        p.wait("sync", es["vector"], sv2)
        dwait("sync", st_scal.tok())
        for h in range(H):
            st_ag1b.issue("sync", lambda e, h=h: e.dma_start(
                out=ag1b_in[(HD + 1) * h:(HD + 1) * (h + 1), 0:HD],
                in_=sum_sb[0:HD + 1, HD * h:HD * (h + 1)]))
        st_ag1b.issue("sync", lambda e: e.dma_start(
            out=ag1b_in[H * (HD + 1):H * (HD + 1) + 1, :], in_=scal_sb[:]))
        dwait("sync", st_ag1b.tok())
        dwait("sync", st_a2kv.tok(0))
        dwait("sync", st_a2kv.tok(1))
        return st_ag1b.tok()


    def f_b(scope):
        sbA = lambda name, shape, dt: scope.enter_context(nc.sbuf_tensor(name, shape, dt))
        ghi = sbA("b_ghi", [H, TL], BF16)
        glo = sbA("b_glo", [H, TL], BF16)
        Ahi = sbA("b_Ahi", [H, TL], BF16)
        Alo = sbA("b_Alo", [H, TL], BF16)
        qfm = sbA("b_qfm", [64, 2, H, CH], BF16)
        kfm = sbA("b_kfm", [64, 2, H, CH], BF16)
        vfm = sbA("b_vfm", [64, 2, H, CH], BF16)
        ktmb = sbA("b_ktm", [128, 2, D], BF16)
        vtmb = sbA("b_vtm", [128, 2, D], BF16)
        otm2 = sbA("b_otm", [128, 2, D], BF16)
        Dm = sbA("b_Dm", [128, 8 * CH], BF16)
        DSv = sbA("b_DSv", [128, 8 * CH], BF16)
        DSk = sbA("b_DSk", [128, 8 * CH], BF16)
        vwb = sbA("b_vwb", [128, D], BF16)
        wlT = sbA("b_wlT", [128, H], BF16)
        decT_s = sbA("b_decT", [128, NCH, H], F32)
        cst = sbA("b_cst", [64, 2, H * (HD + 1)], BF16)
        interS = sbA("b_interS", [128, 2 * 264], F32)
        den_sb = sbA("b_den", [128, NE], F32)
        nqr_sb = sbA("b_nqr", [128, NE], F32)
        numer = sbA("b_numer", [128, 512], F32)
        h2 = sbA("b_h2", [128, 2, D], BF16)
        Gh8 = sbA("b_Gh8", [NC, 32], F32)
        GhT = sbA("b_GhT", [H, NC], F32)
        FtT = sbA("b_FtT", [H, NC], F32)
        GhM = sbA("b_GhM", [H, NC], F32)
        FtM = sbA("b_FtM", [H, NC], F32)
        CSc = sbA("b_CSc", [H, NC], F32)
        macc = sbA("b_macc", [H, 1], F32)
        csum = sbA("b_csum", [65, NC * HD], F32)
        CIN = sbA("b_CIN", [65, H * HD], F32)
        CSb = sbA("b_CSb", [H, NC], BF16)
        sccol8 = sbA("b_sccol8", [65, NC], F32)
        arep = sbA("b_arep", [64, NCH * H], F32)
        brep = sbA("b_brep", [128, NCH * H], F32)
        aT32 = sbA("b_aT32", [32, H], F32)
        aTexp = sbA("b_aTexp", [32, 32 * H], F32)

        p.wait("sync", cc, ag_tok["b"])
        p.wait("vector", cc, ag_tok["b"])
        p.wait("tensor", cc, ag_tok["b"])
        p.wait("scalar", cc, ag_tok["b"])
        sr = H * (HD + 1)
        t1 = st_bp.issue("sync", lambda e: e.dma_start(
            out=Gh8[:, :], in_=ag1b_out.rearrange("(k r) x -> k r x", r=SUMROWS)[:, sr, 0:32]))
        dwait("tensor", t1)
        guard("tensor", psH)
        p.emit("tensor", lambda e: e.transpose(
            psH[0:H, 0:NC], Gh8[:, 0:H], ident_sb[0:NC, 0:NC]), es["tensor"], 1)
        p.emit("tensor", lambda e: e.transpose(
            psH[0:H, NC:2 * NC], Gh8[:, H:2 * H], ident_sb[0:NC, 0:NC]), es["tensor"], 1)
        p.wait("vector", es["tensor"])
        p.emit("vector", lambda e: e.tensor_copy(GhT[:, :], psH[0:H, 0:NC]))
        p.emit("vector", lambda e: e.tensor_copy(FtT[:, :], psH[0:H, NC:2 * NC]),
               es["vector"], 1)
        consumed(psH, es["vector"])
        vdrain()
        p.emit("vector", lambda e: e.tensor_scalar(
            CSc[:, :], prevmask16_sb[:, :], 1e30, None, ALU.mult))
        p.emit("vector", lambda e: e.tensor_tensor(GhM[:, :], GhT[:, :],
                                                   prevmask16_sb[:, :], ALU.mult))
        vdrain()
        p.emit("vector", lambda e: e.tensor_scalar(
            CSc[:, :], CSc[:, :], 1e30, None, ALU.subtract))
        vdrain()
        p.emit("vector", lambda e: e.tensor_tensor(GhM[:, :], GhM[:, :],
                                                   CSc[:, :], ALU.add))
        p.emit("vector", lambda e: e.tensor_tensor(FtM[:, :], FtT[:, :],
                                                   prevmask16_sb[:, :], ALU.mult))
        vdrain()
        p.emit("vector", lambda e: e.memset(macc[:], 0.0))
        vdrain()
        for k in range(NC):
            p.emit("vector", lambda e, k=k: e.tensor_tensor(
                macc[:, :], macc[:, :], GhM[:, k:k + 1], ALU.max))
            vdrain()
            p.emit("vector", lambda e, k=k: e.tensor_tensor(
                macc[:, :], macc[:, :], FtM[:, k:k + 1], ALU.add))
            vdrain()
        p.emit("vector", lambda e: e.tensor_copy(Mi_sb[:, :], macc[:, :]))
        vdrain()
        p.emit("vector", lambda e: e.tensor_tensor_scan(
            CSc[:, :], FtM[:, :], FtM[:, :], 0.0, ALU.add, ALU.bypass))
        vdrain()
        p.emit("vector", lambda e: e.scalar_tensor_tensor(
            CSc[:, :], CSc[:, :], -1.0, GhM[:, :], ALU.mult, ALU.add))
        vdrain()
        p.emit("vector", lambda e: e.tensor_scalar(
            CSc[:, :], CSc[:, :], CSc[:, 7:8], None, ALU.add) if False else
            e.tensor_tensor(CSc[:, :], CSc[:, :], FtM[:, :], ALU.add))
        vdrain()
        p.emit("vector", lambda e: e.reduce_sum(macc[:, :], FtM[:, :], axis=X_AX))
        vdrain()
        p.emit("vector", lambda e: e.tensor_scalar(
            CSc[:, :], CSc[:, :], macc[:, :], None, ALU.add))
        vdrain()
        p.emit("vector", lambda e: e.tensor_scalar(
            CSc[:, :], CSc[:, :], Mi_sb[:, :], None, ALU.subtract), es["vector"], 1)
        p.wait("scalar", es["vector"])
        p.emit("scalar", lambda e: e.activation(CSc[:, :], CSc[:, :], AF.Exp),
               es["scalar"], 1)
        p.wait("vector", es["scalar"])
        p.emit("vector", lambda e: e.tensor_copy(CSb[:, :], CSc[:, :]), es["vector"], 1)
        p.emit("vector", lambda e: e.memset(CIN[:, :], 0.0))
        p.emit("vector", lambda e: e.memset(cst[:, 0, :], 0.0))
        p.emit("vector", lambda e: e.memset(cst[:, 1, :], 0.0))
        mark("vector")
        p.wait("tensor", es["vector"])
        for h in range(H):
            th = st_bp2.issue("sync", lambda e, h=h: e.dma_start(
                out=csum[:, :].rearrange("r (k x) -> r k x", x=HD),
                in_=ag1b_out.rearrange("(k r) x -> k r x", r=SUMROWS)
                [:, (HD + 1) * h:(HD + 1) * (h + 1), 0:HD].rearrange("k r x -> r k x")))
            guard("tensor", psG)
            p.emit("tensor", lambda e, h=h: e.matmul(
                psG[0:65, 0:NC], bdiag16_sb[:, CH * h:CH * h + 65], CSb[:, :],
                start=True, stop=True), es["tensor"], 1)
            p.wait("vector", es["tensor"])
            p.emit("vector", lambda e: e.tensor_copy(sccol8[:, :], psG[0:65, 0:NC]),
                   es["vector"], 1)
            consumed(psG, es["vector"])
            vdrain()
            dwait("vector", th)
            dwait("tensor", th)
            for k in range(NC):
                p.emit("vector", lambda e, h=h, k=k: e.scalar_tensor_tensor(
                    CIN[:, HD * h:HD * (h + 1)], csum[:, HD * k:HD * (k + 1)],
                    sccol8[:, k:k + 1], CIN[:, HD * h:HD * (h + 1)], ALU.mult, ALU.add))
                vdrain()
            mark("vector")
            p.wait("sync", es["vector"])
        for h in range(H):
            p.emit("vector", lambda e, h=h: e.tensor_copy(
                cst[:, 0, (HD + 1) * h:(HD + 1) * h + HD], CIN[0:64, HD * h:HD * (h + 1)]))
        mark("vector")
        p.wait("tensor", es["vector"])
        for h in range(H):
            guard("tensor", psG)
            p.emit("tensor", lambda e, h=h: e.transpose(
                psG[0:HD, 0:1], CIN[64:65, HD * h:HD * (h + 1)], ident_sb[64:65, 64:65]),
                es["tensor"], 1)
            p.wait("vector", es["tensor"])
            p.emit("vector", lambda e, h=h: e.tensor_copy(
                cst[:, 0, (HD + 1) * h + HD:(HD + 1) * (h + 1)], psG[0:HD, 0:1]),
                es["vector"], 1)
            consumed(psG, es["vector"])
        vdrain()
        p.emit("vector", lambda e: e.tensor_copy(Mx_sb[:, 0:1], Mi_sb[:, :]))
        p.emit("vector", lambda e: e.tensor_tensor_scan(
            Mx_sb[:, 1:NCH + 1], GG_sb[:, :], GG_sb[:, :], Mi_sb[:, :],
            ALU.max, ALU.bypass))
        vdrain()
        p.emit("vector", lambda e: e.tensor_tensor(
            aarg_sb[:, :], Mx_sb[:, 0:NCH], Mx_sb[:, 1:NCH + 1], ALU.subtract))
        p.emit("vector", lambda e: e.tensor_tensor(
            barg_sb[:, :], GG_sb[:, :], Mx_sb[:, 1:NCH + 1], ALU.subtract))
        vdrain()
        p.emit("vector", lambda e: e.tensor_tensor(
            mu_sb[:, :], Mx_sb[:, 0:NCH], prefL_sb[:, :], ALU.add))
        vdrain()
        al3 = aloc_sb[:, :].rearrange("h (j t) -> h j t", t=CH)
        p.emit("vector", lambda e: e.tensor_tensor(
            A_sb[:, :].rearrange("h (j t) -> h j t", t=CH), al3,
            bc(al3, mu_sb[:, :].rearrange("h (j o) -> h j o", o=1)), ALU.max))
        vdrain()
        decB = denomI_sb
        A3 = A_sb[:, :].rearrange("h (j t) -> h j t", t=CH)
        p.emit("vector", lambda e: e.tensor_tensor(
            decB[:, :].rearrange("h (j t) -> h j t", t=CH),
            bc(A3, mu_sb[:, :].rearrange("h (j o) -> h j o", o=1)), A3, ALU.subtract),
            es["vector"], 1)
        p.wait("scalar", es["vector"])
        p.emit("scalar", lambda e: e.activation(decB[:, :], decB[:, :], AF.Exp),
               es["scalar"], 1)
        gg3 = g_sb[:, :].rearrange("h (j t) -> h j t", t=CH)
        p.emit("vector", lambda e: e.tensor_tensor(
            omg_sb[:, :].rearrange("h (j t) -> h j t", t=CH), gg3,
            bc(gg3, Gc_sb[:, :].rearrange("h (j o) -> h j o", o=1)), ALU.subtract),
            es["vector"], 1)
        p.wait("scalar", es["vector"])
        p.emit("scalar", lambda e: e.activation(omg_sb[:, :], omg_sb[:, :], AF.Exp),
               es["scalar"], 1)
        wloc_sb = omg_sb
        p.emit("vector", lambda e: e.tensor_copy(ghi[:, :], g_sb[:, :]))
        p.emit("vector", lambda e: e.tensor_copy(Ahi[:, :], A_sb[:, :]))
        vdrain()
        p.emit("vector", lambda e: e.tensor_tensor(glo[:, :], g_sb[:, :], ghi[:, :],
                                                   ALU.subtract))
        p.emit("vector", lambda e: e.tensor_tensor(Alo[:, :], A_sb[:, :], Ahi[:, :],
                                                   ALU.subtract), es["vector"], 1)
        p.wait("tensor", es["scalar"])
        p.wait("tensor", es["vector"])
        for j in range(NCH):
            guard("tensor", psG)
            p.emit("tensor", lambda e, j=j: e.transpose(
                psG[:, 0:H], decB[:, CH * j:CH * (j + 1)], ident_sb[0:H, 0:H]),
                es["tensor"], 1)
            p.wait("vector", es["tensor"])
            p.emit("vector", lambda e, j=j: e.tensor_copy(decT_s[:, j, :], psG[:, 0:H]),
                   es["vector"], 1)
            consumed(psG, es["vector"])
        p.wait("scalar", es["vector"])
        p.emit("scalar", lambda e: e.activation(aarg_sb[:, :], aarg_sb[:, :], AF.Exp))
        sdrain()
        p.emit("scalar", lambda e: e.activation(barg_sb[:, :], barg_sb[:, :], AF.Exp),
               es["scalar"], 1)
        p.wait("tensor", es["scalar"])
        for name_, src_, dst_, rows in (("a", aarg_sb, arep, 64), ("b", barg_sb, brep, 128)):
            guard("tensor", psG)
            p.emit("tensor", lambda e, src_=src_: e.transpose(
                psG[0:32, 0:H], src_[:, :], ident_sb[0:H, 0:H]), es["tensor"], 1)
            p.wait("vector", es["tensor"])
            p.emit("vector", lambda e: e.tensor_copy(aT32[:, :], psG[0:32, 0:H]),
                   es["vector"], 1)
            consumed(psG, es["vector"])
            vdrain()
            a3 = bdiag32_sb[:, :].rearrange("k (j h) -> k j h", h=H)
            p.emit("vector", lambda e, a3=a3: e.tensor_tensor(
                aTexp[:, :].rearrange("k (j h) -> k j h", h=H), a3,
                bc(a3, aT32[:, :].rearrange("k (o h) -> k o h", o=1)), ALU.mult),
                es["vector"], 1)
            p.wait("tensor", es["vector"])
            for seg in range(0, NCH * H, 512):
                guard("tensor", psG)
                p.emit("tensor", lambda e, seg=seg, rows=rows: e.matmul(
                    psG[0:rows, 0:512], ones_sb[0:32, 0:rows], aTexp[:, seg:seg + 512],
                    start=True, stop=True), es["tensor"], 1)
                p.wait("vector", es["tensor"])
                p.emit("vector", lambda e, seg=seg, dst_=dst_, rows=rows: e.tensor_copy(
                    dst_[:, seg:seg + 512][0:rows, :], psG[0:rows, 0:512]),
                    es["vector"], 1)
                consumed(psG, es["vector"])
        barrier()

        def issue_loads(j):
            bufi = j % 2
            for hbm, dst in ((qf_hbm, qfm), (kf_hbm, kfm), (vf_hbm, vfm)):
                st_bq.issue("sync", lambda e, hbm=hbm, dst=dst, j=j, bufi=bufi:
                            e.dma_start(
                    out=dst[:, bufi, :, :],
                    in_=hbm[:, CH * j:CH * (j + 1)].rearrange("(c p) t -> p c t", p=64)),
                    j)
            st_bk.issue("sync", lambda e, j=j, bufi=bufi: e.dma_start(
                out=ktmb[:, bufi, :], in_=kf_hbm[:, CH * j:CH * (j + 1)], transpose=True),
                j)
            st_bk.issue("sync", lambda e, j=j, bufi=bufi: e.dma_start(
                out=vtmb[:, bufi, :], in_=vf_hbm[:, CH * j:CH * (j + 1)], transpose=True),
                j)
            st_bo.issue("sync", lambda e, j=j, bufi=bufi: e.dma_start(
                out=otm2[:, bufi, :], in_=otm_hbm[CH * j:CH * (j + 1), :]), j)

        issue_loads(0)
        issue_loads(1)
        comp_marks = {}
        hstore_tok = {0: None, 1: None}
        for j in range(NCH):
            bufi = j % 2
            js = slice(CH * j, CH * (j + 1))
            ldq = st_bq.tok(j)
            ldk = st_bk.tok(j)
            ldo = st_bo.tok(j)
            guard("tensor", psG)
            p.emit("tensor", lambda e, js=js: e.transpose(
                psG[:, 0:H], wloc_sb[:, js], ident_sb[0:H, 0:H]), es["tensor"], 1)
            p.wait("vector", es["tensor"])
            if j >= 1:
                p.wait("vector", es["tensor"], comp_marks[j - 1])
            p.emit("vector", lambda e, j=j: e.tensor_tensor(
                wlT[:, :], psG[:, 0:H],
                brep[:, H * j:H * (j + 1)], ALU.mult), es["vector"], 1)
            consumed(psG, es["vector"])
            vdrain()
            dwait("vector", ldk)
            v3 = vtmb[:, bufi, :].rearrange("p (h x) -> p h x", x=HD)
            p.emit("vector", lambda e, v3=v3: e.tensor_tensor(
                vwb[:, :].rearrange("p (h x) -> p h x", x=HD), v3,
                bc(v3, wlT[:, :].rearrange("p (h o) -> p h o", o=1)), ALU.mult),
                es["vector"], 1)
            p.wait("tensor", es["vector"])
            for h in range(H):
                bank = (psA, psB, psC)[h // 7]
                col = 65 * (h % 7)
                if h in (0, 7, 14):
                    guard("tensor", bank)
                p.emit("tensor", lambda e, h=h, bufi=bufi, bank=bank, col=col: e.matmul(
                    bank[0:HD, col:col + HD], vwb[:, HD * h:HD * (h + 1)],
                    ktmb[:, bufi, HD * h:HD * (h + 1)], start=True, stop=True))
                p.emit("tensor", lambda e, h=h, bufi=bufi, bank=bank, col=col: e.matmul(
                    bank[0:HD, col + HD:col + HD + 1], ktmb[:, bufi, HD * h:HD * (h + 1)],
                    wlT[:, h:h + 1], start=True, stop=True),
                    es["tensor"] if h == H - 1 else None, 1)
            umark = p.cur(es["tensor"])
            p.wait("vector", es["tensor"], umark)
            if j >= 1:
                p.wait("vector", es["tensor"], comp_marks[j - 1])
            rd, wr = j % 2, (j + 1) % 2
            for bi, bank in enumerate((psA, psB, psC)):
                nh = (7, 7, 2)[bi]
                span = 65 * nh
                h0 = 7 * bi
                c3 = cst[:, rd, 65 * h0:65 * h0 + span].rearrange(
                    "p (h x) -> p h x", x=65)
                o3 = cst[:, wr, 65 * h0:65 * h0 + span].rearrange(
                    "p (h x) -> p h x", x=65)
                a3s = bc(c3, arep[:, H * j + h0:H * j + h0 + nh].rearrange(
                    "p (h o) -> p h o", o=1))
                p.emit("vector", lambda e, c3=c3, o3=o3, a3s=a3s: e.tensor_tensor(
                    o3, c3, a3s, ALU.mult))
            vdrain()
            for bi, bank in enumerate((psA, psB, psC)):
                nh = (7, 7, 2)[bi]
                span = 65 * nh
                h0 = 7 * bi
                p.emit("vector", lambda e, bank=bank, h0=h0, span=span, wr=wr:
                       e.tensor_tensor(
                    cst[:, wr, 65 * h0:65 * h0 + span],
                    cst[:, wr, 65 * h0:65 * h0 + span], bank[0:HD, 0:span], ALU.add),
                    es["vector"], 1)
            chmark = p.cur(es["vector"])
            for bank in (psA, psB, psC):
                consumed(bank, es["vector"], chmark)
            vdrain()
            dwait("tensor", ldq)
            for gH in range(2):
                h0 = 8 * gH
                guard("tensor", psD)
                guard("tensor", psE)
                for hl, garr in ((0, ghi), (1, glo)):
                    for seg in range(2):
                        bank = (psD, psE)[seg]
                        p.emit("tensor", lambda e, garr=garr, js=js, gH=gH, seg=seg,
                               bank=bank, hl=hl: e.matmul(
                            bank[:, :], garr[:, js],
                            bdiag16_sb[:, CH * (8 * gH + 4 * seg):CH * (8 * gH + 4 * seg + 4)],
                            start=(hl == 0), stop=False))
                for h8 in range(8):
                    h = h0 + h8
                    bank = (psD, psE)[h8 // 4]
                    colb = CH * (h8 % 4)
                    for hl, Aarr in ((0, Ahi), (1, Alo)):
                        p.emit("tensor", lambda e, h=h, js=js, Aarr=Aarr, bank=bank,
                               colb=colb, hl=hl, h8=h8: e.matmul(
                            bank[:, colb:colb + CH],
                            negoh16_sb[:, CH * h:CH * (h + 1)], Aarr[:, js],
                            start=False, stop=(hl == 1 and h8 % 4 == 3)),
                            es["tensor"] if (hl == 1 and h8 == 7) else None, 1)
                p.wait("scalar", es["tensor"])
                p.emit("scalar", lambda e: e.activation(Dm[:, 0:512], psD[:, :], AF.Exp))
                sdrain()
                p.emit("scalar", lambda e: e.activation(Dm[:, 512:1024], psE[:, :],
                                                        AF.Exp), es["scalar"], 1)
                consumed(psD, es["scalar"])
                consumed(psE, es["scalar"])
                p.wait("vector", es["scalar"])
                p.emit("vector", lambda e, gH=gH: e.tensor_tensor(
                    Dm[:, :], Dm[:, :], tri01_sb[:, CH * 8 * gH:CH * 8 * (gH + 1)],
                    ALU.mult), es["vector"], 1)
                dmark = p.cur(es["vector"])
                vdrain()
                guard("tensor", psF)
                guard("tensor", psG)
                p.wait("tensor", es["vector"], dmark)
                for h8 in range(8):
                    h = h0 + h8
                    bank = (psF, psG)[h8 // 4]
                    colb = CH * (h8 % 4)
                    p.emit("tensor", lambda e, h=h, bufi=bufi, bank=bank, colb=colb:
                           e.matmul(
                        bank[:, colb:colb + CH], vfm[:, bufi, h, :],
                        qfm[:, bufi, h, :], start=True, stop=True),
                        es["tensor"] if h8 == 7 else None, 1)
                p.wait("vector", es["tensor"])
                p.emit("vector", lambda e: e.tensor_tensor(
                    DSv[:, 0:512], psF[:, :], Dm[:, 0:512], ALU.mult))
                p.emit("vector", lambda e: e.tensor_tensor(
                    DSv[:, 512:1024], psG[:, :], Dm[:, 512:1024], ALU.mult),
                    es["vector"], 1)
                consumed(psF, es["vector"])
                consumed(psG, es["vector"])
                guard("tensor", psD)
                guard("tensor", psE)
                for h8 in range(8):
                    h = h0 + h8
                    bank = (psD, psE)[h8 // 4]
                    colb = CH * (h8 % 4)
                    p.emit("tensor", lambda e, h=h, bufi=bufi, bank=bank, colb=colb:
                           e.matmul(
                        bank[:, colb:colb + CH], kfm[:, bufi, h, :],
                        qfm[:, bufi, h, :], start=True, stop=True),
                        es["tensor"] if h8 == 7 else None, 1)
                p.wait("vector", es["tensor"])
                p.emit("vector", lambda e: e.tensor_tensor(
                    DSk[:, 0:512], psD[:, :], Dm[:, 0:512], ALU.mult))
                p.emit("vector", lambda e: e.tensor_tensor(
                    DSk[:, 512:1024], psE[:, :], Dm[:, 512:1024], ALU.mult),
                    es["vector"], 1)
                consumed(psD, es["vector"])
                consumed(psE, es["vector"])
                dsmark = p.cur(es["vector"])
                guard("tensor", psH)
                guard("tensor", psA)
                guard("tensor", psB)
                p.wait("tensor", es["vector"], dsmark)
                for h8 in range(8):
                    h = h0 + h8
                    p.emit("tensor", lambda e, h8=h8, h=h, bufi=bufi: e.matmul(
                        psH[:, HD * h8:HD * (h8 + 1)], DSv[:, CH * h8:CH * (h8 + 1)],
                        ktmb[:, bufi, HD * h:HD * (h + 1)], start=True, stop=True))
                    bank = (psA, psB)[h8 // 4]
                    p.emit("tensor", lambda e, h8=h8, bank=bank: e.matmul(
                        bank[:, 260 + (h8 % 4):260 + (h8 % 4) + 1],
                        DSk[:, CH * h8:CH * (h8 + 1)], onesb_sb[:, 0:1],
                        start=True, stop=True))
                    p.emit("tensor", lambda e, h8=h8, h=h, bufi=bufi, bank=bank, rd=rd:
                           e.matmul(
                        bank[:, 65 * (h8 % 4):65 * (h8 % 4) + 65],
                        qfm[:, bufi, h, :], cst[0:HD, rd, 65 * h:65 * (h + 1)],
                        start=True, stop=True), es["tensor"] if h8 == 7 else None, 1)
                p.wait("vector", es["tensor"])
                dwait("vector", ldo)
                if hstore_tok[bufi] is not None and gH == 0:
                    dwait("vector", hstore_tok[bufi])
                for bi, bank in enumerate((psA, psB)):
                    i3 = bank[:, 0:260].rearrange("p (h x) -> p h x", x=65)
                    p.emit("vector", lambda e, bank=bank, bi=bi, i3=i3, j=j, h0=h0:
                           e.tensor_tensor(
                        interS[:, 264 * bi:264 * bi + 260].rearrange(
                            "p (h x) -> p h x", x=65), i3,
                        bc(i3, decT_s[:, j, h0 + 4 * bi:h0 + 4 * bi + 4].rearrange(
                            "p (h o) -> p h o", o=1)), ALU.mult))
                    p.emit("vector", lambda e, bank=bank, bi=bi: e.tensor_copy(
                        interS[:, 264 * bi + 260:264 * bi + 264], bank[:, 260:264]),
                        es["vector"], 1)
                consumed(psA, es["vector"])
                consumed(psB, es["vector"])
                vdrain()
                p.emit("vector", lambda e: e.tensor_tensor(
                    numer[:, :].rearrange("p (g h x) -> p g h x", g=2, x=HD),
                    psH[:, :].rearrange("p (g h x) -> p g h x", g=2, x=HD),
                    interS[:, :].rearrange("p (g y) -> p g y", g=2)[:, :, 0:260]
                    .rearrange("p g (h x) -> p g h x", x=65)[:, :, :, 0:HD],
                    ALU.add), es["vector"], 1)
                consumed(psH, es["vector"])
                p.emit("vector", lambda e, gH=gH: e.tensor_tensor(
                    den_sb[:, 0:NE].rearrange("p (g x) -> p g x", g=2),
                    interS[:, :].rearrange("p (g y) -> p g y", g=2)[:, :, 260:264],
                    interS[:, :].rearrange("p (g y) -> p g y", g=2)[:, :, 0:260]
                    .rearrange("p g (h x) -> p g h x", x=65)[:, :, :, HD],
                    ALU.add))
                p.emit("vector", lambda e: e.scalar_tensor_tensor(
                    nqr_sb[:, 0:NE].rearrange("p (g x) -> p g x", g=2),
                    interS[:, :].rearrange("p (g y) -> p g y", g=2)[:, :, 260:264],
                    -1.0,
                    interS[:, :].rearrange("p (g y) -> p g y", g=2)[:, :, 0:260]
                    .rearrange("p g (h x) -> p g h x", x=65)[:, :, :, HD],
                    ALU.mult, ALU.subtract))
                vdrain()
                p.emit("vector", lambda e, gH=gH, bufi=bufi: e.tensor_tensor(
                    numer[:, :], numer[:, :], otm2[:, bufi, 512 * gH:512 * (gH + 1)],
                    ALU.mult))
                p.emit("vector", lambda e: e.tensor_tensor(
                    den_sb[:, 0:NE], den_sb[:, 0:NE], nqr_sb[:, 0:NE], ALU.max))
                vdrain()
                p.emit("vector", lambda e: e.tensor_scalar(
                    den_sb[:, 0:NE], den_sb[:, 0:NE], 1.0, None, ALU.max))
                vdrain()
                p.emit("vector", lambda e: e.reciprocal(nqr_sb[:, 0:NE], den_sb[:, 0:NE]))
                vdrain()
                n3 = numer[:, :].rearrange("p (h x) -> p h x", x=HD)
                p.emit("vector", lambda e, n3=n3, gH=gH, bufi=bufi: e.tensor_tensor(
                    h2[:, bufi, 512 * gH:512 * (gH + 1)].rearrange(
                        "p (h x) -> p h x", x=HD), n3,
                    bc(n3, nqr_sb[:, 0:NE].rearrange("p (h o) -> p h o", o=1)),
                    ALU.mult), es["vector"], 1)
            comp_marks[j] = p.cur(es["tensor"])
            hv = mark("vector")
            p.wait("sync", es["vector"], hv)
            hstore_tok[bufi] = st_bh.issue("sync", lambda e, j=j, bufi=bufi: e.dma_start(
                out=hat_hbm[CH * j:CH * (j + 1), :], in_=h2[:, bufi, :]), bufi)
            if j + 2 < NCH:
                p.wait("sync", es["tensor"], comp_marks[j])
                issue_loads(j + 2)
        dwait("sync", st_bh.tok(0))
        dwait("sync", st_bh.tok(1))

    ag_tok = {}

    def ag_emit(key, tok, t_in, t_out):
        dwait("gpsimd", tok)
        ag_tok[key] = p.emit("gpsimd", lambda e, t_in=t_in, t_out=t_out:
                             e.collective_compute(
            "AllGather", ALU.bypass, replica_groups=[list(range(NC))],
            ins=[t_in], outs=[t_out]), cc, 1)


    def f_c(scope, wcols, flrep):
        sbC = lambda name, shape, dt: scope.enter_context(nc.sbuf_tensor(name, shape, dt))
        w1sb = sbC("w1sb", [128, 8, MID], BF16)
        w2sb = sbC("w2sb", [128, 32, D], BF16)
        xg2 = sbC("xg2", [128, 8, 512], BF16)
        hmid = sbC("hmid", [128, 32, 512], BF16)
        sout = sbC("sout", [128, 2, D], BF16)
        ids16d = sbC("ids16d", [16, 2048], I16)
        idxq = sbC("idxq", [16, 2048], I16)
        dstq = sbC("dstq", [16, 1024], I16)
        mq = sbC("mq", [16, 512], F32)
        idsrep = sbC("idsrep", [128, 256], I16)
        fi16 = sbC("fi16", [128, 256], I16)
        idsr_f = mq[0:16, 0:256]
        cnts8 = sbC("cnts8", [8, 8], F32)
        big_a = sbC("big_a", [128, 32, NE], F32)
        big_b = sbC("big_b", [128, 32, NE], F32)
        sp = sbC("sp", [128, 10, 256], F32)
        selbG = sbC("selbG", [128, 256], BF16)
        selbO = selbG
        trow_tot = sp[0:1, 8, :]
        trow_scan = sp[0:1, 9, :]
        trow_excl = sp[0:1, 0, :]
        trow_pref = sp[0:1, 1, 0:8]
        iotaj_sb = sp[0:1, 6, :]
        pcol1_sb = sbC("pcol1_sb", [128, 1], F32)

        for k in range(8):
            st_cw.issue("gpsimd", lambda e, k=k: e.dma_start(
                out=w1sb[:, k, :], in_=w1_h[128 * k:128 * (k + 1), :]))
        for k in range(32):
            st_cw.issue("gpsimd", lambda e, k=k: e.dma_start(
                out=w2sb[:, k, :], in_=w2_h[128 * k:128 * (k + 1), :]))

        st_cr.issue("sync", lambda e: e.dma_start(out=iotaj_sb[:, :], in_=iotaj_h))
        st_cr.issue("sync", lambda e: e.dma_start(out=pcol1_sb[:], in_=pcol1_h))
        p.wait("sync", cc, ag_tok["b"])
        st_cr.issue("sync", lambda e: e.dma_start(
            out=cnts8[:, :].rearrange("c (o e) -> c o e", o=1),
            in_=ag1b_out.rearrange("(c r) d -> c r d", r=SUMROWS)
            [:, SUMROWS - 1:SUMROWS, 32:40]))
        p.wait("sync", cc, ag_tok["c"])
        lsel, cnt, sel = sp[:, 0, :], sp[:, 1, :], sp[:, 2, :]
        rank, iota1, ids = sp[:, 3, :], sp[:, 4, :], sp[:, 5, :]
        lh_tok = {}
        lh_tok[0] = st_cl.issue("sync", lambda e: e.dma_start(
            out=big_a[:, :, :],
            in_=ag1c_out.rearrange("(j p) e -> p j e", p=128)[:, 0:32, :]))
        for jh in range(8):
            if jh >= 1:
                hm = mark("vector")
                p.wait("sync", es["vector"], hm)
                lh_tok[jh] = st_cl.issue("sync", lambda e, jh=jh: e.dma_start(
                    out=big_a[:, :, :],
                    in_=ag1c_out.rearrange("(j p) e -> p j e", p=128)
                    [:, 32 * jh:32 * (jh + 1), :]))
            hs = slice(32 * jh, 32 * (jh + 1))
            dwait("vector", lh_tok[jh])
            l3 = big_a[:, :, :]
            ohv = ohrep_sb[:, :].rearrange("p (o e) -> p o e", o=1)
            p.emit("vector", lambda e, l3=l3, ohv=ohv: e.tensor_tensor(
                big_b[:, :, :], l3, bc(l3, ohv), ALU.mult))
            vdrain()
            p.emit("vector", lambda e, hs=hs: e.reduce_sum(
                lsel[:, hs], big_b[:, :, :], axis=X_AX))
            vdrain()
            lse = lsel[:, hs].rearrange("p (j o) -> p j o", o=1)
            p.emit("vector", lambda e, l3=l3, lse=lse: e.tensor_tensor(
                big_b[:, :, :], l3, bc(l3, lse), ALU.is_gt))
            vdrain()
            p.emit("vector", lambda e, hs=hs: e.reduce_sum(
                cnt[:, hs], big_b[:, :, :], axis=X_AX))
            vdrain()
        p.emit("vector", lambda e: e.tensor_scalar(
            sel[:, :], cnt[:, :], 1.5, None, ALU.is_lt))
        vdrain()
        p.emit("vector", lambda e: e.tensor_scalar(
            sel[:, :], sel[:, :], valid_sb[:, 0:1], None, ALU.mult))
        vdrain()
        p.emit("vector", lambda e: e.tensor_copy(selbG[:, :], sel[:, :]),
               es["vector"], 1)
        p.wait("tensor", es["vector"])
        guard("tensor", psA)
        p.emit("tensor", lambda e: e.matmul(
            psA[:, 0:256], trist_sb[:, :], selbG[:, :], start=True, stop=False))
        guard("tensor", psB)
        p.emit("tensor", lambda e: e.matmul(
            psB[0:1, 0:256], onesb_sb[:, 0:1], selbG[:, :], start=True, stop=True),
            es["tensor"], 1)
        guard("tensor", psC)
        dwait("tensor", st_cr.tok())
        p.emit("tensor", lambda e: e.matmul(
            psC[:, 0:256], ones_sb[0:1, 0:128], iotaj_sb[0:1, :], start=True,
            stop=True), es["tensor"], 1)
        p.wait("vector", es["tensor"])
        p.emit("vector", lambda e: e.tensor_copy(trow_tot[:, :], psB[0:1, 0:256]))
        consumed(psB, es["vector"])
        p.emit("vector", lambda e: e.tensor_scalar(
            iota1[:, :], psC[:, 0:256], pcol1_sb[:, 0:1], None, ALU.add))
        consumed(psC, es["vector"])
        vdrain()
        p.emit("vector", lambda e: e.tensor_tensor_scan(
            trow_scan[:, :], trow_tot[:, :], trow_tot[:, :], 0.0,
            ALU.add, ALU.bypass))
        vdrain()
        p.emit("vector", lambda e: e.memset(trow_excl[:, 0:1], 0.0))
        p.emit("vector", lambda e: e.tensor_copy(
            trow_excl[:, 1:256], trow_scan[:, 0:255]), es["vector"], 1)
        p.wait("tensor", es["vector"])
        p.emit("tensor", lambda e: e.matmul(
            psA[:, 0:256], ones_sb[0:1, 0:128], trow_excl[:, :], start=False,
            stop=True), es["tensor"], 1)
        p.wait("vector", es["tensor"])
        p.emit("vector", lambda e: e.tensor_copy(rank[:, :], psA[:, 0:256]))
        consumed(psA, es["vector"])
        vdrain()
        dbg_dump("rank", rank[:, :])
        dbg_dump("gsel", sel[:, :])
        p.emit("vector", lambda e: e.tensor_scalar(
            cnt[:, :], rank[:, :], float(CAP), None, ALU.is_lt))
        vdrain()
        p.emit("vector", lambda e: e.tensor_tensor(
            sel[:, :], sel[:, :], cnt[:, :], ALU.mult))
        vdrain()
        p.emit("vector", lambda e: e.tensor_tensor(
            ids[:, :], iota1[:, :], sel[:, :], ALU.mult))
        vdrain()
        p.emit("vector", lambda e: e.tensor_scalar(
            ids[:, :], ids[:, :], 1.0, None, ALU.subtract), es["vector"], 1)
        p.wait("tensor", es["vector"])
        swb = (psC, psD, psE, psF)
        for pb in range(8):
            bank = swb[pb // 2]
            if pb % 2 == 0:
                guard("tensor", bank)
            p.emit("tensor", lambda e, bank=bank, pb=pb: e.matmul(
                bank[0:16, 256 * (pb % 2):256 * (pb % 2) + 256],
                ident_sb[:, 16 * pb:16 * (pb + 1)], ids[:, :], start=True,
                stop=True), es["tensor"], 1)
        p.wait("vector", es["tensor"])
        for pb in range(8):
            bank = swb[pb // 2]
            p.emit("vector", lambda e, bank=bank, pb=pb: e.tensor_scalar(
                ids16d[0:16, pb:2048:8],
                bank[0:16, 256 * (pb % 2):256 * (pb % 2) + 256], 1.0, None,
                ALU.mult), es["vector"] if pb % 2 == 1 else None, 1)
            if pb % 2 == 1:
                consumed(bank, es["vector"])
        dbg_dump("tgt16", ids16d[:, :])
        dwait("gpsimd", st_cw.tok())
        p.emit("gpsimd", lambda e: e.load_library(library_config.local_scatter))
        rq, mk_ = sp[:, 4, :], sp[:, 1, :]
        fv = ids
        p.emit("vector", lambda e: e.tensor_scalar(
            rq[:, :], rank[:, :], 0.0625, None, ALU.mult))
        vdrain()
        p.emit("vector", lambda e: e.tensor_scalar(
            fi16[:, :], rq[:, :], 1.0, None, ALU.mult), es["vector"], 1)
        p.wait("scalar", es["vector"])
        p.emit("scalar", lambda e: e.activation(mk_[:, :], fi16[:, :], AF.Copy),
               es["scalar"], 1)
        p.wait("vector", es["scalar"])
        p.emit("vector", lambda e: e.scalar_tensor_tensor(
            rq[:, :], mk_[:, :], -16.0, rank[:, :], ALU.mult, ALU.add))
        vdrain()
        p.emit("vector", lambda e: e.tensor_scalar(
            fv[:, :], rq[:, :], -0.5, None, ALU.is_lt))
        vdrain()
        p.emit("vector", lambda e: e.tensor_tensor(
            mk_[:, :], mk_[:, :], fv[:, :], ALU.subtract))
        p.emit("vector", lambda e: e.scalar_tensor_tensor(
            rq[:, :], fv[:, :], 16.0, rq[:, :], ALU.mult, ALU.add))
        vdrain()
        p.emit("vector", lambda e: e.scalar_tensor_tensor(
            fv[:, :], rq[:, :], 256.0, mk_[:, :], ALU.mult, ALU.add))
        vdrain()
        for q in range(4):
            p.emit("vector", lambda e, q=q: e.tensor_scalar(
                mk_[:, :], fv[:, :], 1024.0 * q - 0.5, None, ALU.is_gt))
            p.emit("vector", lambda e, q=q: e.tensor_scalar(
                rq[:, :], fv[:, :], 1024.0 * q + 1023.5, None, ALU.is_lt))
            vdrain()
            p.emit("vector", lambda e: e.tensor_tensor(
                mk_[:, :], mk_[:, :], rq[:, :], ALU.mult))
            vdrain()
            p.emit("vector", lambda e: e.tensor_tensor(
                mk_[:, :], mk_[:, :], sel[:, :], ALU.mult))
            p.emit("vector", lambda e, q=q: e.tensor_scalar(
                rq[:, :], fv[:, :], -1024.0 * q, None, ALU.add))
            vdrain()
            p.emit("vector", lambda e: e.tensor_tensor(
                rq[:, :], rq[:, :], mk_[:, :], ALU.mult))
            vdrain()
            p.emit("vector", lambda e: e.scalar_tensor_tensor(
                rq[:, :], mk_[:, :], -1.0, rq[:, :], ALU.add, ALU.add),
                es["vector"], 1)
            p.wait("tensor", es["vector"])
            for pb in range(8):
                bank = swb[pb // 2]
                if pb % 2 == 0:
                    guard("tensor", bank)
                p.emit("tensor", lambda e, bank=bank, pb=pb: e.matmul(
                    bank[0:16, 256 * (pb % 2):256 * (pb % 2) + 256],
                    ident_sb[:, 16 * pb:16 * (pb + 1)], rq[:, :], start=True,
                    stop=True), es["tensor"], 1)
            p.wait("vector", es["tensor"])
            p.wait("vector", es["gpsimd"])
            for pb in range(8):
                bank = swb[pb // 2]
                p.emit("vector", lambda e, bank=bank, pb=pb: e.tensor_scalar(
                    idxq[0:16, pb:2048:8],
                    bank[0:16, 256 * (pb % 2):256 * (pb % 2) + 256], 1.0, None,
                    ALU.mult), es["vector"] if pb % 2 == 1 else None, 1)
                if pb % 2 == 1:
                    consumed(bank, es["vector"])
            iv = mark("vector")
            p.wait("gpsimd", es["vector"], iv)
            p.wait("gpsimd", es["scalar"])
            p.emit("gpsimd", lambda e: e.local_scatter(
                dstq[:, :], ids16d[:, :], idxq[:, :], channels=16,
                num_elems=1024, num_idxs=2048), es["gpsimd"], 1)
            for hq in range(2):
                p.wait("scalar", es["gpsimd"])
                if hq == 1:
                    p.wait("scalar", es["tensor"])
                p.emit("scalar", lambda e, hq=hq: e.activation(
                    mq[:, :], dstq[:, 512 * hq:512 * (hq + 1)], AF.Copy),
                    es["scalar"], 1)
                p.wait("tensor", es["scalar"])
                guard("tensor", psG)
                p.emit("tensor", lambda e: e.matmul(
                    psG[0:1, 0:512], ones_sb[0:16, 0:1],
                    mq[0:16, 0:512], start=True, stop=True),
                    es["tensor"], 1)
                p.wait("vector", es["tensor"])
                p.emit("vector", lambda e: e.tensor_copy(
                    sp[0:1, 8:10, :].rearrange("o a b -> o (a b)"), psG[0:1, 0:512]),
                    es["vector"], 1)
                consumed(psG, es["vector"])
                mv = p.cur(es["vector"])
                p.wait("sync", es["vector"], mv)
                st_cr.issue("sync", lambda e, q=q, hq=hq: e.dma_start(
                    out=lmrow_h[2 * q + hq:2 * q + hq + 1, :],
                    in_=sp[0:1, 8:10, :].rearrange("o a b -> o (a b)")))
                dwait("vector", st_cr.tok())
        dwait("sync", st_cr.tok())
        st_cr.issue("sync", lambda e: e.dma_start(
            out=idsr_f[:, :],
            in_=lmrow_h.rearrange("a b -> (a b)").rearrange("(p c) -> p c", p=16)))
        dwait("vector", st_cr.tok())
        p.emit("vector", lambda e: e.tensor_scalar(
            idsrep[0:16, :], idsr_f[:, :], 1.0, None, ALU.mult), es["vector"], 1)
        dbg_dump("sidf", idsr_f[:, :])
        sgv = mark("vector")
        p.wait("sync", es["vector"], sgv)
        for rp in (16, 32, 64):
            dwait("sync", st_cr.tok())
            st_cr.issue("sync", lambda e, rp=rp: e.dma_start(
                out=idsrep[rp:2 * rp, :], in_=idsrep[0:rp, :]))
        p.emit("gpsimd", lambda e: e.load_library(library_config.mlp))

        ownrank, ownkeep, wsm = sp[:, 3, :], sp[:, 1, :], sp[:, 4, :]
        ocols, fla, flb = sp[:, 0, :], sp[:, 5, 0:NCH], sp[:, 2, 0:NCH]
        runf, runl = sp[:, 6, 0:NCH], sp[:, 7, 0:NCH]
        tmpA, tmpB = sp[:, 8, :], sp[:, 9, :]
        dwait("tensor", st_cr.tok())
        guard("tensor", psF)
        p.emit("tensor", lambda e: e.matmul(
            psF[0:1, 0:NE], prevall_sb[:, :], cnts8[:, :], start=True, stop=True),
            es["tensor"], 1)
        p.wait("vector", es["tensor"])
        p.emit("vector", lambda e: e.tensor_copy(trow_pref[:, :], psF[0:1, 0:NE]))
        consumed(psF, es["vector"])
        p.emit("vector", lambda e: e.tensor_copy(selbO[:, :], ownsel_sb[:, :]),
               es["vector"], 1)
        p.wait("tensor", es["vector"])
        guard("tensor", psG)
        p.emit("tensor", lambda e: e.matmul(
            psG[:, 0:256], trist_sb[:, :], selbO[:, :], start=True, stop=False))
        guard("tensor", psH)
        p.emit("tensor", lambda e: e.matmul(
            psH[0:1, 0:256], onesb_sb[:, 0:1], selbO[:, :], start=True, stop=True),
            es["tensor"], 1)
        p.wait("vector", es["tensor"])
        p.emit("vector", lambda e: e.tensor_copy(trow_tot[:, :], psH[0:1, 0:256]))
        consumed(psH, es["vector"])
        vdrain()
        for ep in range(NE):
            p.emit("vector", lambda e, ep=ep: e.tensor_tensor_scan(
                trow_scan[:, ep:256:NE], trow_tot[:, ep:256:NE],
                trow_tot[:, ep:256:NE], 0.0, ALU.add, ALU.bypass))
        vdrain()
        p.emit("vector", lambda e: e.memset(trow_excl[:, 0:8], 0.0))
        p.emit("vector", lambda e: e.tensor_copy(
            trow_excl[:, 8:256], trow_scan[:, 0:248]))
        vdrain()
        ov = trow_excl[:, :].rearrange("o (j e) -> o j e", e=NE)
        pv = trow_pref[:, :].rearrange("o (x e) -> o x e", x=1)
        p.emit("vector", lambda e, ov=ov, pv=pv: e.tensor_tensor(
            ov, ov, bc(ov, pv), ALU.add), es["vector"], 1)
        p.wait("tensor", es["vector"])
        p.emit("tensor", lambda e: e.matmul(
            psG[:, 0:256], ones_sb[0:1, 0:128], trow_excl[:, :], start=False,
            stop=True), es["tensor"], 1)
        p.wait("vector", es["tensor"])
        p.emit("vector", lambda e: e.tensor_copy(ownrank[:, :], psG[:, 0:256]))
        consumed(psG, es["vector"])
        vdrain()
        p.emit("vector", lambda e: e.tensor_scalar(
            ownkeep[:, :], ownrank[:, :], float(CAP), None, ALU.is_lt))
        vdrain()
        p.emit("vector", lambda e: e.tensor_tensor(
            ownkeep[:, :], ownkeep[:, :], ownsel_sb[:, :], ALU.mult))
        vdrain()
        l3o = lownT_sb[:, :, :]
        w3 = wsm[:, :].rearrange("p (j e) -> p j e", e=NE)
        p.emit("vector", lambda e: e.reduce_max(
            ocols[:, 0:NCH], lownT_sb[:, :, :], axis=X_AX))
        vdrain()
        mcv = ocols[:, 0:NCH].rearrange("p (j o) -> p j o", o=1)
        p.emit("vector", lambda e, w3=w3, l3o=l3o, mcv=mcv: e.tensor_tensor(
            w3, l3o, bc(l3o, mcv), ALU.subtract), es["vector"], 1)
        p.wait("scalar", es["vector"])
        p.emit("scalar", lambda e: e.activation(wsm[:, :], wsm[:, :], AF.Exp),
               es["scalar"], 1)
        p.wait("vector", es["scalar"])
        p.emit("vector", lambda e, w3=w3: e.reduce_sum(
            ocols[:, 32:32 + NCH], w3, axis=X_AX))
        vdrain()
        p.emit("vector", lambda e: e.reciprocal(
            ocols[:, 64:64 + NCH], ocols[:, 32:32 + NCH]))
        vdrain()
        rcv = ocols[:, 64:64 + NCH].rearrange("p (j o) -> p j o", o=1)
        p.emit("vector", lambda e, w3=w3, rcv=rcv: e.tensor_tensor(
            w3, w3, bc(w3, rcv), ALU.mult))
        vdrain()
        p.emit("vector", lambda e: e.tensor_tensor(
            wcols[:, 2, :], wsm[:, 7:256:NE], ownsel_sb[:, 7:256:NE], ALU.mult))
        for arr in (runf, runl, wcols[:, 0, :], wcols[:, 1, :], fla, flb):
            p.emit("vector", lambda e, arr=arr: e.memset(arr, 0.0))
        vdrain()
        for pas, rng, run, wacc, flacc in (
                (0, range(7), runf, wcols[:, 0, :], fla),
                (1, range(6, -1, -1), runl, wcols[:, 1, :], flb)):
            for ep in rng:
                kv = ownkeep[:, ep:256:NE]
                p.emit("vector", lambda e, run=run: e.tensor_scalar(
                    tmpA[:, 0:NCH], run, 0.5, None, ALU.is_lt))
                vdrain()
                p.emit("vector", lambda e, kv=kv: e.tensor_tensor(
                    tmpA[:, 0:NCH], tmpA[:, 0:NCH], kv, ALU.mult))
                p.emit("vector", lambda e, ep=ep: e.tensor_scalar(
                    tmpB[:, 0:NCH], ownrank[:, ep:256:NE], float(CAP * ep), None,
                    ALU.add))
                vdrain()
                p.emit("vector", lambda e, run=run, kv=kv: e.tensor_tensor(
                    run, run, kv, ALU.add))
                p.emit("vector", lambda e: e.tensor_tensor(
                    tmpB[:, 0:NCH], tmpB[:, 0:NCH], tmpA[:, 0:NCH], ALU.mult))
                vdrain()
                p.emit("vector", lambda e, ep=ep: e.tensor_tensor(
                    tmpA[:, 0:NCH], tmpA[:, 0:NCH], wsm[:, ep:256:NE], ALU.mult))
                vdrain()
                p.emit("vector", lambda e, flacc=flacc: e.tensor_tensor(
                    flacc, flacc, tmpB[:, 0:NCH], ALU.add))
                p.emit("vector", lambda e, wacc=wacc: e.tensor_tensor(
                    wacc, wacc, tmpA[:, 0:NCH], ALU.add))
                vdrain()
        p.emit("vector", lambda e: e.tensor_tensor(
            tmpA[:, 0:NCH], fla[:, 0:NCH], flb[:, 0:NCH], ALU.is_gt))
        p.emit("vector", lambda e: e.tensor_tensor(
            tmpB[:, 0:NCH], flb[:, 0:NCH], fla[:, 0:NCH], ALU.is_gt))
        vdrain()
        p.emit("vector", lambda e: e.tensor_tensor(
            tmpA[:, 0:NCH], tmpA[:, 0:NCH], tmpB[:, 0:NCH], ALU.add))
        vdrain()
        p.emit("vector", lambda e: e.tensor_tensor(
            wcols[:, 1, :], wcols[:, 1, :], tmpA[:, 0:NCH], ALU.mult),
            es["vector"], 1)
        p.wait("tensor", es["vector"])
        for pas, flv, c0 in ((0, fla, 0), (1, flb, 256)):
            bank = (psF, psG)[pas]
            guard("tensor", bank)
            for pb in range(8):
                p.emit("tensor", lambda e, bank=bank, pb=pb, flv=flv: e.matmul(
                    bank[0:16, 32 * pb:32 * (pb + 1)],
                    ident_sb[:, 16 * pb:16 * (pb + 1)], flv[:, 0:NCH], start=True,
                    stop=True), es["tensor"] if pb == 7 else None, 1)
            p.wait("vector", es["tensor"])
            for pb in range(8):
                p.emit("vector", lambda e, bank=bank, pb=pb, c0=c0: e.tensor_scalar(
                    flrep[0:16, c0 + pb:c0 + 256:8], bank[0:16, 32 * pb:32 * (pb + 1)],
                    1.0, None, ALU.mult), es["vector"] if pb == 7 else None, 1)
            consumed(bank, es["vector"])
        dbg_dump("wcols", wcols[:, 0:3, :])
        dbg_dump("ownrank", ownrank[:, :])
        flv2 = mark("vector")
        p.wait("sync", es["vector"], flv2)
        for rp in (16, 32, 64):
            dwait("sync", st_cr.tok())
            st_cr.issue("sync", lambda e, rp=rp: e.dma_start(
                out=flrep[rp:2 * rp, :], in_=flrep[0:rp, :]))
        if "flrep" in dbg:
            dwait("sync", st_cr.tok())
        dbg_dump("flrep", flrep[:, :])

        p.wait("gpsimd", cc, ag_tok["a"])
        dwait("gpsimd", st_cr.tok())
        banksW2 = (psA, psB, psC, psD, psE, psF, psG, psH)
        hm_mark = {}
        w1done = {}
        sstore = {0: None, 1: None}
        for g in range(8):
            if g >= 1:
                p.wait("gpsimd", es["tensor"], w1done[g - 1])
            st_cg.issue("gpsimd", lambda e, g=g: e.dma_gather(
                xg2[:, :, :], ag1a_out, idsrep[:, 32 * g:32 * (g + 1)],
                num_idxs=512, num_idxs_reg=512, elem_size=D, transpose=True))
            dwait("tensor", st_cg.tok())
            if g == 0:
                dwait("tensor", st_cw.tok())
            for mt in range(32):
                bank = (psA, psB)[mt % 2]
                guard("tensor", bank)
                for k in range(8):
                    p.emit("tensor", lambda e, bank=bank, mt=mt, k=k:
                           e.matmul(bank[:, 0:512],
                                    w1sb[:, k, 128 * mt:128 * (mt + 1)],
                                    xg2[:, k, :], start=(k == 0),
                                    stop=(k == 7)),
                           es["tensor"] if k == 7 else None, 1)
                eng = ("scalar", "vector")[mt % 2]
                p.wait(eng, es["tensor"])
                if eng == "scalar":
                    p.emit("scalar", lambda e, bank=bank, mt=mt: e.activation(
                        hmid[:, mt, :], bank[:, :], AF.Relu), es["scalar"], 1)
                else:
                    p.emit("vector", lambda e, bank=bank, mt=mt: e.tensor_scalar(
                        hmid[:, mt, :], bank[:, :], 0.0, None, ALU.max),
                        es["vector"], 1)
                consumed(bank, es[eng])
                hm_mark[mt] = (es[eng], p.cur(es[eng]))
            w1done[g] = p.cur(es["tensor"])
            for bank in banksW2:
                guard("tensor", bank)
            for k2 in range(32):
                p.wait("tensor", hm_mark[k2][0], hm_mark[k2][1])
                for st in range(4):
                    for seg in range(2):
                        bank = banksW2[2 * st + seg]
                        p.emit("tensor", lambda e, bank=bank, k2=k2, st=st, seg=seg:
                               e.matmul(bank[:, 0:512],
                                        hmid[:, k2, 128 * st:128 * (st + 1)],
                                        w2sb[:, k2, 512 * seg:512 * (seg + 1)],
                                        start=(k2 == 0), stop=(k2 == 31)),
                               es["tensor"] if (k2 == 31 and st == 3 and seg == 1)
                               else None, 1)
            p.wait("vector", es["tensor"])
            for st in range(4):
                sbufi = st % 2
                if sstore[sbufi] is not None:
                    dwait("vector", sstore[sbufi])
                for seg in range(2):
                    bank = banksW2[2 * st + seg]
                    p.emit("vector", lambda e, bank=bank, sbufi=sbufi, seg=seg:
                           e.tensor_copy(sout[:, sbufi, 512 * seg:512 * (seg + 1)],
                                         bank[:, :]), es["vector"], 1)
                    consumed(bank, es["vector"])
                p.wait("sync", es["vector"])
                sstore[sbufi] = st_cs.issue("sync", lambda e, g=g, st=st, sbufi=sbufi:
                                            e.dma_start(
                    out=ag2_in[512 * g + 128 * st:512 * g + 128 * (st + 1), :],
                    in_=sout[:, sbufi, :]), sbufi)
        dwait("gpsimd", st_cs.tok(0))
        dwait("gpsimd", st_cs.tok(1))
        ag_tok["2"] = p.emit("gpsimd", lambda e: e.collective_compute(
            "AllGather", ALU.bypass, replica_groups=[list(range(NC))],
            ins=[ag2_in], outs=[ag2_out]), cc, 1)

    def f_d(scope, wcols, flrep):
        sbD = lambda name, shape, dt: scope.enter_context(nc.sbuf_tensor(name, shape, dt))
        x2 = sbD("d_x2", [128, 2, 4, D], F32)
        hat2 = sbD("d_hat2", [128, 2, 4, D], BF16)
        xn2 = sbD("d_xn2", [128, 2, 4, D], BF16)
        rowa2 = sbD("d_rowa", [128, 2, 4, D], BF16)
        rowb2 = sbD("d_rowb", [128, 2, 4, D], BF16)
        hatf = sbD("d_hatf", [128, 4, D], F32)
        cen = sbD("d_cen", [128, 4, D], F32)
        outb = sbD("d_out", [128, 2, 4, D], F32)
        gamr = sbD("d_gamr", [1, D], F32)
        betr = sbD("d_betr", [1, D], F32)
        gamrep = sbD("d_gamrep", [128, D], F32)
        betrep = sbD("d_betrep", [128, D], F32)
        dcol = sbD("d_col", [128, 4, 4], F32)
        st_dc.issue("sync", lambda e: e.dma_start(out=gamr[:], in_=gam_h))
        st_dc.issue("sync", lambda e: e.dma_start(out=betr[:], in_=bet_h))
        dwait("tensor", st_dc.tok())
        reps = ((gamr, gamrep, psA, psB), (betr, betrep, psC, psD))
        for src, dst, b0, b1 in reps:
            for half in range(2):
                bank = (b0, b1)[half]
                guard("tensor", bank)
                p.emit("tensor", lambda e, bank=bank, src=src, half=half: e.matmul(
                    bank[:, 0:512], ones_sb[0:1, 0:128],
                    src[0:1, 512 * half:512 * (half + 1)], start=True, stop=True),
                    es["tensor"], 1)
        p.wait("vector", es["tensor"])
        for src, dst, b0, b1 in reps:
            for half in range(2):
                bank = (b0, b1)[half]
                p.emit("vector", lambda e, bank=bank, dst=dst, half=half:
                       e.tensor_copy(dst[:, 512 * half:512 * (half + 1)],
                                     bank[:, :]), es["vector"], 1)
                consumed(bank, es["vector"])
        p.wait("gpsimd", cc, ag_tok["2"])
        dcomp = {}
        hread = {}
        for g in range(8):
            bufi = g % 2
            rows = slice(512 * g, 512 * (g + 1))
            if g >= 2:
                for sem_, v_ in dcomp[g - 2]:
                    p.wait("sync", sem_, v_)
                    p.wait("gpsimd", sem_, v_)
            st_dld.issue("sync", lambda e, rows=rows, bufi=bufi: e.dma_start(
                out=x2[:, bufi, :, :],
                in_=x_h[rows, :].rearrange("(r p) d -> p r d", p=128)), bufi)
            st_dld.issue("sync", lambda e, rows=rows, bufi=bufi: e.dma_start(
                out=hat2[:, bufi, :, :],
                in_=hat_hbm[rows, :].rearrange("(r p) d -> p r d", p=128)), bufi)
            st_dld.issue("sync", lambda e, rows=rows, bufi=bufi: e.dma_start(
                out=xn2[:, bufi, :, :],
                in_=ag1a_in[rows, :].rearrange("(r p) d -> p r d", p=128)), bufi)
            st_dg.issue("gpsimd", lambda e, g=g, bufi=bufi: e.dma_gather(
                rowa2[:, bufi, :, :], ag2_out, flrep[:, 32 * g:32 * (g + 1)],
                num_idxs=512, num_idxs_reg=512, elem_size=D), bufi, GINC)
            st_dg.issue("gpsimd", lambda e, g=g, bufi=bufi: e.dma_gather(
                rowb2[:, bufi, :, :], ag2_out, flrep[:, 256 + 32 * g:256 + 32 * (g + 1)],
                num_idxs=512, num_idxs_reg=512, elem_size=D), bufi, GINC)
            dwait("scalar", st_dld.tok(bufi))
            if g >= 1:
                p.wait("scalar", es["vector"], hread[g - 1])
            for r in range(4):
                p.emit("scalar", lambda e, r=r, bufi=bufi: e.activation(
                    sqdump[:, :], hat2[:, bufi, r, :], AF.Copy,
                    accum_out=dcol[:, 0, r:r + 1]))
                sdrain()
            for r in range(4):
                p.emit("scalar", lambda e, r=r, bufi=bufi: e.activation(
                    sqdump[:, :], hat2[:, bufi, r, :], AF.Square,
                    accum_out=dcol[:, 1, r:r + 1]))
                sdrain()
            smk = mark("scalar")
            p.wait("vector", es["scalar"], smk)
            p.emit("vector", lambda e: e.tensor_scalar(
                dcol[:, 0, :], dcol[:, 0, :], 1.0 / D, None, ALU.mult))
            p.emit("vector", lambda e: e.tensor_scalar(
                dcol[:, 1, :], dcol[:, 1, :], 1.0 / D, None, ALU.mult))
            vdrain()
            p.emit("vector", lambda e: e.tensor_tensor(
                dcol[:, 2, :], dcol[:, 0, :], dcol[:, 0, :], ALU.mult))
            vdrain()
            p.emit("vector", lambda e: e.tensor_tensor(
                dcol[:, 1, :], dcol[:, 1, :], dcol[:, 2, :], ALU.subtract))
            vmk = mark("vector")
            p.wait("scalar", es["vector"], vmk)
            p.emit("scalar", lambda e: e.activation(
                dcol[:, 3, :], dcol[:, 1, :], AF.Ln, bias=eps_col[:]))
            sdrain()
            p.emit("scalar", lambda e: e.activation(
                dcol[:, 3, :], dcol[:, 3, :], AF.Exp, scale=-0.5))
            smk1 = mark("scalar")
            p.wait("vector", es["scalar"], smk1)
            p.emit("vector", lambda e: e.tensor_tensor(
                dcol[:, 2, :], dcol[:, 0, :], dcol[:, 3, :], ALU.mult),
                es["vector"], 1)
            vmk2 = p.cur(es["vector"])
            p.wait("scalar", es["vector"], vmk2)
            for r in range(4):
                p.emit("scalar", lambda e, r=r, bufi=bufi: e.activation(
                    hatf[:, r, :], hat2[:, bufi, r, :], AF.Copy,
                    scale=dcol[:, 3, r:r + 1]))
            smk2 = mark("scalar")
            p.wait("vector", es["scalar"], smk2)
            dwait("vector", st_dld.tok(bufi))
            dwait("vector", st_dg.tok(bufi))
            dwait("vector", st_out.tok(bufi))
            for r in range(4):
                p.emit("vector", lambda e, r=r: e.scalar_tensor_tensor(
                    cen[:, r, :], hatf[:, r, :], dcol[:, 2, r:r + 1],
                    gamrep[:, :], ALU.subtract, ALU.mult),
                    es["vector"] if r == 3 else None, 1)
            hread[g] = p.cur(es["vector"])
            vdrain()
            for r in range(4):
                p.emit("vector", lambda e, r=r: e.tensor_tensor(
                    cen[:, r, :], cen[:, r, :], betrep[:, :], ALU.add))
            vdrain()
            for r in range(4):
                p.emit("vector", lambda e, r=r, bufi=bufi: e.tensor_tensor(
                    outb[:, bufi, r, :], cen[:, r, :], x2[:, bufi, r, :], ALU.add))
            vdrain()
            for src2, wi in ((rowa2, 0), (rowb2, 1), (xn2, 2)):
                for r in range(4):
                    p.emit("vector", lambda e, src2=src2, wi=wi, r=r, bufi=bufi,
                           g=g: e.scalar_tensor_tensor(
                        outb[:, bufi, r, :], src2[:, bufi, r, :],
                        wcols[:, wi, 4 * g + r:4 * g + r + 1],
                        outb[:, bufi, r, :], ALU.mult, ALU.add))
                vdrain()
            dmk = mark("vector")
            p.wait("sync", es["vector"], dmk)
            st_out.issue("sync", lambda e, rows=rows, bufi=bufi: e.dma_start(
                out=out_h[rows, :].rearrange("(r p) d -> p r d", p=128),
                in_=outb[:, bufi, :, :]), bufi)
            dcomp[g] = ((es["vector"], dmk), (es["scalar"], smk2))
        dwait("sync", st_out.tok(0))
        dwait("sync", st_out.tok(1))

    f_init()
    scope_a0 = ExitStack()
    a_tok = f_a0(scope_a0)
    barrier()
    scope_a0.close()
    scope_a1 = ExitStack()
    c_tok = f_a1(scope_a1)
    ag_emit("c", c_tok, ag1c_in, ag1c_out)
    barrier()
    scope_a1.close()
    scope1.close()
    scope_a2 = ExitStack()
    b_tok = f_a2(scope_a2)
    ag_emit("b", b_tok, ag1b_in, ag1b_out)
    ag_emit("a", a_tok, ag1a_in, ag1a_out)
    barrier()
    scope_a2.close()
    scope_b = ExitStack()
    f_b(scope_b)
    barrier()
    scope_b.close()
    scope_ab.close()

    for gname, gap in (("g", g_sb), ("aloc", aloc_sb), ("omg", omg_sb),
                       ("gi_i", gi_i), ("gi_f", gi_f)):
        if gname in dbg:
            dbg_h[gname] = dram("dbg_" + gname, [H, TL], F32, "ExternalOutput")
            p.wait("sync", es["vector"])
            p.wait("sync", es["scalar"])
            st_dbg.issue("sync", lambda e, gap=gap, gname=gname: e.dma_start(
                out=dbg_h[gname], in_=gap))
    dbg2 = tuple(x for x in dbg if x in scratch_map)
    for name in dbg2:
        src = scratch_map[name]
        dbg_h[name] = dram("dbg_" + name, list(src.shape), src.tensor.dtype,
                           "ExternalOutput")
        st_dbgg.issue("gpsimd", lambda e, s=src, d=dbg_h[name]: e.dma_start(out=d, in_=s))
    if dbg:
        dwait("gpsimd", st_dbgg.tok())
        dwait("sync", st_dbg.tok())

    scope_cd = ExitStack()
    wcols = scope_cd.enter_context(nc.sbuf_tensor("wcols", [128, 4, NCH], F32))
    flrep = scope_cd.enter_context(nc.sbuf_tensor("flrep", [128, 512], I16))
    scope_c = ExitStack()
    f_c(scope_c, wcols, flrep)
    barrier()
    scope_c.close()
    scope_d = ExitStack()
    f_d(scope_d, wcols, flrep)
    barrier()
    scope_d.close()
    scope_cd.close()

    with nc.Block() as block:
        @block.sync
        def _(sync):
            for fn in p.q["sync"]:
                fn(sync)

        @block.tensor
        def _(tensor):
            for fn in p.q["tensor"]:
                fn(tensor)

        @block.vector
        def _(vector):
            for fn in p.q["vector"]:
                fn(vector)

        @block.scalar
        def _(scalar):
            for fn in p.q["scalar"]:
                fn(scalar)

        @block.gpsimd
        def _(gpsimd):
            for fn in p.q["gpsimd"]:
                fn(gpsimd)

    ctx.close()
    mybir.codegen_inst_isa_subclasses(nc)
    return nc


def make_consts():
    ident = np.eye(128, dtype=np.float32)
    trist = np.triu(np.ones((128, 128), np.float32), 1).astype(NPBF)
    base = np.tril(np.ones((CH, CH), np.float32)).T
    tri01 = np.ascontiguousarray(np.concatenate([base] * H, axis=1)).astype(NPBF)
    bdiag16 = np.zeros((H, H * CH), np.float32)
    negoh16 = np.zeros((H, H * CH), np.float32)
    for h in range(H):
        bdiag16[h, h * CH:(h + 1) * CH] = 1.0
        negoh16[h, h * CH:(h + 1) * CH] = -1.0
    bdiag32 = np.zeros((32, 32 * H), np.float32)
    for j in range(32):
        bdiag32[j, j * H:(j + 1) * H] = 1.0
    iotaj = (np.arange(256, dtype=np.float32) * 128.0).reshape(1, 256)
    pcol1 = (np.arange(128, dtype=np.float32) + 1.0).reshape(128, 1)
    return dict(ident=ident, tristb=trist, tri01=tri01,
                bdiag16=bdiag16.astype(NPBF), negoh16=negoh16.astype(NPBF),
                bdiag32=bdiag32, iotaj=iotaj, pcol1=pcol1)


def make_inmaps(inputs):
    x = np.ascontiguousarray(inputs["L0_attn_x"])
    consts = make_consts()
    wif = np.ascontiguousarray(np.concatenate(
        [inputs["L0_attn_w_i"], inputs["L0_attn_w_f"]], 1))
    bif = np.concatenate([inputs["L0_attn_b_i"], inputs["L0_attn_b_f"]]).reshape(2 * H, 1)
    rwt = np.ascontiguousarray(inputs["L0_ffn_router_w"].T)
    rb = np.ascontiguousarray(inputs["L0_ffn_router_b"].reshape(NE, 1))
    in_maps = []
    for c in range(NC):
        b = c // (NC // B)
        q = c % (NC // B)
        ohrep = np.zeros((128, NE), np.float32)
        valid = np.zeros((128, 1), np.float32)
        if c < 7:
            ohrep[:, c] = 1.0
            valid[:] = 1.0
        prevmask = np.zeros((NC, 1), np.float32)
        prevmask[(b * (NC // B)):c, 0] = 1.0
        prevmask16 = np.tile(prevmask.T, (H, 1))
        prevall = np.zeros((NC, 1), np.float32)
        prevall[:c, 0] = 1.0
        ec = min(c, 6)
        in_maps.append(dict(
            x=np.ascontiguousarray(x[b, q * TL:(q + 1) * TL]),
            wq=inputs["L0_attn_W_q"], wk=inputs["L0_attn_W_k"],
            wv=inputs["L0_attn_W_v"], wo=inputs["L0_attn_W_o"],
            bo=np.ascontiguousarray(inputs["L0_attn_b_o"].reshape(1, D)),
            wif=wif, bif=bif, rwt=rwt, rb=rb,
            gam=np.ascontiguousarray(inputs["L0_attn_gamma"].reshape(1, D)),
            bet=np.ascontiguousarray(inputs["L0_attn_beta"].reshape(1, D)),
            w1=np.ascontiguousarray(inputs["L0_ffn_experts_inter"][ec]),
            w2=np.ascontiguousarray(inputs["L0_ffn_experts_out"][ec]),
            ohrep=ohrep, valid=valid, prevmask=prevmask, prevall=prevall,
            prevmask16=prevmask16,
            **consts))
    return in_maps


def kernel(**inputs):
    in_maps = make_inmaps(inputs)
    nc = build_program()
    res = run_bass_kernel_spmd(nc, in_maps, core_ids=list(range(NC)))
    outs = [res.results[c]["out"] for c in range(NC)]
    return np.concatenate(outs, 0).reshape(B, T, D).astype(np.float32)




# version 12
# speedup vs baseline: 1.0207x; 1.0207x over previous
import numpy as np
import ml_dtypes
import concourse.bass as bass
import concourse.mybir as mybir
from concourse.bass_utils import run_bass_kernel_spmd
from concourse import library_config
from contextlib import ExitStack

F32 = mybir.dt.float32
BF16 = mybir.dt.bfloat16
I16 = mybir.dt.int16
U32 = mybir.dt.uint32
AF = mybir.ActivationFunctionType
ALU = mybir.AluOpType
NPBF = ml_dtypes.bfloat16
X_AX = mybir.AxisListType.X

NC, B, T, D, H, HD = 8, 2, 16384, 1024, 16, 64
TL, CH = 4096, 128
NCH = TL // CH
NE, CAP, MID = 8, 4096, 4096
NTOK = B * T
LN_EPS = 1e-5
TTILE = 512
NTILE = TL // TTILE
SUMROWS = H * (HD + 1) + 1
ENGS = ("sync", "tensor", "vector", "scalar", "gpsimd")
GINC = 16


class Prog:
    def __init__(self):
        self.q = {e: [] for e in ENGS}
        self.semcnt = {}

    def emit(self, eng, fn, sem=None, dn=0):
        if sem is not None:
            v = self.inc(sem, dn)
            self.q[eng].append(lambda e, fn=fn, sem=sem, dn=dn: fn(e).then_inc(sem, dn))
            return v
        self.q[eng].append(fn)
        return None

    def inc(self, sem, n):
        self.semcnt[sem.name] = self.semcnt.get(sem.name, 0) + n
        return self.semcnt[sem.name]

    def cur(self, sem):
        return self.semcnt.get(sem.name, 0)

    def wait(self, eng, sem, v=None):
        if v is None:
            v = self.cur(sem)
        self.q[eng].append(lambda e, sem=sem, v=v: e.wait_ge(sem, v))
        return v


def build_program(dbg=()):
    nc = bass.Bass("TRN2", target_bir_lowering=False)
    p = Prog()
    ctx = ExitStack()
    sb = lambda name, shape, dt: ctx.enter_context(nc.sbuf_tensor(name, shape, dt))
    ps_ = lambda name, shape: ctx.enter_context(nc.psum_tensor(name, shape, F32))
    mksem = lambda name: ctx.enter_context(nc.semaphore(name))

    def dram(name, shape, dt, kind=None):
        if kind:
            return nc.dram_tensor(name, shape, dt, kind=kind).ap()
        return nc.dram_tensor(name, shape, dt).ap()

    class DS:

        def __init__(self, name, nsems=1):
            self.sems = [mksem(f"{name}{k}") for k in range(nsems)]

        def issue(self, eng, fn, idx=0, dn=16):
            s = self.sems[idx % len(self.sems)]
            p.q[eng].append(lambda e, fn=fn, s=s: fn(e).then_inc(s, 16))
            v = p.inc(s, dn)
            return (s, v)

        def tok(self, idx=0):
            s = self.sems[idx % len(self.sems)]
            return (s, p.cur(s))

    def dwait(eng, token):
        p.wait(eng, token[0], token[1])

    x_h = dram("x", [TL, D], F32, "ExternalInput")
    wq_h = dram("wq", [D, D], F32, "ExternalInput")
    wk_h = dram("wk", [D, D], F32, "ExternalInput")
    wv_h = dram("wv", [D, D], F32, "ExternalInput")
    wo_h = dram("wo", [D, D], F32, "ExternalInput")
    bo_h = dram("bo", [1, D], F32, "ExternalInput")
    wif_h = dram("wif", [D, 2 * H], F32, "ExternalInput")
    bif_h = dram("bif", [2 * H, 1], F32, "ExternalInput")
    rwt_h = dram("rwt", [D, NE], F32, "ExternalInput")
    rb_h = dram("rb", [NE, 1], F32, "ExternalInput")
    gam_h = dram("gam", [1, D], F32, "ExternalInput")
    bet_h = dram("bet", [1, D], F32, "ExternalInput")
    w1_h = dram("w1", [D, MID], F32, "ExternalInput")
    w2_h = dram("w2", [MID, D], F32, "ExternalInput")
    ident_h = dram("ident", [128, 128], F32, "ExternalInput")
    tristb_h = dram("tristb", [128, 128], BF16, "ExternalInput")
    tri01_h = dram("tri01", [128, H * CH], BF16, "ExternalInput")
    bdiag16_h = dram("bdiag16", [H, H * CH], BF16, "ExternalInput")
    negoh16_h = dram("negoh16", [H, H * CH], BF16, "ExternalInput")
    bdiag32_h = dram("bdiag32", [32, 32 * H], F32, "ExternalInput")
    prevmask16_h = dram("prevmask16", [H, NC], F32, "ExternalInput")
    ohrep_h = dram("ohrep", [128, NE], F32, "ExternalInput")
    valid_h = dram("valid", [128, 1], F32, "ExternalInput")
    prevmask_h = dram("prevmask", [NC, 1], F32, "ExternalInput")
    prevall_h = dram("prevall", [NC, 1], F32, "ExternalInput")
    iotaj_h = dram("iotaj", [1, 256], F32, "ExternalInput")
    pcol1_h = dram("pcol1", [128, 1], F32, "ExternalInput")

    ag1a_in = dram("ag1a_in", [TL, D], BF16)
    ag1a_out = dram("ag1a_out", [NTOK, D], BF16)
    ag1b_in = dram("ag1b_in", [SUMROWS, HD], F32)
    ag1b_out = dram("ag1b_out", [NC * SUMROWS, HD], F32)
    ag1c_in = dram("ag1c_in", [TL, NE], F32)
    ag1c_out = dram("ag1c_out", [NTOK, NE], F32)
    lmrow_h = dram("lmrow", [8, 512], F32)
    ag2_in = dram("ag2_in", [CAP, D], BF16)
    ag2_out = dram("ag2_out", [NC * CAP, D], BF16)
    xnlo_hbm = dram("xnlo", [TL, D], BF16)
    qf_hbm = dram("qf", [D, TL], BF16)
    kf_hbm = dram("kf", [D, TL], BF16)
    vf_hbm = dram("vf", [D, TL], BF16)
    otm_hbm = dram("otm", [TL, D], BF16)
    hat_hbm = dram("hat", [TL, D], BF16)
    out_h = dram("out", [TL, D], F32, "ExternalOutput")
    scratch_map = dict(ag1a=ag1a_in, xnlo=xnlo_hbm, qf=qf_hbm, kf=kf_hbm,
                       vf=vf_hbm, otm=otm_hbm, ag1c=ag1c_in, ag1b=ag1b_in,
                       ag2in=ag2_in, ag1ao=ag1a_out, ag1bo=ag1b_out,
                       ag1co=ag1c_out, ag2o=ag2_out, hat=hat_hbm)
    dbg_h = {}

    cc = mksem("cc")
    es = {e: mksem("es_" + e) for e in ENGS}

    def mark(eng):
        if eng == "sync":
            return p.emit(eng, lambda x: x.nop(nofuse=True), es[eng], 1)
        return p.emit(eng, lambda x: x.drain(), es[eng], 1)

    def barrier(engines=ENGS, producers=ENGS):
        vals = {}
        for e in producers:
            vals[e] = mark(e)
        for e in engines:
            for pr in producers:
                if pr != e:
                    p.wait(e, es[pr], vals[pr])

    def dbg_dump(name, ap):
        if name not in dbg:
            return
        t = dram("dbg_" + name, list(ap.shape), ap.tensor.dtype, "ExternalOutput")
        dbg_h[name] = t
        vals = {e2: mark(e2) for e2 in ("vector", "scalar", "tensor", "gpsimd")}
        for e2, v in vals.items():
            p.wait("sync", es[e2], v)
        tk = st_dbg.issue("sync", lambda e, t=t, ap=ap: e.dma_start(out=t, in_=ap))
        for e2 in ("vector", "scalar", "tensor", "gpsimd"):
            dwait(e2, tk)

    def bc(main_ap, small_ap):
        a1, a2 = bass.broadcast_tensor_aps(main_ap, small_ap)
        return a2

    def vdrain():
        p.emit("vector", lambda e: e.drain())

    def sdrain():
        p.emit("scalar", lambda e: e.drain())

    rwh_sb = sb("rwh_sb", [128, 8, NE], BF16)
    rwl_sb = sb("rwl_sb", [128, 8, NE], BF16)
    rwf_sb = sb("rwf_sb", [128, 8, NE], F32)
    bo_sb = sb("bo_sb", [1, D], BF16)
    bif_sb = sb("bif_sb", [2 * H, 1], F32)
    bif_sb2 = sb("bif_sb2", [H, 1], F32)
    rb_sb = sb("rb_sb", [NE, 1], F32)
    ident_sb = sb("ident_sb", [128, 128], F32)
    trist_sb = sb("trist_sb", [128, 128], BF16)
    ohrep_sb = sb("ohrep_sb", [128, NE], F32)
    valid_sb = sb("valid_sb", [128, 1], F32)
    prevmask_sb = sb("prevmask_sb", [NC, 1], F32)
    prevall_sb = sb("prevall_sb", [NC, 1], F32)
    ones_sb = sb("ones_sb", [128, 128], F32)
    onesb_sb = sb("onesb_sb", [128, 128], BF16)
    lownT_sb = sb("lownT_sb", [128, NCH, NE], F32)
    ownsel_sb = sb("ownsel_sb", [128, NCH * NE], F32)
    sqdump = sb("sqdump", [128, D], BF16)
    eps_col = sb("eps_col", [128, 1], F32)
    scope_ab = ExitStack()
    sbab = lambda name, shape, dt: scope_ab.enter_context(nc.sbuf_tensor(name, shape, dt))
    gpool = sbab("gpool", [16, 4, TL], F32)
    gi_i = gpool[:, 0, :]
    omg_sb = gpool[:, 0, :]
    dec_sb = gpool[:, 0, :]
    gi_f = gpool[:, 1, :]
    aloc_sb = gpool[:, 1, :]
    denomI_sb = gpool[:, 1, :]
    logf_sb = gpool[0:8, 2, :]
    Fc_sb = gpool[:, 2, :]
    A_sb = gpool[:, 2, :]
    g_sb = gpool[:, 3, :]
    gpool2 = sbab("gpool2", [16, 512], F32)
    Lc_sb = gpool2[:, 0:NCH]
    Gc_sb = gpool2[:, 32:32 + NCH]
    prefL_sb = gpool2[:, 64:64 + NCH]
    Ghat_sb = gpool2[:, 96:97]
    Ftot_sb = gpool2[:, 97:98]
    GG_sb = gpool2[:, 98:130]
    po_sb = gpool2[:, 130:162]
    mu_sb = gpool2[:, 162:194]
    Mi_sb = gpool2[:, 194:195]
    Mx_sb = gpool2[:, 200:233]
    aarg_sb = gpool2[:, 240:272]
    barg_sb = gpool2[:, 272:304]
    bdiag16_sb = sbab("bdiag16_sb", [H, H * CH], BF16)
    negoh16_sb = sbab("negoh16_sb", [H, H * CH], BF16)
    bdiag32_sb = sbab("bdiag32_sb", [32, 32 * H], F32)
    prevmask16_sb = sbab("prevmask16_sb", [H, NC], F32)
    tri01_sb = sbab("tri01_sb", [128, H * CH], BF16)
    gT_sb = sbab("gT_sb", [128, NCH, H], F32)

    psA = ps_("psA", [128, 512])
    psB = ps_("psB", [128, 512])
    psC = ps_("psC", [128, 512])
    psD = ps_("psD", [128, 512])
    psE = ps_("psE", [128, 512])
    psF = ps_("psF", [128, 512])
    psG = ps_("psG", [128, 512])
    psH = ps_("psH", [128, 512])
    bank_guard = {}

    def guard(eng, bank):
        gv = bank_guard.get(id(bank))
        if gv is not None:
            p.wait(eng, gv[0], gv[1])

    def consumed(bank, semx, v=None):
        if v is None:
            v = p.cur(semx)
        bank_guard[id(bank)] = (semx, v)

    st_init = DS("sIni")
    st_initg = DS("sInG")
    st_a0x = DS("sA0x", 2)
    st_a0s = DS("sA0s", 2)
    st_ag1a = DS("sAGa")
    st_a1x = DS("sA1x", 2)
    st_qkv = DS("sQkv")
    st_o = DS("sO", 2)
    st_ag1c = DS("sAGc")
    st_a2kv = DS("sA2k", 2)
    st_scal = DS("sScl")
    st_ag1b = DS("sAGb")
    st_dbg = DS("sDbg")
    st_bp = DS("sBp")
    st_bp2 = DS("sBp2")
    st_bq = DS("sBq", 2)
    st_bk = DS("sBk", 2)
    st_bo = DS("sBo", 2)
    st_bh = DS("sBh", 2)
    st_dbgg = DS("sDbgG")
    st_cw = DS("sCw")
    st_cr = DS("sCr")
    st_cl = DS("sCl")
    st_cg = DS("sCg")
    st_cs = DS("sCs", 2)
    st_dc = DS("sDc")
    st_dld = DS("sDld", 2)
    st_dg = DS("sDg", 2)
    st_out = DS("sOut", 2)

    scope1 = ExitStack()
    sb1 = lambda name, shape, dt: scope1.enter_context(nc.sbuf_tensor(name, shape, dt))
    wq_sb = sb1("wq_sb", [128, 8, D], BF16)
    wk_sb = sb1("wk_sb", [128, 8, D], BF16)
    wv_sb = sb1("wv_sb", [128, 8, D], BF16)
    wo_sb = sb1("wo_sb", [128, 8, D], BF16)
    wif_sb = sb1("wif_sb", [128, 8, 2 * H], BF16)

    def f_init():
        for w_sb_, w_h_ in ((wq_sb, wq_h), (wk_sb, wk_h), (wv_sb, wv_h), (wo_sb, wo_h),
                            (wif_sb, wif_h)):
            for k in range(8):
                st_initg.issue("gpsimd", lambda e, o=w_sb_[:, k, :],
                               i=w_h_[128 * k:128 * (k + 1), :]: e.dma_start(out=o, in_=i))
        st_initg.issue("gpsimd", lambda e: e.dma_start(out=bo_sb[:], in_=bo_h))
        for k in range(8):
            st_init.issue("sync", lambda e, k=k: e.dma_start(
                out=rwf_sb[:, k, :], in_=rwt_h[128 * k:128 * (k + 1), :]))
        st_init.issue("sync", lambda e: e.dma_start(out=bif_sb2[:], in_=bif_h[H:2 * H, :]))
        for o, i in ((bif_sb[:], bif_h), (rb_sb[:], rb_h), (ident_sb[:], ident_h),
                     (trist_sb[:], tristb_h), (tri01_sb[:], tri01_h),
                     (bdiag16_sb[:], bdiag16_h), (negoh16_sb[:], negoh16_h),
                     (ohrep_sb[:], ohrep_h), (valid_sb[:], valid_h),
                     (bdiag32_sb[:], bdiag32_h), (prevmask16_sb[:], prevmask16_h),
                     (prevmask_sb[:], prevmask_h), (prevall_sb[:], prevall_h)):
            st_init.issue("sync", lambda e, o=o, i=i: e.dma_start(out=o, in_=i))
        dwait("vector", st_init.tok())
        dwait("vector", st_initg.tok())
        p.emit("vector", lambda e: e.memset(ones_sb[:], 1.0))
        p.emit("vector", lambda e: e.memset(onesb_sb[:], 1.0))
        p.emit("vector", lambda e: e.memset(eps_col[:], LN_EPS))
        for k in range(8):
            p.emit("vector", lambda e, k=k: e.tensor_copy(rwh_sb[:, k, :], rwf_sb[:, k, :]))
        vdrain()
        for k in range(8):
            p.emit("vector", lambda e, k=k: e.tensor_tensor(
                rwl_sb[:, k, :], rwf_sb[:, k, :], rwh_sb[:, k, :], ALU.subtract))
        mark("vector")

    def f_a0(scope):
        RT = 2
        T0T = RT * 128
        sbA = lambda name, shape, dt: scope.enter_context(nc.sbuf_tensor(name, shape, dt))
        xa = sbA("a0_x", [128, 2, RT, D], F32)
        xn32 = sbA("a0_xn", [128, RT, D], F32)
        xnh = sbA("a0_xnh", [128, RT, D], BF16)
        xnl = sbA("a0_xnl", [128, RT, D], BF16)
        mcol = sbA("a0_m", [128, RT], F32)
        vcol = sbA("a0_v", [128, RT], F32)
        rcol = sbA("a0_r", [128, RT], F32)
        ntl = TL // T0T
        stA_tok = {}
        for i in range(ntl):
            bufi = i % 2
            t0 = i * T0T
            if i >= 2:
                dwait("sync", stA_tok[i - 2])
            ldt = st_a0x.issue("sync", lambda e, t0=t0, bufi=bufi: e.dma_start(
                out=xa[:, bufi, :, :],
                in_=x_h[t0:t0 + T0T, :].rearrange("(r p) d -> p r d", p=128)), i)
            dwait("vector", ldt)
            for r in range(RT):
                p.emit("vector", lambda e, bufi=bufi, r=r: e.reduce_sum(
                    mcol[:, r:r + 1], xa[:, bufi, r, :], axis=X_AX))
            vdrain()
            p.emit("vector", lambda e: e.tensor_scalar(
                mcol[:], mcol[:], 1.0 / D, None, ALU.mult))
            vdrain()
            for r in range(RT):
                p.emit("vector", lambda e, bufi=bufi, r=r: e.tensor_scalar(
                    xn32[:, r, :], xa[:, bufi, r, :], mcol[:, r:r + 1], None,
                    ALU.subtract))
            mark("vector")
            p.wait("scalar", es["vector"])
            for r in range(RT):
                p.emit("scalar", lambda e, r=r: e.activation(
                    sqdump[:, :], xn32[:, r, :], AF.Square, accum_out=vcol[:, r:r + 1]))
                sdrain()
            p.emit("scalar", lambda e: e.activation(
                rcol[:], vcol[:], AF.Ln, scale=1.0 / D, bias=eps_col[:]))
            sdrain()
            p.emit("scalar", lambda e: e.activation(
                rcol[:], rcol[:], AF.Exp, scale=-0.5), es["scalar"], 1)
            p.wait("vector", es["scalar"])
            if i >= 1:
                dwait("vector", stA_tok[i - 1])
                dwait("vector", st_ag1a.tok())
            for r in range(RT):
                p.emit("vector", lambda e, r=r: e.tensor_scalar(
                    xnh[:, r, :], xn32[:, r, :], rcol[:, r:r + 1], None, ALU.mult))
            vdrain()
            for r in range(RT):
                p.emit("vector", lambda e, r=r: e.scalar_tensor_tensor(
                    xnl[:, r, :], xn32[:, r, :], rcol[:, r:r + 1], xnh[:, r, :],
                    ALU.mult, ALU.subtract))
            stv = mark("vector")
            p.wait("sync", es["vector"], stv)
            st_ag1a.issue("sync", lambda e, t0=t0: e.dma_start(
                out=ag1a_in[t0:t0 + T0T, :].rearrange("(r p) d -> p r d", p=128),
                in_=xnh[:, :, :]))
            stA_tok[i] = st_a0s.issue("sync", lambda e, t0=t0: e.dma_start(
                out=xnlo_hbm[t0:t0 + T0T, :].rearrange("(r p) d -> p r d", p=128),
                in_=xnl[:, :, :]), i)
        dwait("sync", st_ag1a.tok())
        dwait("sync", st_a0s.tok(0))
        dwait("sync", st_a0s.tok(1))
        return st_ag1a.tok()

    def f_a1(scope):
        sbA = lambda name, shape, dt: scope.enter_context(nc.sbuf_tensor(name, shape, dt))
        xh = sbA("a1_xh", [128, 2, 8, TTILE], BF16)
        xl = sbA("a1_xl", [128, 2, 8, TTILE], BF16)
        qkv_sb = sbA("a1_qkv", [128, 8, TTILE], BF16)
        osig_sb = sbA("a1_osig", [128, 2, D], BF16)
        qkv_tok = [None]
        osig_tok = {0: None, 1: None}
        tile_done = {}
        for i in range(NTILE):
            bufi = i % 2
            t0 = i * TTILE
            if i >= 2:
                p.wait("sync", es["tensor"], tile_done[i - 2])
            for k in range(8):
                st_a1x.issue("sync", lambda e, k=k, bufi=bufi, t0=t0: e.dma_start(
                    out=xh[:, bufi, k, :],
                    in_=ag1a_in[t0:t0 + TTILE, 128 * k:128 * (k + 1)], transpose=True), i)
                st_a1x.issue("sync", lambda e, k=k, bufi=bufi, t0=t0: e.dma_start(
                    out=xl[:, bufi, k, :],
                    in_=xnlo_hbm[t0:t0 + TTILE, 128 * k:128 * (k + 1)], transpose=True), i)
            ldt = st_a1x.tok(i)
            dwait("tensor", ldt)
            if i == 0:
                dwait("tensor", st_initg.tok())
                p.wait("tensor", es["vector"])
            for w_sb_, hbm in ((wq_sb, qf_hbm), (wk_sb, kf_hbm), (wv_sb, vf_hbm)):
                if qkv_tok[0] is not None:
                    dwait("vector", qkv_tok[0])
                for hc in range(8):
                    bank = (psA, psB)[hc % 2]
                    guard("tensor", bank)
                    for k in range(8):
                        p.emit("tensor", lambda e, w=w_sb_, hc=hc, k=k, bufi=bufi,
                               bank=bank: e.matmul(
                                   bank[:, :], w[:, k, 128 * hc:128 * (hc + 1)],
                                   xh[:, bufi, k, :], start=(k == 0), stop=(k == 7)),
                               es["tensor"] if k == 7 else None, 1)
                    p.wait("vector", es["tensor"])
                    scale = 0.125 if hbm is qf_hbm else 1.0
                    p.emit("vector", lambda e, hc=hc, bank=bank, scale=scale:
                           e.tensor_scalar(qkv_sb[:, hc, :], bank[:, :], scale, None,
                                           ALU.mult), es["vector"], 1)
                    consumed(bank, es["vector"])
                stv = p.cur(es["vector"])
                p.wait("sync", es["vector"], stv)
                for hc in range(8):
                    qkv_tok[0] = st_qkv.issue("sync", lambda e, hc=hc, t0=t0, hbm=hbm:
                                              e.dma_start(
                        out=hbm[128 * hc:128 * (hc + 1), t0:t0 + TTILE],
                        in_=qkv_sb[:, hc, :]))
            guard("tensor", psC)
            guard("tensor", psH)
            for half, bank in ((0, psC), (1, psH)):
                for k in range(8):
                    p.emit("tensor", lambda e, k=k, bufi=bufi, half=half, bank=bank:
                           e.matmul(bank[0:H, :], wif_sb[:, k, H * half:H * (half + 1)],
                                    xh[:, bufi, k, :], start=(k == 0), stop=(k == 7)),
                           es["tensor"] if (k == 7 and half == 1) else None, 1)
            p.wait("vector", es["tensor"])
            p.emit("vector", lambda e, t0=t0: e.tensor_scalar(
                gi_i[:, t0:t0 + TTILE], psC[0:H, :], bif_sb[0:H, :], None, ALU.add))
            p.emit("vector", lambda e, t0=t0: e.tensor_scalar(
                gi_f[:, t0:t0 + TTILE], psH[0:H, :], bif_sb2[:], None, ALU.add),
                es["vector"], 1)
            consumed(psC, es["vector"])
            consumed(psH, es["vector"])
            guard("tensor", psD)
            for term, (wl, xr) in enumerate(((rwh_sb, xh), (rwl_sb, xh), (rwh_sb, xl))):
                for k in range(8):
                    last = (term == 2 and k == 7)
                    p.emit("tensor", lambda e, k=k, bufi=bufi, wl=wl, xr=xr,
                           stt=(term == 0 and k == 0), spp=last: e.matmul(
                               psD[0:NE, :], wl[:, k, :], xr[:, bufi, k, :],
                               start=stt, stop=spp),
                           es["tensor"] if last else None, 1)
            p.wait("vector", es["tensor"])
            p.emit("vector", lambda e, t0=t0: e.tensor_scalar(
                logf_sb[:, t0:t0 + TTILE], psD[0:NE, :], rb_sb[:], None, ALU.add),
                es["vector"], 1)
            consumed(psD, es["vector"])
            for tt in range(4):
                obufi = tt % 2
                for seg in range(2):
                    bank = (psE, psF)[seg]
                    guard("tensor", bank)
                    for k in range(8):
                        p.emit("tensor", lambda e, k=k, tt=tt, seg=seg, bufi=bufi,
                               bank=bank: e.matmul(
                                   bank[:, :], xh[:, bufi, k, 128 * tt:128 * (tt + 1)],
                                   wo_sb[:, k, 512 * seg:512 * (seg + 1)],
                                   start=(k == 0), stop=False))
                    p.emit("tensor", lambda e, seg=seg, bank=bank: e.matmul(
                        bank[:, :], onesb_sb[0:1, :],
                        bo_sb[0:1, 512 * seg:512 * (seg + 1)],
                        start=False, stop=True), es["tensor"] if seg == 1 else None, 1)
                p.wait("scalar", es["tensor"])
                if osig_tok[obufi] is not None:
                    dwait("scalar", osig_tok[obufi])
                for seg in range(2):
                    p.emit("scalar", lambda e, seg=seg, obufi=obufi: e.activation(
                        osig_sb[:, obufi, 512 * seg:512 * (seg + 1)],
                        (psE, psF)[seg][:, :], AF.Sigmoid),
                        es["scalar"] if seg == 1 else None, 1)
                consumed(psE, es["scalar"])
                consumed(psF, es["scalar"])
                p.wait("sync", es["scalar"])
                osig_tok[obufi] = st_o.issue("sync", lambda e, t0=t0, tt=tt,
                                             obufi=obufi: e.dma_start(
                    out=otm_hbm[t0 + 128 * tt:t0 + 128 * (tt + 1), :],
                    in_=osig_sb[:, obufi, :]), obufi)
            tile_done[i] = p.cur(es["tensor"])
        for j in range(NCH):
            guard("tensor", psG)
            p.emit("tensor", lambda e, j=j: e.transpose(
                psG[:, 0:NE], logf_sb[:, CH * j:CH * (j + 1)], ident_sb[0:NE, 0:NE]),
                es["tensor"], 1)
            p.wait("vector", es["tensor"])
            p.emit("vector", lambda e, j=j: e.tensor_copy(
                lownT_sb[:, j, :], psG[:, 0:NE]), es["vector"], 1)
            consumed(psG, es["vector"])
        lv = mark("vector")
        p.wait("sync", es["vector"], lv)
        st_ag1c.issue("sync", lambda e: e.dma_start(
            out=ag1c_in.rearrange("(j p) x -> p j x", p=128), in_=lownT_sb[:, :, :]))
        dwait("sync", st_qkv.tok())
        dwait("sync", st_o.tok(0))
        dwait("sync", st_o.tok(1))
        dwait("sync", st_a1x.tok(0))
        dwait("sync", st_a1x.tok(1))
        return st_ag1c.tok()

    def f_a2(scope):
        sbA = lambda name, shape, dt: scope.enter_context(nc.sbuf_tensor(name, shape, dt))
        neg_col = sbA("neg_col", [H, 1], F32)
        ktm2 = sbA("ktm2", [128, 2, D], BF16)
        vtm2 = sbA("vtm2", [128, 2, D], BF16)
        vw_sb = sbA("vw_sb", [128, H * (HD + 1)], BF16)
        wT_sb = sbA("wT_sb", [128, H], F32)
        sum_sb = sbA("sum_sb", [65, 1024], F32)
        scal_sb = sbA("scal_sb", [1, HD], F32)
        cnt_sb = sbA("cnt_sb", [128, NCH * NE], F32)
        tmp8_sb = sbA("tmp8_sb", [128, NCH * NE], F32)
        cnt8_sb = sbA("cnt8_sb", [128, NE], F32)
        p.emit("vector", lambda e: e.memset(neg_col[:], -1e30))
        p.emit("vector", lambda e: e.memset(scal_sb[:], 0.0))
        vdrain()
        for j in range(NCH):
            js = slice(CH * j, CH * (j + 1))
            p.emit("vector", lambda e, js=js: e.tensor_tensor_scan(
                Fc_sb[:, js], gi_f[:, js], gi_f[:, js], 0.0, ALU.add, ALU.bypass))
        vdrain()
        p.emit("vector", lambda e: e.tensor_tensor(
            g_sb[:, :], gi_i[:, :], Fc_sb[:, :], ALU.subtract))
        vdrain()
        for j in range(NCH):
            js = slice(CH * j, CH * (j + 1))
            p.emit("vector", lambda e, js=js: e.tensor_tensor_scan(
                aloc_sb[:, js], g_sb[:, js], g_sb[:, js], neg_col[:], ALU.max,
                ALU.bypass))
        vdrain()
        p.emit("vector", lambda e: e.tensor_copy(Lc_sb[:, :], Fc_sb[:, 127::128]))
        p.emit("vector", lambda e: e.tensor_copy(Gc_sb[:, :], aloc_sb[:, 127::128]))
        vdrain()
        p.emit("vector", lambda e: e.tensor_tensor_scan(
            GG_sb[:, :], Lc_sb[:, :], Lc_sb[:, :], 0.0, ALU.add, ALU.bypass))
        vdrain()
        p.emit("vector", lambda e: e.memset(prefL_sb[:, 0:1], 0.0))
        p.emit("vector", lambda e: e.tensor_copy(prefL_sb[:, 1:NCH], GG_sb[:, 0:NCH - 1]))
        p.emit("vector", lambda e: e.tensor_copy(Ftot_sb[:, :], GG_sb[:, NCH - 1:NCH]))
        vdrain()
        p.emit("vector", lambda e: e.tensor_tensor(
            GG_sb[:, :], Gc_sb[:, :], prefL_sb[:, :], ALU.subtract))
        vdrain()
        p.emit("vector", lambda e: e.reduce_max(Ghat_sb[:, :], GG_sb[:, :], axis=X_AX))
        vdrain()
        p.emit("vector", lambda e: e.tensor_scalar(
            po_sb[:, :], prefL_sb[:, :], Ghat_sb[:, :], None, ALU.add))
        vdrain()
        g3 = g_sb[:, :].rearrange("h (j t) -> h j t", t=CH)
        p.emit("vector", lambda e: e.tensor_tensor(
            Fc_sb[:, :].rearrange("h (j t) -> h j t", t=CH), g3,
            bc(g3, po_sb[:, :].rearrange("h (j o) -> h j o", o=1)), ALU.subtract),
            es["vector"], 1)
        p.wait("scalar", es["vector"])
        p.emit("scalar", lambda e: e.activation(omg_sb[:, :], Fc_sb[:, :], AF.Exp),
               es["scalar"], 1)
        p.emit("vector", lambda e: e.memset(cnt_sb[:], 0.0))
        vdrain()
        l3 = lownT_sb[:, :, :]
        for ep in range(NE):
            p.emit("vector", lambda e, ep=ep: e.tensor_tensor(
                tmp8_sb[:].rearrange("p (j x) -> p j x", x=NE),
                bc(l3, lownT_sb[:, :, ep:ep + 1]), l3, ALU.is_gt))
            vdrain()
            p.emit("vector", lambda e: e.tensor_tensor(
                cnt_sb[:], cnt_sb[:], tmp8_sb[:], ALU.add))
            vdrain()
        p.emit("vector", lambda e: e.tensor_scalar(
            ownsel_sb[:], cnt_sb[:], 1.5, None, ALU.is_lt))
        vdrain()
        p.emit("vector", lambda e: e.reduce_sum(
            cnt8_sb[:, :], ownsel_sb[:].rearrange("p (j x) -> p x j", x=NE), axis=X_AX),
            es["vector"], 1)
        p.wait("tensor", es["vector"])
        guard("tensor", psD)
        p.emit("tensor", lambda e: e.matmul(psD[0:1, 0:NE], ones_sb[:, 0:1],
                                            cnt8_sb[:, :], start=True, stop=True),
               es["tensor"], 1)
        p.wait("vector", es["tensor"])
        p.emit("vector", lambda e: e.tensor_copy(scal_sb[0:1, 32:32 + NE],
                                                 psD[0:1, 0:NE]), es["vector"], 1)
        consumed(psD, es["vector"])
        scv = mark("vector")
        p.wait("sync", es["vector"], scv)
        st_scal.issue("sync", lambda e: e.dma_start(out=scal_sb[0:1, 0:H],
                                                    in_=Ghat_sb[:, :]))
        st_scal.issue("sync", lambda e: e.dma_start(out=scal_sb[0:1, H:2 * H],
                                                    in_=Ftot_sb[:, :]))
        ktm_marks = {}
        for j in range(NCH):
            bufi = j % 2
            js = slice(CH * j, CH * (j + 1))
            if j >= 2:
                p.wait("sync", es["tensor"], ktm_marks[j - 2])
            st_a2kv.issue("sync", lambda e, j=j, bufi=bufi: e.dma_start(
                out=ktm2[:, bufi, :], in_=kf_hbm[:, CH * j:CH * (j + 1)],
                transpose=True), j)
            st_a2kv.issue("sync", lambda e, j=j, bufi=bufi: e.dma_start(
                out=vtm2[:, bufi, :], in_=vf_hbm[:, CH * j:CH * (j + 1)],
                transpose=True), j)
            ldt = st_a2kv.tok(j)
            guard("tensor", psG)
            if j == 0:
                p.wait("tensor", es["scalar"])
            p.emit("tensor", lambda e, js=js: e.transpose(
                psG[:, 0:H], omg_sb[:, js], ident_sb[0:H, 0:H]), es["tensor"], 1)
            p.emit("tensor", lambda e, js=js: e.transpose(
                psG[:, H:2 * H], g_sb[:, js], ident_sb[0:H, 0:H]), es["tensor"], 1)
            p.wait("vector", es["tensor"])
            if j >= 1:
                p.wait("vector", es["tensor"], ktm_marks[j - 1])
            p.emit("vector", lambda e: e.tensor_copy(wT_sb[:, :], psG[:, 0:H]))
            p.emit("vector", lambda e, j=j: e.tensor_copy(gT_sb[:, j, :],
                                                          psG[:, H:2 * H]),
                   es["vector"], 1)
            consumed(psG, es["vector"])
            vdrain()
            dwait("vector", ldt)
            v3 = vtm2[:, bufi, :].rearrange("p (h x) -> p h x", x=HD)
            p.emit("vector", lambda e, v3=v3: e.tensor_tensor(
                vw_sb[:].rearrange("p (h x) -> p h x", x=HD + 1)[:, :, 0:HD],
                v3, bc(v3, wT_sb[:, :].rearrange("p (h o) -> p h o", o=1)), ALU.mult))
            p.emit("vector", lambda e: e.tensor_copy(
                vw_sb[:].rearrange("p (h x) -> p h x", x=HD + 1)[:, :, HD:HD + 1],
                wT_sb[:, :].rearrange("p (h o) -> p h o", o=1)), es["vector"], 1)
            p.wait("tensor", es["vector"])
            for h in range(H):
                bank = psE if h < 8 else psF
                if j == 0 and h % 8 == 0:
                    guard("tensor", bank)
                p.emit("tensor", lambda e, h=h, j=j, bufi=bufi, bank=bank: e.matmul(
                    bank[0:HD + 1, HD * (h % 8):HD * (h % 8 + 1)],
                    vw_sb[:, (HD + 1) * h:(HD + 1) * (h + 1)],
                    ktm2[:, bufi, HD * h:HD * (h + 1)],
                    start=(j == 0 and h % 8 == 0), stop=(j == NCH - 1 and h % 8 == 7)),
                    es["tensor"] if h == H - 1 else None, 1)
            ktm_marks[j] = p.cur(es["tensor"])
        p.wait("vector", es["tensor"])
        p.emit("vector", lambda e: e.tensor_copy(sum_sb[0:HD + 1, 0:512],
                                                 psE[0:HD + 1, :]))
        p.emit("vector", lambda e: e.tensor_copy(sum_sb[0:HD + 1, 512:1024],
                                                 psF[0:HD + 1, :]), es["vector"], 1)
        consumed(psE, es["vector"])
        consumed(psF, es["vector"])
        sv2 = mark("vector")
        p.wait("sync", es["vector"], sv2)
        dwait("sync", st_scal.tok())
        for h in range(H):
            st_ag1b.issue("sync", lambda e, h=h: e.dma_start(
                out=ag1b_in[(HD + 1) * h:(HD + 1) * (h + 1), 0:HD],
                in_=sum_sb[0:HD + 1, HD * h:HD * (h + 1)]))
        st_ag1b.issue("sync", lambda e: e.dma_start(
            out=ag1b_in[H * (HD + 1):H * (HD + 1) + 1, :], in_=scal_sb[:]))
        dwait("sync", st_ag1b.tok())
        dwait("sync", st_a2kv.tok(0))
        dwait("sync", st_a2kv.tok(1))
        return st_ag1b.tok()


    def f_b(scope):
        sbA = lambda name, shape, dt: scope.enter_context(nc.sbuf_tensor(name, shape, dt))
        ghi = sbA("b_ghi", [H, TL], BF16)
        glo = sbA("b_glo", [H, TL], BF16)
        Ahi = sbA("b_Ahi", [H, TL], BF16)
        Alo = sbA("b_Alo", [H, TL], BF16)
        qfm = sbA("b_qfm", [64, 2, H, CH], BF16)
        kfm = sbA("b_kfm", [64, 2, H, CH], BF16)
        vfm = sbA("b_vfm", [64, 2, H, CH], BF16)
        ktmb = sbA("b_ktm", [128, 2, D], BF16)
        vtmb = sbA("b_vtm", [128, 2, D], BF16)
        otm2 = sbA("b_otm", [128, 2, D], BF16)
        Dm = sbA("b_Dm", [128, 8 * CH], BF16)
        DSv = sbA("b_DSv", [128, 8 * CH], BF16)
        DSk = sbA("b_DSk", [128, 8 * CH], BF16)
        vwb = sbA("b_vwb", [128, D], BF16)
        wlT = sbA("b_wlT", [128, H], BF16)
        decT_s = sbA("b_decT", [128, NCH, H], F32)
        cst = sbA("b_cst", [64, 2, H * (HD + 1)], BF16)
        interS = sbA("b_interS", [128, 2 * 264], F32)
        den_sb = sbA("b_den", [128, NE], F32)
        nqr_sb = sbA("b_nqr", [128, NE], F32)
        numer = sbA("b_numer", [128, 512], F32)
        h2 = sbA("b_h2", [128, 2, D], BF16)
        Gh8 = sbA("b_Gh8", [NC, 32], F32)
        GhT = sbA("b_GhT", [H, NC], F32)
        FtT = sbA("b_FtT", [H, NC], F32)
        GhM = sbA("b_GhM", [H, NC], F32)
        FtM = sbA("b_FtM", [H, NC], F32)
        CSc = sbA("b_CSc", [H, NC], F32)
        macc = sbA("b_macc", [H, 1], F32)
        csum = sbA("b_csum", [65, NC * HD], F32)
        CIN = sbA("b_CIN", [65, H * HD], F32)
        CSb = sbA("b_CSb", [H, NC], BF16)
        sccol8 = sbA("b_sccol8", [65, NC], F32)
        arep = sbA("b_arep", [64, NCH * H], F32)
        brep = sbA("b_brep", [128, NCH * H], F32)
        aT32 = sbA("b_aT32", [32, H], F32)
        aTexp = sbA("b_aTexp", [32, 32 * H], F32)

        p.wait("sync", cc, ag_tok["b"])
        p.wait("vector", cc, ag_tok["b"])
        p.wait("tensor", cc, ag_tok["b"])
        p.wait("scalar", cc, ag_tok["b"])
        sr = H * (HD + 1)
        t1 = st_bp.issue("sync", lambda e: e.dma_start(
            out=Gh8[:, :], in_=ag1b_out.rearrange("(k r) x -> k r x", r=SUMROWS)[:, sr, 0:32]))
        dwait("tensor", t1)
        guard("tensor", psH)
        p.emit("tensor", lambda e: e.transpose(
            psH[0:H, 0:NC], Gh8[:, 0:H], ident_sb[0:NC, 0:NC]), es["tensor"], 1)
        p.emit("tensor", lambda e: e.transpose(
            psH[0:H, NC:2 * NC], Gh8[:, H:2 * H], ident_sb[0:NC, 0:NC]), es["tensor"], 1)
        p.wait("vector", es["tensor"])
        p.emit("vector", lambda e: e.tensor_copy(GhT[:, :], psH[0:H, 0:NC]))
        p.emit("vector", lambda e: e.tensor_copy(FtT[:, :], psH[0:H, NC:2 * NC]),
               es["vector"], 1)
        consumed(psH, es["vector"])
        vdrain()
        p.emit("vector", lambda e: e.tensor_scalar(
            CSc[:, :], prevmask16_sb[:, :], 1e30, None, ALU.mult))
        p.emit("vector", lambda e: e.tensor_tensor(GhM[:, :], GhT[:, :],
                                                   prevmask16_sb[:, :], ALU.mult))
        vdrain()
        p.emit("vector", lambda e: e.tensor_scalar(
            CSc[:, :], CSc[:, :], 1e30, None, ALU.subtract))
        vdrain()
        p.emit("vector", lambda e: e.tensor_tensor(GhM[:, :], GhM[:, :],
                                                   CSc[:, :], ALU.add))
        p.emit("vector", lambda e: e.tensor_tensor(FtM[:, :], FtT[:, :],
                                                   prevmask16_sb[:, :], ALU.mult))
        vdrain()
        p.emit("vector", lambda e: e.memset(macc[:], 0.0))
        vdrain()
        for k in range(NC):
            p.emit("vector", lambda e, k=k: e.tensor_tensor(
                macc[:, :], macc[:, :], GhM[:, k:k + 1], ALU.max))
            vdrain()
            p.emit("vector", lambda e, k=k: e.tensor_tensor(
                macc[:, :], macc[:, :], FtM[:, k:k + 1], ALU.add))
            vdrain()
        p.emit("vector", lambda e: e.tensor_copy(Mi_sb[:, :], macc[:, :]))
        vdrain()
        p.emit("vector", lambda e: e.tensor_tensor_scan(
            CSc[:, :], FtM[:, :], FtM[:, :], 0.0, ALU.add, ALU.bypass))
        vdrain()
        p.emit("vector", lambda e: e.scalar_tensor_tensor(
            CSc[:, :], CSc[:, :], -1.0, GhM[:, :], ALU.mult, ALU.add))
        vdrain()
        p.emit("vector", lambda e: e.tensor_scalar(
            CSc[:, :], CSc[:, :], CSc[:, 7:8], None, ALU.add) if False else
            e.tensor_tensor(CSc[:, :], CSc[:, :], FtM[:, :], ALU.add))
        vdrain()
        p.emit("vector", lambda e: e.reduce_sum(macc[:, :], FtM[:, :], axis=X_AX))
        vdrain()
        p.emit("vector", lambda e: e.tensor_scalar(
            CSc[:, :], CSc[:, :], macc[:, :], None, ALU.add))
        vdrain()
        p.emit("vector", lambda e: e.tensor_scalar(
            CSc[:, :], CSc[:, :], Mi_sb[:, :], None, ALU.subtract), es["vector"], 1)
        p.wait("scalar", es["vector"])
        p.emit("scalar", lambda e: e.activation(CSc[:, :], CSc[:, :], AF.Exp),
               es["scalar"], 1)
        p.wait("vector", es["scalar"])
        p.emit("vector", lambda e: e.tensor_copy(CSb[:, :], CSc[:, :]), es["vector"], 1)
        p.emit("vector", lambda e: e.memset(CIN[:, :], 0.0))
        p.emit("vector", lambda e: e.memset(cst[:, 0, :], 0.0))
        p.emit("vector", lambda e: e.memset(cst[:, 1, :], 0.0))
        mark("vector")
        p.wait("tensor", es["vector"])
        for h in range(H):
            th = st_bp2.issue("sync", lambda e, h=h: e.dma_start(
                out=csum[:, :].rearrange("r (k x) -> r k x", x=HD),
                in_=ag1b_out.rearrange("(k r) x -> k r x", r=SUMROWS)
                [:, (HD + 1) * h:(HD + 1) * (h + 1), 0:HD].rearrange("k r x -> r k x")))
            guard("tensor", psG)
            p.emit("tensor", lambda e, h=h: e.matmul(
                psG[0:65, 0:NC], bdiag16_sb[:, CH * h:CH * h + 65], CSb[:, :],
                start=True, stop=True), es["tensor"], 1)
            p.wait("vector", es["tensor"])
            p.emit("vector", lambda e: e.tensor_copy(sccol8[:, :], psG[0:65, 0:NC]),
                   es["vector"], 1)
            consumed(psG, es["vector"])
            vdrain()
            dwait("vector", th)
            dwait("tensor", th)
            for k in range(NC):
                p.emit("vector", lambda e, h=h, k=k: e.scalar_tensor_tensor(
                    CIN[:, HD * h:HD * (h + 1)], csum[:, HD * k:HD * (k + 1)],
                    sccol8[:, k:k + 1], CIN[:, HD * h:HD * (h + 1)], ALU.mult, ALU.add))
                vdrain()
            mark("vector")
            p.wait("sync", es["vector"])
        for h in range(H):
            p.emit("vector", lambda e, h=h: e.tensor_copy(
                cst[:, 0, (HD + 1) * h:(HD + 1) * h + HD], CIN[0:64, HD * h:HD * (h + 1)]))
        mark("vector")
        p.wait("tensor", es["vector"])
        for h in range(H):
            guard("tensor", psG)
            p.emit("tensor", lambda e, h=h: e.transpose(
                psG[0:HD, 0:1], CIN[64:65, HD * h:HD * (h + 1)], ident_sb[64:65, 64:65]),
                es["tensor"], 1)
            p.wait("vector", es["tensor"])
            p.emit("vector", lambda e, h=h: e.tensor_copy(
                cst[:, 0, (HD + 1) * h + HD:(HD + 1) * (h + 1)], psG[0:HD, 0:1]),
                es["vector"], 1)
            consumed(psG, es["vector"])
        vdrain()
        p.emit("vector", lambda e: e.tensor_copy(Mx_sb[:, 0:1], Mi_sb[:, :]))
        p.emit("vector", lambda e: e.tensor_tensor_scan(
            Mx_sb[:, 1:NCH + 1], GG_sb[:, :], GG_sb[:, :], Mi_sb[:, :],
            ALU.max, ALU.bypass))
        vdrain()
        p.emit("vector", lambda e: e.tensor_tensor(
            aarg_sb[:, :], Mx_sb[:, 0:NCH], Mx_sb[:, 1:NCH + 1], ALU.subtract))
        p.emit("vector", lambda e: e.tensor_tensor(
            barg_sb[:, :], GG_sb[:, :], Mx_sb[:, 1:NCH + 1], ALU.subtract))
        vdrain()
        p.emit("vector", lambda e: e.tensor_tensor(
            mu_sb[:, :], Mx_sb[:, 0:NCH], prefL_sb[:, :], ALU.add))
        vdrain()
        al3 = aloc_sb[:, :].rearrange("h (j t) -> h j t", t=CH)
        p.emit("vector", lambda e: e.tensor_tensor(
            A_sb[:, :].rearrange("h (j t) -> h j t", t=CH), al3,
            bc(al3, mu_sb[:, :].rearrange("h (j o) -> h j o", o=1)), ALU.max))
        vdrain()
        decB = denomI_sb
        A3 = A_sb[:, :].rearrange("h (j t) -> h j t", t=CH)
        p.emit("vector", lambda e: e.tensor_tensor(
            decB[:, :].rearrange("h (j t) -> h j t", t=CH),
            bc(A3, mu_sb[:, :].rearrange("h (j o) -> h j o", o=1)), A3, ALU.subtract),
            es["vector"], 1)
        p.wait("scalar", es["vector"])
        p.emit("scalar", lambda e: e.activation(decB[:, :], decB[:, :], AF.Exp),
               es["scalar"], 1)
        gg3 = g_sb[:, :].rearrange("h (j t) -> h j t", t=CH)
        p.emit("vector", lambda e: e.tensor_tensor(
            omg_sb[:, :].rearrange("h (j t) -> h j t", t=CH), gg3,
            bc(gg3, Gc_sb[:, :].rearrange("h (j o) -> h j o", o=1)), ALU.subtract),
            es["vector"], 1)
        p.wait("scalar", es["vector"])
        p.emit("scalar", lambda e: e.activation(omg_sb[:, :], omg_sb[:, :], AF.Exp),
               es["scalar"], 1)
        wloc_sb = omg_sb
        p.emit("vector", lambda e: e.tensor_copy(ghi[:, :], g_sb[:, :]))
        p.emit("vector", lambda e: e.tensor_copy(Ahi[:, :], A_sb[:, :]))
        vdrain()
        p.emit("vector", lambda e: e.tensor_tensor(glo[:, :], g_sb[:, :], ghi[:, :],
                                                   ALU.subtract))
        p.emit("vector", lambda e: e.tensor_tensor(Alo[:, :], A_sb[:, :], Ahi[:, :],
                                                   ALU.subtract), es["vector"], 1)
        p.wait("tensor", es["scalar"])
        p.wait("tensor", es["vector"])
        for j in range(NCH):
            guard("tensor", psG)
            p.emit("tensor", lambda e, j=j: e.transpose(
                psG[:, 0:H], decB[:, CH * j:CH * (j + 1)], ident_sb[0:H, 0:H]),
                es["tensor"], 1)
            p.wait("vector", es["tensor"])
            p.emit("vector", lambda e, j=j: e.tensor_copy(decT_s[:, j, :], psG[:, 0:H]),
                   es["vector"], 1)
            consumed(psG, es["vector"])
        p.wait("scalar", es["vector"])
        p.emit("scalar", lambda e: e.activation(aarg_sb[:, :], aarg_sb[:, :], AF.Exp))
        sdrain()
        p.emit("scalar", lambda e: e.activation(barg_sb[:, :], barg_sb[:, :], AF.Exp),
               es["scalar"], 1)
        p.wait("tensor", es["scalar"])
        for name_, src_, dst_, rows in (("a", aarg_sb, arep, 64), ("b", barg_sb, brep, 128)):
            guard("tensor", psG)
            p.emit("tensor", lambda e, src_=src_: e.transpose(
                psG[0:32, 0:H], src_[:, :], ident_sb[0:H, 0:H]), es["tensor"], 1)
            p.wait("vector", es["tensor"])
            p.emit("vector", lambda e: e.tensor_copy(aT32[:, :], psG[0:32, 0:H]),
                   es["vector"], 1)
            consumed(psG, es["vector"])
            vdrain()
            a3 = bdiag32_sb[:, :].rearrange("k (j h) -> k j h", h=H)
            p.emit("vector", lambda e, a3=a3: e.tensor_tensor(
                aTexp[:, :].rearrange("k (j h) -> k j h", h=H), a3,
                bc(a3, aT32[:, :].rearrange("k (o h) -> k o h", o=1)), ALU.mult),
                es["vector"], 1)
            p.wait("tensor", es["vector"])
            for seg in range(0, NCH * H, 512):
                guard("tensor", psG)
                p.emit("tensor", lambda e, seg=seg, rows=rows: e.matmul(
                    psG[0:rows, 0:512], ones_sb[0:32, 0:rows], aTexp[:, seg:seg + 512],
                    start=True, stop=True), es["tensor"], 1)
                p.wait("vector", es["tensor"])
                p.emit("vector", lambda e, seg=seg, dst_=dst_, rows=rows: e.tensor_copy(
                    dst_[:, seg:seg + 512][0:rows, :], psG[0:rows, 0:512]),
                    es["vector"], 1)
                consumed(psG, es["vector"])
        barrier()

        def issue_loads(j):
            bufi = j % 2
            for hbm, dst in ((qf_hbm, qfm), (kf_hbm, kfm), (vf_hbm, vfm)):
                st_bq.issue("sync", lambda e, hbm=hbm, dst=dst, j=j, bufi=bufi:
                            e.dma_start(
                    out=dst[:, bufi, :, :],
                    in_=hbm[:, CH * j:CH * (j + 1)].rearrange("(c p) t -> p c t", p=64)),
                    j)
            st_bk.issue("sync", lambda e, j=j, bufi=bufi: e.dma_start(
                out=ktmb[:, bufi, :], in_=kf_hbm[:, CH * j:CH * (j + 1)], transpose=True),
                j)
            st_bk.issue("sync", lambda e, j=j, bufi=bufi: e.dma_start(
                out=vtmb[:, bufi, :], in_=vf_hbm[:, CH * j:CH * (j + 1)], transpose=True),
                j)
            st_bo.issue("sync", lambda e, j=j, bufi=bufi: e.dma_start(
                out=otm2[:, bufi, :], in_=otm_hbm[CH * j:CH * (j + 1), :]), j)

        issue_loads(0)
        issue_loads(1)
        comp_marks = {}
        hstore_tok = {0: None, 1: None}
        for j in range(NCH):
            bufi = j % 2
            js = slice(CH * j, CH * (j + 1))
            ldq = st_bq.tok(j)
            ldk = st_bk.tok(j)
            ldo = st_bo.tok(j)
            guard("tensor", psG)
            p.emit("tensor", lambda e, js=js: e.transpose(
                psG[:, 0:H], wloc_sb[:, js], ident_sb[0:H, 0:H]), es["tensor"], 1)
            p.wait("vector", es["tensor"])
            if j >= 1:
                p.wait("vector", es["tensor"], comp_marks[j - 1])
            p.emit("vector", lambda e, j=j: e.tensor_tensor(
                wlT[:, :], psG[:, 0:H],
                brep[:, H * j:H * (j + 1)], ALU.mult), es["vector"], 1)
            consumed(psG, es["vector"])
            vdrain()
            dwait("vector", ldk)
            v3 = vtmb[:, bufi, :].rearrange("p (h x) -> p h x", x=HD)
            p.emit("vector", lambda e, v3=v3: e.tensor_tensor(
                vwb[:, :].rearrange("p (h x) -> p h x", x=HD), v3,
                bc(v3, wlT[:, :].rearrange("p (h o) -> p h o", o=1)), ALU.mult),
                es["vector"], 1)
            p.wait("tensor", es["vector"])
            for h in range(H):
                bank = (psA, psB, psC)[h // 7]
                col = 65 * (h % 7)
                if h in (0, 7, 14):
                    guard("tensor", bank)
                p.emit("tensor", lambda e, h=h, bufi=bufi, bank=bank, col=col: e.matmul(
                    bank[0:HD, col:col + HD], vwb[:, HD * h:HD * (h + 1)],
                    ktmb[:, bufi, HD * h:HD * (h + 1)], start=True, stop=True))
                p.emit("tensor", lambda e, h=h, bufi=bufi, bank=bank, col=col: e.matmul(
                    bank[0:HD, col + HD:col + HD + 1], ktmb[:, bufi, HD * h:HD * (h + 1)],
                    wlT[:, h:h + 1], start=True, stop=True),
                    es["tensor"] if h == H - 1 else None, 1)
            umark = p.cur(es["tensor"])
            p.wait("vector", es["tensor"], umark)
            if j >= 1:
                p.wait("vector", es["tensor"], comp_marks[j - 1])
            rd, wr = j % 2, (j + 1) % 2
            for bi, bank in enumerate((psA, psB, psC)):
                nh = (7, 7, 2)[bi]
                span = 65 * nh
                h0 = 7 * bi
                c3 = cst[:, rd, 65 * h0:65 * h0 + span].rearrange(
                    "p (h x) -> p h x", x=65)
                o3 = cst[:, wr, 65 * h0:65 * h0 + span].rearrange(
                    "p (h x) -> p h x", x=65)
                a3s = bc(c3, arep[:, H * j + h0:H * j + h0 + nh].rearrange(
                    "p (h o) -> p h o", o=1))
                p.emit("vector", lambda e, c3=c3, o3=o3, a3s=a3s: e.tensor_tensor(
                    o3, c3, a3s, ALU.mult))
            vdrain()
            for bi, bank in enumerate((psA, psB, psC)):
                nh = (7, 7, 2)[bi]
                span = 65 * nh
                h0 = 7 * bi
                p.emit("vector", lambda e, bank=bank, h0=h0, span=span, wr=wr:
                       e.tensor_tensor(
                    cst[:, wr, 65 * h0:65 * h0 + span],
                    cst[:, wr, 65 * h0:65 * h0 + span], bank[0:HD, 0:span], ALU.add),
                    es["vector"], 1)
            chmark = p.cur(es["vector"])
            for bank in (psA, psB, psC):
                consumed(bank, es["vector"], chmark)
            vdrain()
            dwait("tensor", ldq)
            for gH in range(2):
                h0 = 8 * gH
                guard("tensor", psD)
                guard("tensor", psE)
                for hl, garr in ((0, ghi), (1, glo)):
                    for seg in range(2):
                        bank = (psD, psE)[seg]
                        p.emit("tensor", lambda e, garr=garr, js=js, gH=gH, seg=seg,
                               bank=bank, hl=hl: e.matmul(
                            bank[:, :], garr[:, js],
                            bdiag16_sb[:, CH * (8 * gH + 4 * seg):CH * (8 * gH + 4 * seg + 4)],
                            start=(hl == 0), stop=False))
                for h8 in range(8):
                    h = h0 + h8
                    bank = (psD, psE)[h8 // 4]
                    colb = CH * (h8 % 4)
                    for hl, Aarr in ((0, Ahi), (1, Alo)):
                        p.emit("tensor", lambda e, h=h, js=js, Aarr=Aarr, bank=bank,
                               colb=colb, hl=hl, h8=h8: e.matmul(
                            bank[:, colb:colb + CH],
                            negoh16_sb[:, CH * h:CH * (h + 1)], Aarr[:, js],
                            start=False, stop=(hl == 1 and h8 % 4 == 3)),
                            es["tensor"] if (hl == 1 and h8 == 7) else None, 1)
                p.wait("scalar", es["tensor"])
                p.emit("scalar", lambda e: e.activation(Dm[:, 0:512], psD[:, :], AF.Exp))
                sdrain()
                p.emit("scalar", lambda e: e.activation(Dm[:, 512:1024], psE[:, :],
                                                        AF.Exp), es["scalar"], 1)
                consumed(psD, es["scalar"])
                consumed(psE, es["scalar"])
                p.wait("vector", es["scalar"])
                p.emit("vector", lambda e, gH=gH: e.tensor_tensor(
                    Dm[:, :], Dm[:, :], tri01_sb[:, CH * 8 * gH:CH * 8 * (gH + 1)],
                    ALU.mult), es["vector"], 1)
                dmark = p.cur(es["vector"])
                vdrain()
                guard("tensor", psF)
                guard("tensor", psG)
                p.wait("tensor", es["vector"], dmark)
                for h8 in range(8):
                    h = h0 + h8
                    bank = (psF, psG)[h8 // 4]
                    colb = CH * (h8 % 4)
                    p.emit("tensor", lambda e, h=h, bufi=bufi, bank=bank, colb=colb:
                           e.matmul(
                        bank[:, colb:colb + CH], vfm[:, bufi, h, :],
                        qfm[:, bufi, h, :], start=True, stop=True),
                        es["tensor"] if h8 == 7 else None, 1)
                p.wait("vector", es["tensor"])
                p.emit("vector", lambda e: e.tensor_tensor(
                    DSv[:, 0:512], psF[:, :], Dm[:, 0:512], ALU.mult))
                p.emit("vector", lambda e: e.tensor_tensor(
                    DSv[:, 512:1024], psG[:, :], Dm[:, 512:1024], ALU.mult),
                    es["vector"], 1)
                consumed(psF, es["vector"])
                consumed(psG, es["vector"])
                guard("tensor", psD)
                guard("tensor", psE)
                for h8 in range(8):
                    h = h0 + h8
                    bank = (psD, psE)[h8 // 4]
                    colb = CH * (h8 % 4)
                    p.emit("tensor", lambda e, h=h, bufi=bufi, bank=bank, colb=colb:
                           e.matmul(
                        bank[:, colb:colb + CH], kfm[:, bufi, h, :],
                        qfm[:, bufi, h, :], start=True, stop=True),
                        es["tensor"] if h8 == 7 else None, 1)
                p.wait("vector", es["tensor"])
                p.emit("vector", lambda e: e.tensor_tensor(
                    DSk[:, 0:512], psD[:, :], Dm[:, 0:512], ALU.mult))
                p.emit("vector", lambda e: e.tensor_tensor(
                    DSk[:, 512:1024], psE[:, :], Dm[:, 512:1024], ALU.mult),
                    es["vector"], 1)
                consumed(psD, es["vector"])
                consumed(psE, es["vector"])
                dsmark = p.cur(es["vector"])
                guard("tensor", psH)
                guard("tensor", psA)
                guard("tensor", psB)
                p.wait("tensor", es["vector"], dsmark)
                for h8 in range(8):
                    h = h0 + h8
                    p.emit("tensor", lambda e, h8=h8, h=h, bufi=bufi: e.matmul(
                        psH[:, HD * h8:HD * (h8 + 1)], DSv[:, CH * h8:CH * (h8 + 1)],
                        ktmb[:, bufi, HD * h:HD * (h + 1)], start=True, stop=True))
                    bank = (psA, psB)[h8 // 4]
                    p.emit("tensor", lambda e, h8=h8, bank=bank: e.matmul(
                        bank[:, 260 + (h8 % 4):260 + (h8 % 4) + 1],
                        DSk[:, CH * h8:CH * (h8 + 1)], onesb_sb[:, 0:1],
                        start=True, stop=True))
                    p.emit("tensor", lambda e, h8=h8, h=h, bufi=bufi, bank=bank, rd=rd:
                           e.matmul(
                        bank[:, 65 * (h8 % 4):65 * (h8 % 4) + 65],
                        qfm[:, bufi, h, :], cst[0:HD, rd, 65 * h:65 * (h + 1)],
                        start=True, stop=True), es["tensor"] if h8 == 7 else None, 1)
                p.wait("vector", es["tensor"])
                dwait("vector", ldo)
                if hstore_tok[bufi] is not None and gH == 0:
                    dwait("vector", hstore_tok[bufi])
                for bi, bank in enumerate((psA, psB)):
                    i3 = bank[:, 0:260].rearrange("p (h x) -> p h x", x=65)
                    p.emit("vector", lambda e, bank=bank, bi=bi, i3=i3, j=j, h0=h0:
                           e.tensor_tensor(
                        interS[:, 264 * bi:264 * bi + 260].rearrange(
                            "p (h x) -> p h x", x=65), i3,
                        bc(i3, decT_s[:, j, h0 + 4 * bi:h0 + 4 * bi + 4].rearrange(
                            "p (h o) -> p h o", o=1)), ALU.mult))
                    p.emit("vector", lambda e, bank=bank, bi=bi: e.tensor_copy(
                        interS[:, 264 * bi + 260:264 * bi + 264], bank[:, 260:264]),
                        es["vector"], 1)
                consumed(psA, es["vector"])
                consumed(psB, es["vector"])
                vdrain()
                p.emit("vector", lambda e: e.tensor_tensor(
                    numer[:, :].rearrange("p (g h x) -> p g h x", g=2, x=HD),
                    psH[:, :].rearrange("p (g h x) -> p g h x", g=2, x=HD),
                    interS[:, :].rearrange("p (g y) -> p g y", g=2)[:, :, 0:260]
                    .rearrange("p g (h x) -> p g h x", x=65)[:, :, :, 0:HD],
                    ALU.add), es["vector"], 1)
                consumed(psH, es["vector"])
                p.emit("vector", lambda e, gH=gH: e.tensor_tensor(
                    den_sb[:, 0:NE].rearrange("p (g x) -> p g x", g=2),
                    interS[:, :].rearrange("p (g y) -> p g y", g=2)[:, :, 260:264],
                    interS[:, :].rearrange("p (g y) -> p g y", g=2)[:, :, 0:260]
                    .rearrange("p g (h x) -> p g h x", x=65)[:, :, :, HD],
                    ALU.add))
                p.emit("vector", lambda e: e.scalar_tensor_tensor(
                    nqr_sb[:, 0:NE].rearrange("p (g x) -> p g x", g=2),
                    interS[:, :].rearrange("p (g y) -> p g y", g=2)[:, :, 260:264],
                    -1.0,
                    interS[:, :].rearrange("p (g y) -> p g y", g=2)[:, :, 0:260]
                    .rearrange("p g (h x) -> p g h x", x=65)[:, :, :, HD],
                    ALU.mult, ALU.subtract))
                vdrain()
                p.emit("vector", lambda e, gH=gH, bufi=bufi: e.tensor_tensor(
                    numer[:, :], numer[:, :], otm2[:, bufi, 512 * gH:512 * (gH + 1)],
                    ALU.mult))
                p.emit("vector", lambda e: e.tensor_tensor(
                    den_sb[:, 0:NE], den_sb[:, 0:NE], nqr_sb[:, 0:NE], ALU.max))
                vdrain()
                p.emit("vector", lambda e: e.tensor_scalar(
                    den_sb[:, 0:NE], den_sb[:, 0:NE], 1.0, None, ALU.max))
                vdrain()
                p.emit("vector", lambda e: e.reciprocal(nqr_sb[:, 0:NE], den_sb[:, 0:NE]))
                vdrain()
                n3 = numer[:, :].rearrange("p (h x) -> p h x", x=HD)
                p.emit("vector", lambda e, n3=n3, gH=gH, bufi=bufi: e.tensor_tensor(
                    h2[:, bufi, 512 * gH:512 * (gH + 1)].rearrange(
                        "p (h x) -> p h x", x=HD), n3,
                    bc(n3, nqr_sb[:, 0:NE].rearrange("p (h o) -> p h o", o=1)),
                    ALU.mult), es["vector"], 1)
            comp_marks[j] = p.cur(es["tensor"])
            hv = mark("vector")
            p.wait("sync", es["vector"], hv)
            hstore_tok[bufi] = st_bh.issue("sync", lambda e, j=j, bufi=bufi: e.dma_start(
                out=hat_hbm[CH * j:CH * (j + 1), :], in_=h2[:, bufi, :]), bufi)
            if j + 2 < NCH:
                p.wait("sync", es["tensor"], comp_marks[j])
                issue_loads(j + 2)
        dwait("sync", st_bh.tok(0))
        dwait("sync", st_bh.tok(1))

    ag_tok = {}

    def ag_emit(key, tok, t_in, t_out):
        dwait("gpsimd", tok)
        ag_tok[key] = p.emit("gpsimd", lambda e, t_in=t_in, t_out=t_out:
                             e.collective_compute(
            "AllGather", ALU.bypass, replica_groups=[list(range(NC))],
            ins=[t_in], outs=[t_out]), cc, 1)


    def f_c(scope, wcols, flrep):
        sbC = lambda name, shape, dt: scope.enter_context(nc.sbuf_tensor(name, shape, dt))
        w1sb = sbC("w1sb", [128, 8, MID], BF16)
        w2sb = sbC("w2sb", [128, 32, D], BF16)
        xg2 = sbC("xg2", [128, 8, 512], BF16)
        hmid = sbC("hmid", [128, 32, 512], BF16)
        sout = sbC("sout", [128, 2, D], BF16)
        ids16d = sbC("ids16d", [16, 2048], I16)
        idxq = sbC("idxq", [16, 2048], I16)
        dstq = sbC("dstq", [16, 1024], I16)
        mq = sbC("mq", [16, 512], F32)
        idsrep = sbC("idsrep", [128, 256], I16)
        fi16 = sbC("fi16", [128, 256], I16)
        idsr_f = mq[0:16, 0:256]
        cnts8 = sbC("cnts8", [8, 8], F32)
        big_a = sbC("big_a", [128, 32, NE], F32)
        big_b = sbC("big_b", [128, 32, NE], F32)
        sp = sbC("sp", [128, 10, 256], F32)
        selbG = sbC("selbG", [128, 256], BF16)
        selbO = selbG
        trow_tot = sp[0:1, 8, :]
        trow_scan = sp[0:1, 9, :]
        trow_excl = sp[0:1, 0, :]
        trow_pref = sp[0:1, 1, 0:8]
        iotaj_sb = sp[0:1, 6, :]
        pcol1_sb = sbC("pcol1_sb", [128, 1], F32)

        for k in range(8):
            st_cw.issue("gpsimd", lambda e, k=k: e.dma_start(
                out=w1sb[:, k, :], in_=w1_h[128 * k:128 * (k + 1), :]))
        for k in range(32):
            st_cw.issue("gpsimd", lambda e, k=k: e.dma_start(
                out=w2sb[:, k, :], in_=w2_h[128 * k:128 * (k + 1), :]))

        st_cr.issue("sync", lambda e: e.dma_start(out=iotaj_sb[:, :], in_=iotaj_h))
        st_cr.issue("sync", lambda e: e.dma_start(out=pcol1_sb[:], in_=pcol1_h))
        p.wait("sync", cc, ag_tok["b"])
        st_cr.issue("sync", lambda e: e.dma_start(
            out=cnts8[:, :].rearrange("c (o e) -> c o e", o=1),
            in_=ag1b_out.rearrange("(c r) d -> c r d", r=SUMROWS)
            [:, SUMROWS - 1:SUMROWS, 32:40]))
        p.wait("sync", cc, ag_tok["c"])
        lsel, cnt, sel = sp[:, 0, :], sp[:, 1, :], sp[:, 2, :]
        rank, iota1, ids = sp[:, 3, :], sp[:, 4, :], sp[:, 5, :]
        lh_tok = {}
        lh_tok[0] = st_cl.issue("sync", lambda e: e.dma_start(
            out=big_a[:, :, :],
            in_=ag1c_out.rearrange("(j p) e -> p j e", p=128)[:, 0:32, :]))
        for jh in range(8):
            if jh >= 1:
                hm = mark("vector")
                p.wait("sync", es["vector"], hm)
                lh_tok[jh] = st_cl.issue("sync", lambda e, jh=jh: e.dma_start(
                    out=big_a[:, :, :],
                    in_=ag1c_out.rearrange("(j p) e -> p j e", p=128)
                    [:, 32 * jh:32 * (jh + 1), :]))
            hs = slice(32 * jh, 32 * (jh + 1))
            dwait("vector", lh_tok[jh])
            l3 = big_a[:, :, :]
            ohv = ohrep_sb[:, :].rearrange("p (o e) -> p o e", o=1)
            p.emit("vector", lambda e, l3=l3, ohv=ohv: e.tensor_tensor(
                big_b[:, :, :], l3, bc(l3, ohv), ALU.mult))
            vdrain()
            p.emit("vector", lambda e, hs=hs: e.reduce_sum(
                lsel[:, hs], big_b[:, :, :], axis=X_AX))
            vdrain()
            lse = lsel[:, hs].rearrange("p (j o) -> p j o", o=1)
            p.emit("vector", lambda e, l3=l3, lse=lse: e.tensor_tensor(
                big_b[:, :, :], l3, bc(l3, lse), ALU.is_gt))
            vdrain()
            p.emit("vector", lambda e, hs=hs: e.reduce_sum(
                cnt[:, hs], big_b[:, :, :], axis=X_AX))
            vdrain()
        p.emit("vector", lambda e: e.tensor_scalar(
            sel[:, :], cnt[:, :], 1.5, None, ALU.is_lt))
        vdrain()
        p.emit("vector", lambda e: e.tensor_scalar(
            sel[:, :], sel[:, :], valid_sb[:, 0:1], None, ALU.mult))
        vdrain()
        p.emit("vector", lambda e: e.tensor_copy(selbG[:, :], sel[:, :]),
               es["vector"], 1)
        p.wait("tensor", es["vector"])
        guard("tensor", psA)
        p.emit("tensor", lambda e: e.matmul(
            psA[:, 0:256], trist_sb[:, :], selbG[:, :], start=True, stop=False))
        guard("tensor", psB)
        p.emit("tensor", lambda e: e.matmul(
            psB[0:1, 0:256], onesb_sb[:, 0:1], selbG[:, :], start=True, stop=True),
            es["tensor"], 1)
        guard("tensor", psC)
        dwait("tensor", st_cr.tok())
        p.emit("tensor", lambda e: e.matmul(
            psC[:, 0:256], ones_sb[0:1, 0:128], iotaj_sb[0:1, :], start=True,
            stop=True), es["tensor"], 1)
        p.wait("vector", es["tensor"])
        p.emit("vector", lambda e: e.tensor_copy(trow_tot[:, :], psB[0:1, 0:256]))
        consumed(psB, es["vector"])
        p.emit("vector", lambda e: e.tensor_scalar(
            iota1[:, :], psC[:, 0:256], pcol1_sb[:, 0:1], None, ALU.add))
        consumed(psC, es["vector"])
        vdrain()
        p.emit("vector", lambda e: e.tensor_tensor_scan(
            trow_scan[:, :], trow_tot[:, :], trow_tot[:, :], 0.0,
            ALU.add, ALU.bypass))
        vdrain()
        p.emit("vector", lambda e: e.memset(trow_excl[:, 0:1], 0.0))
        p.emit("vector", lambda e: e.tensor_copy(
            trow_excl[:, 1:256], trow_scan[:, 0:255]), es["vector"], 1)
        p.wait("tensor", es["vector"])
        p.emit("tensor", lambda e: e.matmul(
            psA[:, 0:256], ones_sb[0:1, 0:128], trow_excl[:, :], start=False,
            stop=True), es["tensor"], 1)
        p.wait("vector", es["tensor"])
        p.emit("vector", lambda e: e.tensor_copy(rank[:, :], psA[:, 0:256]))
        consumed(psA, es["vector"])
        vdrain()
        dbg_dump("rank", rank[:, :])
        dbg_dump("gsel", sel[:, :])
        p.emit("vector", lambda e: e.tensor_scalar(
            cnt[:, :], rank[:, :], float(CAP), None, ALU.is_lt))
        vdrain()
        p.emit("vector", lambda e: e.tensor_tensor(
            sel[:, :], sel[:, :], cnt[:, :], ALU.mult))
        vdrain()
        p.emit("vector", lambda e: e.tensor_tensor(
            ids[:, :], iota1[:, :], sel[:, :], ALU.mult))
        vdrain()
        p.emit("vector", lambda e: e.tensor_scalar(
            ids[:, :], ids[:, :], 1.0, None, ALU.subtract), es["vector"], 1)
        p.wait("tensor", es["vector"])
        swb = (psC, psD, psE, psF)
        for pb in range(8):
            bank = swb[pb // 2]
            if pb % 2 == 0:
                guard("tensor", bank)
            p.emit("tensor", lambda e, bank=bank, pb=pb: e.matmul(
                bank[0:16, 256 * (pb % 2):256 * (pb % 2) + 256],
                ident_sb[:, 16 * pb:16 * (pb + 1)], ids[:, :], start=True,
                stop=True), es["tensor"], 1)
        p.wait("vector", es["tensor"])
        for pb in range(8):
            bank = swb[pb // 2]
            p.emit("vector", lambda e, bank=bank, pb=pb: e.tensor_scalar(
                ids16d[0:16, pb:2048:8],
                bank[0:16, 256 * (pb % 2):256 * (pb % 2) + 256], 1.0, None,
                ALU.mult), es["vector"] if pb % 2 == 1 else None, 1)
            if pb % 2 == 1:
                consumed(bank, es["vector"])
        dbg_dump("tgt16", ids16d[:, :])
        dwait("gpsimd", st_cw.tok())
        p.emit("gpsimd", lambda e: e.load_library(library_config.local_scatter))
        rq, mk_ = sp[:, 4, :], sp[:, 1, :]
        fv = ids
        p.emit("vector", lambda e: e.tensor_scalar(
            rq[:, :], rank[:, :], 0.0625, None, ALU.mult))
        vdrain()
        p.emit("vector", lambda e: e.tensor_scalar(
            fi16[:, :], rq[:, :], 1.0, None, ALU.mult), es["vector"], 1)
        p.wait("scalar", es["vector"])
        p.emit("scalar", lambda e: e.activation(mk_[:, :], fi16[:, :], AF.Copy),
               es["scalar"], 1)
        p.wait("vector", es["scalar"])
        p.emit("vector", lambda e: e.scalar_tensor_tensor(
            rq[:, :], mk_[:, :], -16.0, rank[:, :], ALU.mult, ALU.add))
        vdrain()
        p.emit("vector", lambda e: e.tensor_scalar(
            fv[:, :], rq[:, :], -0.5, None, ALU.is_lt))
        vdrain()
        p.emit("vector", lambda e: e.tensor_tensor(
            mk_[:, :], mk_[:, :], fv[:, :], ALU.subtract))
        p.emit("vector", lambda e: e.scalar_tensor_tensor(
            rq[:, :], fv[:, :], 16.0, rq[:, :], ALU.mult, ALU.add))
        vdrain()
        p.emit("vector", lambda e: e.scalar_tensor_tensor(
            fv[:, :], rq[:, :], 256.0, mk_[:, :], ALU.mult, ALU.add))
        vdrain()
        for q in range(4):
            p.emit("vector", lambda e, q=q: e.tensor_scalar(
                mk_[:, :], fv[:, :], 1024.0 * q - 0.5, None, ALU.is_gt))
            p.emit("vector", lambda e, q=q: e.tensor_scalar(
                rq[:, :], fv[:, :], 1024.0 * q + 1023.5, None, ALU.is_lt))
            vdrain()
            p.emit("vector", lambda e: e.tensor_tensor(
                mk_[:, :], mk_[:, :], rq[:, :], ALU.mult))
            vdrain()
            p.emit("vector", lambda e: e.tensor_tensor(
                mk_[:, :], mk_[:, :], sel[:, :], ALU.mult))
            p.emit("vector", lambda e, q=q: e.tensor_scalar(
                rq[:, :], fv[:, :], -1024.0 * q, None, ALU.add))
            vdrain()
            p.emit("vector", lambda e: e.tensor_tensor(
                rq[:, :], rq[:, :], mk_[:, :], ALU.mult))
            vdrain()
            p.emit("vector", lambda e: e.scalar_tensor_tensor(
                rq[:, :], mk_[:, :], -1.0, rq[:, :], ALU.add, ALU.add),
                es["vector"], 1)
            p.wait("tensor", es["vector"])
            for pb in range(8):
                bank = swb[pb // 2]
                if pb % 2 == 0:
                    guard("tensor", bank)
                p.emit("tensor", lambda e, bank=bank, pb=pb: e.matmul(
                    bank[0:16, 256 * (pb % 2):256 * (pb % 2) + 256],
                    ident_sb[:, 16 * pb:16 * (pb + 1)], rq[:, :], start=True,
                    stop=True), es["tensor"], 1)
            p.wait("vector", es["tensor"])
            p.wait("vector", es["gpsimd"])
            for pb in range(8):
                bank = swb[pb // 2]
                p.emit("vector", lambda e, bank=bank, pb=pb: e.tensor_scalar(
                    idxq[0:16, pb:2048:8],
                    bank[0:16, 256 * (pb % 2):256 * (pb % 2) + 256], 1.0, None,
                    ALU.mult), es["vector"] if pb % 2 == 1 else None, 1)
                if pb % 2 == 1:
                    consumed(bank, es["vector"])
            iv = mark("vector")
            p.wait("gpsimd", es["vector"], iv)
            p.wait("gpsimd", es["scalar"])
            p.emit("gpsimd", lambda e: e.local_scatter(
                dstq[:, :], ids16d[:, :], idxq[:, :], channels=16,
                num_elems=1024, num_idxs=2048), es["gpsimd"], 1)
            for hq in range(2):
                p.wait("scalar", es["gpsimd"])
                if hq == 1:
                    p.wait("scalar", es["tensor"])
                p.emit("scalar", lambda e, hq=hq: e.activation(
                    mq[:, :], dstq[:, 512 * hq:512 * (hq + 1)], AF.Copy),
                    es["scalar"], 1)
                p.wait("tensor", es["scalar"])
                guard("tensor", psG)
                p.emit("tensor", lambda e: e.matmul(
                    psG[0:1, 0:512], ones_sb[0:16, 0:1],
                    mq[0:16, 0:512], start=True, stop=True),
                    es["tensor"], 1)
                p.wait("vector", es["tensor"])
                p.emit("vector", lambda e: e.tensor_copy(
                    sp[0:1, 8:10, :].rearrange("o a b -> o (a b)"), psG[0:1, 0:512]),
                    es["vector"], 1)
                consumed(psG, es["vector"])
                mv = p.cur(es["vector"])
                p.wait("sync", es["vector"], mv)
                st_cr.issue("sync", lambda e, q=q, hq=hq: e.dma_start(
                    out=lmrow_h[2 * q + hq:2 * q + hq + 1, :],
                    in_=sp[0:1, 8:10, :].rearrange("o a b -> o (a b)")))
                dwait("vector", st_cr.tok())
        dwait("sync", st_cr.tok())
        st_cr.issue("sync", lambda e: e.dma_start(
            out=idsr_f[:, :],
            in_=lmrow_h.rearrange("a b -> (a b)").rearrange("(p c) -> p c", p=16)))
        dwait("vector", st_cr.tok())
        p.emit("vector", lambda e: e.tensor_scalar(
            idsrep[0:16, :], idsr_f[:, :], 1.0, None, ALU.mult), es["vector"], 1)
        dbg_dump("sidf", idsr_f[:, :])
        sgv = mark("vector")
        p.wait("sync", es["vector"], sgv)
        for rp in (16, 32, 64):
            dwait("sync", st_cr.tok())
            st_cr.issue("sync", lambda e, rp=rp: e.dma_start(
                out=idsrep[rp:2 * rp, :], in_=idsrep[0:rp, :]))
        p.emit("gpsimd", lambda e: e.load_library(library_config.mlp))

        ownrank, ownkeep, wsm = sp[:, 3, :], sp[:, 1, :], sp[:, 4, :]
        ocols, fla, flb = sp[:, 0, :], sp[:, 5, 0:NCH], sp[:, 2, 0:NCH]
        runf, runl = sp[:, 6, 0:NCH], sp[:, 7, 0:NCH]
        tmpA, tmpB = sp[:, 8, :], sp[:, 9, :]
        dwait("tensor", st_cr.tok())
        guard("tensor", psF)
        p.emit("tensor", lambda e: e.matmul(
            psF[0:1, 0:NE], prevall_sb[:, :], cnts8[:, :], start=True, stop=True),
            es["tensor"], 1)
        p.wait("vector", es["tensor"])
        p.emit("vector", lambda e: e.tensor_copy(trow_pref[:, :], psF[0:1, 0:NE]))
        consumed(psF, es["vector"])
        p.emit("vector", lambda e: e.tensor_copy(selbO[:, :], ownsel_sb[:, :]),
               es["vector"], 1)
        p.wait("tensor", es["vector"])
        guard("tensor", psG)
        p.emit("tensor", lambda e: e.matmul(
            psG[:, 0:256], trist_sb[:, :], selbO[:, :], start=True, stop=False))
        guard("tensor", psH)
        p.emit("tensor", lambda e: e.matmul(
            psH[0:1, 0:256], onesb_sb[:, 0:1], selbO[:, :], start=True, stop=True),
            es["tensor"], 1)
        p.wait("vector", es["tensor"])
        p.emit("vector", lambda e: e.tensor_copy(trow_tot[:, :], psH[0:1, 0:256]))
        consumed(psH, es["vector"])
        vdrain()
        for ep in range(NE):
            p.emit("vector", lambda e, ep=ep: e.tensor_tensor_scan(
                trow_scan[:, ep:256:NE], trow_tot[:, ep:256:NE],
                trow_tot[:, ep:256:NE], 0.0, ALU.add, ALU.bypass))
        vdrain()
        p.emit("vector", lambda e: e.memset(trow_excl[:, 0:8], 0.0))
        p.emit("vector", lambda e: e.tensor_copy(
            trow_excl[:, 8:256], trow_scan[:, 0:248]))
        vdrain()
        ov = trow_excl[:, :].rearrange("o (j e) -> o j e", e=NE)
        pv = trow_pref[:, :].rearrange("o (x e) -> o x e", x=1)
        p.emit("vector", lambda e, ov=ov, pv=pv: e.tensor_tensor(
            ov, ov, bc(ov, pv), ALU.add), es["vector"], 1)
        p.wait("tensor", es["vector"])
        p.emit("tensor", lambda e: e.matmul(
            psG[:, 0:256], ones_sb[0:1, 0:128], trow_excl[:, :], start=False,
            stop=True), es["tensor"], 1)
        p.wait("vector", es["tensor"])
        p.emit("vector", lambda e: e.tensor_copy(ownrank[:, :], psG[:, 0:256]))
        consumed(psG, es["vector"])
        vdrain()
        p.emit("vector", lambda e: e.tensor_scalar(
            ownkeep[:, :], ownrank[:, :], float(CAP), None, ALU.is_lt))
        vdrain()
        p.emit("vector", lambda e: e.tensor_tensor(
            ownkeep[:, :], ownkeep[:, :], ownsel_sb[:, :], ALU.mult))
        vdrain()
        l3o = lownT_sb[:, :, :]
        w3 = wsm[:, :].rearrange("p (j e) -> p j e", e=NE)
        p.emit("vector", lambda e: e.reduce_max(
            ocols[:, 0:NCH], lownT_sb[:, :, :], axis=X_AX))
        vdrain()
        mcv = ocols[:, 0:NCH].rearrange("p (j o) -> p j o", o=1)
        p.emit("vector", lambda e, w3=w3, l3o=l3o, mcv=mcv: e.tensor_tensor(
            w3, l3o, bc(l3o, mcv), ALU.subtract), es["vector"], 1)
        p.wait("scalar", es["vector"])
        p.emit("scalar", lambda e: e.activation(wsm[:, :], wsm[:, :], AF.Exp),
               es["scalar"], 1)
        p.wait("vector", es["scalar"])
        p.emit("vector", lambda e, w3=w3: e.reduce_sum(
            ocols[:, 32:32 + NCH], w3, axis=X_AX))
        vdrain()
        p.emit("vector", lambda e: e.reciprocal(
            ocols[:, 64:64 + NCH], ocols[:, 32:32 + NCH]))
        vdrain()
        rcv = ocols[:, 64:64 + NCH].rearrange("p (j o) -> p j o", o=1)
        p.emit("vector", lambda e, w3=w3, rcv=rcv: e.tensor_tensor(
            w3, w3, bc(w3, rcv), ALU.mult))
        vdrain()
        p.emit("vector", lambda e: e.tensor_tensor(
            wcols[:, 2, :], wsm[:, 7:256:NE], ownsel_sb[:, 7:256:NE], ALU.mult))
        for arr in (runf, runl, wcols[:, 0, :], wcols[:, 1, :], fla, flb):
            p.emit("vector", lambda e, arr=arr: e.memset(arr, 0.0))
        vdrain()
        for pas, rng, run, wacc, flacc in (
                (0, range(7), runf, wcols[:, 0, :], fla),
                (1, range(6, -1, -1), runl, wcols[:, 1, :], flb)):
            for ep in rng:
                kv = ownkeep[:, ep:256:NE]
                p.emit("vector", lambda e, run=run: e.tensor_scalar(
                    tmpA[:, 0:NCH], run, 0.5, None, ALU.is_lt))
                vdrain()
                p.emit("vector", lambda e, kv=kv: e.tensor_tensor(
                    tmpA[:, 0:NCH], tmpA[:, 0:NCH], kv, ALU.mult))
                p.emit("vector", lambda e, ep=ep: e.tensor_scalar(
                    tmpB[:, 0:NCH], ownrank[:, ep:256:NE], float(CAP * ep), None,
                    ALU.add))
                vdrain()
                p.emit("vector", lambda e, run=run, kv=kv: e.tensor_tensor(
                    run, run, kv, ALU.add))
                p.emit("vector", lambda e: e.tensor_tensor(
                    tmpB[:, 0:NCH], tmpB[:, 0:NCH], tmpA[:, 0:NCH], ALU.mult))
                vdrain()
                p.emit("vector", lambda e, ep=ep: e.tensor_tensor(
                    tmpA[:, 0:NCH], tmpA[:, 0:NCH], wsm[:, ep:256:NE], ALU.mult))
                vdrain()
                p.emit("vector", lambda e, flacc=flacc: e.tensor_tensor(
                    flacc, flacc, tmpB[:, 0:NCH], ALU.add))
                p.emit("vector", lambda e, wacc=wacc: e.tensor_tensor(
                    wacc, wacc, tmpA[:, 0:NCH], ALU.add))
                vdrain()
        p.emit("vector", lambda e: e.tensor_tensor(
            tmpA[:, 0:NCH], fla[:, 0:NCH], flb[:, 0:NCH], ALU.is_gt))
        p.emit("vector", lambda e: e.tensor_tensor(
            tmpB[:, 0:NCH], flb[:, 0:NCH], fla[:, 0:NCH], ALU.is_gt))
        vdrain()
        p.emit("vector", lambda e: e.tensor_tensor(
            tmpA[:, 0:NCH], tmpA[:, 0:NCH], tmpB[:, 0:NCH], ALU.add))
        vdrain()
        p.emit("vector", lambda e: e.tensor_tensor(
            wcols[:, 1, :], wcols[:, 1, :], tmpA[:, 0:NCH], ALU.mult),
            es["vector"], 1)
        p.wait("tensor", es["vector"])
        for pas, flv, c0 in ((0, fla, 0), (1, flb, 256)):
            bank = (psF, psG)[pas]
            guard("tensor", bank)
            for pb in range(8):
                p.emit("tensor", lambda e, bank=bank, pb=pb, flv=flv: e.matmul(
                    bank[0:16, 32 * pb:32 * (pb + 1)],
                    ident_sb[:, 16 * pb:16 * (pb + 1)], flv[:, 0:NCH], start=True,
                    stop=True), es["tensor"] if pb == 7 else None, 1)
            p.wait("vector", es["tensor"])
            for pb in range(8):
                p.emit("vector", lambda e, bank=bank, pb=pb, c0=c0: e.tensor_scalar(
                    flrep[0:16, c0 + pb:c0 + 256:8], bank[0:16, 32 * pb:32 * (pb + 1)],
                    1.0, None, ALU.mult), es["vector"] if pb == 7 else None, 1)
            consumed(bank, es["vector"])
        dbg_dump("wcols", wcols[:, 0:3, :])
        dbg_dump("ownrank", ownrank[:, :])
        flv2 = mark("vector")
        p.wait("sync", es["vector"], flv2)
        for rp in (16, 32, 64):
            dwait("sync", st_cr.tok())
            st_cr.issue("sync", lambda e, rp=rp: e.dma_start(
                out=flrep[rp:2 * rp, :], in_=flrep[0:rp, :]))
        if "flrep" in dbg:
            dwait("sync", st_cr.tok())
        dbg_dump("flrep", flrep[:, :])

        p.wait("gpsimd", cc, ag_tok["a"])
        dwait("gpsimd", st_cr.tok())
        banksW2 = (psA, psB, psC, psD, psE, psF, psG, psH)
        hm_mark = {}
        w1done = {}
        sstore = {0: None, 1: None}
        for g in range(8):
            if g >= 1:
                p.wait("gpsimd", es["tensor"], w1done[g - 1])
            st_cg.issue("gpsimd", lambda e, g=g: e.dma_gather(
                xg2[:, :, :], ag1a_out, idsrep[:, 32 * g:32 * (g + 1)],
                num_idxs=512, num_idxs_reg=512, elem_size=D, transpose=True))
            dwait("tensor", st_cg.tok())
            if g == 0:
                dwait("tensor", st_cw.tok())
            for mt in range(32):
                bank = (psA, psB)[mt % 2]
                guard("tensor", bank)
                for k in range(8):
                    p.emit("tensor", lambda e, bank=bank, mt=mt, k=k:
                           e.matmul(bank[:, 0:512],
                                    w1sb[:, k, 128 * mt:128 * (mt + 1)],
                                    xg2[:, k, :], start=(k == 0),
                                    stop=(k == 7)),
                           es["tensor"] if k == 7 else None, 1)
                eng = ("scalar", "vector")[mt % 2]
                p.wait(eng, es["tensor"])
                if eng == "scalar":
                    p.emit("scalar", lambda e, bank=bank, mt=mt: e.activation(
                        hmid[:, mt, :], bank[:, :], AF.Relu), es["scalar"], 1)
                else:
                    p.emit("vector", lambda e, bank=bank, mt=mt: e.tensor_scalar(
                        hmid[:, mt, :], bank[:, :], 0.0, None, ALU.max),
                        es["vector"], 1)
                consumed(bank, es[eng])
                hm_mark[mt] = (es[eng], p.cur(es[eng]))
            w1done[g] = p.cur(es["tensor"])
            for bank in banksW2:
                guard("tensor", bank)
            for k2 in range(32):
                p.wait("tensor", hm_mark[k2][0], hm_mark[k2][1])
                for st in range(4):
                    for seg in range(2):
                        bank = banksW2[2 * st + seg]
                        p.emit("tensor", lambda e, bank=bank, k2=k2, st=st, seg=seg:
                               e.matmul(bank[:, 0:512],
                                        hmid[:, k2, 128 * st:128 * (st + 1)],
                                        w2sb[:, k2, 512 * seg:512 * (seg + 1)],
                                        start=(k2 == 0), stop=(k2 == 31)),
                               es["tensor"] if (k2 == 31 and st == 3 and seg == 1)
                               else None, 1)
            p.wait("vector", es["tensor"])
            for st in range(4):
                sbufi = st % 2
                if sstore[sbufi] is not None:
                    dwait("vector", sstore[sbufi])
                for seg in range(2):
                    bank = banksW2[2 * st + seg]
                    p.emit("vector", lambda e, bank=bank, sbufi=sbufi, seg=seg:
                           e.tensor_copy(sout[:, sbufi, 512 * seg:512 * (seg + 1)],
                                         bank[:, :]), es["vector"], 1)
                    consumed(bank, es["vector"])
                p.wait("sync", es["vector"])
                sstore[sbufi] = st_cs.issue("sync", lambda e, g=g, st=st, sbufi=sbufi:
                                            e.dma_start(
                    out=ag2_in[512 * g + 128 * st:512 * g + 128 * (st + 1), :],
                    in_=sout[:, sbufi, :]), sbufi)
        dwait("gpsimd", st_cs.tok(0))
        dwait("gpsimd", st_cs.tok(1))
        ag_tok["2"] = p.emit("gpsimd", lambda e: e.collective_compute(
            "AllGather", ALU.bypass, replica_groups=[list(range(NC))],
            ins=[ag2_in], outs=[ag2_out]), cc, 1)

    def f_d(scope, wcols, flrep):
        sbD = lambda name, shape, dt: scope.enter_context(nc.sbuf_tensor(name, shape, dt))
        x2 = sbD("d_x2", [128, 2, 4, D], F32)
        hat2 = sbD("d_hat2", [128, 2, 4, D], BF16)
        xn2 = sbD("d_xn2", [128, 2, 4, D], BF16)
        rowa2 = sbD("d_rowa", [128, 2, 4, D], BF16)
        rowb2 = sbD("d_rowb", [128, 2, 4, D], BF16)
        hatf = sbD("d_hatf", [128, 4, D], F32)
        cen = sbD("d_cen", [128, 4, D], F32)
        outb = sbD("d_out", [128, 2, 4, D], F32)
        gamr = sbD("d_gamr", [1, D], F32)
        betr = sbD("d_betr", [1, D], F32)
        gamrep = sbD("d_gamrep", [128, D], F32)
        betrep = sbD("d_betrep", [128, D], F32)
        dcol = sbD("d_col", [128, 4, 4], F32)
        st_dc.issue("sync", lambda e: e.dma_start(out=gamr[:], in_=gam_h))
        st_dc.issue("sync", lambda e: e.dma_start(out=betr[:], in_=bet_h))
        dwait("tensor", st_dc.tok())
        reps = ((gamr, gamrep, psA, psB), (betr, betrep, psC, psD))
        for src, dst, b0, b1 in reps:
            for half in range(2):
                bank = (b0, b1)[half]
                guard("tensor", bank)
                p.emit("tensor", lambda e, bank=bank, src=src, half=half: e.matmul(
                    bank[:, 0:512], ones_sb[0:1, 0:128],
                    src[0:1, 512 * half:512 * (half + 1)], start=True, stop=True),
                    es["tensor"], 1)
        p.wait("vector", es["tensor"])
        for src, dst, b0, b1 in reps:
            for half in range(2):
                bank = (b0, b1)[half]
                p.emit("vector", lambda e, bank=bank, dst=dst, half=half:
                       e.tensor_copy(dst[:, 512 * half:512 * (half + 1)],
                                     bank[:, :]), es["vector"], 1)
                consumed(bank, es["vector"])
        p.wait("gpsimd", cc, ag_tok["2"])
        dcomp = {}
        hread = {}
        for g in range(8):
            bufi = g % 2
            rows = slice(512 * g, 512 * (g + 1))
            if g >= 2:
                for sem_, v_ in dcomp[g - 2]:
                    p.wait("sync", sem_, v_)
                    p.wait("gpsimd", sem_, v_)
            st_dld.issue("sync", lambda e, rows=rows, bufi=bufi: e.dma_start(
                out=x2[:, bufi, :, :],
                in_=x_h[rows, :].rearrange("(r p) d -> p r d", p=128)), bufi)
            st_dld.issue("sync", lambda e, rows=rows, bufi=bufi: e.dma_start(
                out=hat2[:, bufi, :, :],
                in_=hat_hbm[rows, :].rearrange("(r p) d -> p r d", p=128)), bufi)
            st_dld.issue("sync", lambda e, rows=rows, bufi=bufi: e.dma_start(
                out=xn2[:, bufi, :, :],
                in_=ag1a_in[rows, :].rearrange("(r p) d -> p r d", p=128)), bufi)
            st_dg.issue("gpsimd", lambda e, g=g, bufi=bufi: e.dma_gather(
                rowa2[:, bufi, :, :], ag2_out, flrep[:, 32 * g:32 * (g + 1)],
                num_idxs=512, num_idxs_reg=512, elem_size=D), bufi, GINC)
            st_dg.issue("gpsimd", lambda e, g=g, bufi=bufi: e.dma_gather(
                rowb2[:, bufi, :, :], ag2_out, flrep[:, 256 + 32 * g:256 + 32 * (g + 1)],
                num_idxs=512, num_idxs_reg=512, elem_size=D), bufi, GINC)
            dwait("scalar", st_dld.tok(bufi))
            if g >= 1:
                p.wait("scalar", es["vector"], hread[g - 1])
            for r in range(4):
                p.emit("scalar", lambda e, r=r, bufi=bufi: e.activation(
                    sqdump[:, :], hat2[:, bufi, r, :], AF.Copy,
                    accum_out=dcol[:, 0, r:r + 1]))
                sdrain()
            for r in range(4):
                p.emit("scalar", lambda e, r=r, bufi=bufi: e.activation(
                    sqdump[:, :], hat2[:, bufi, r, :], AF.Square,
                    accum_out=dcol[:, 1, r:r + 1]))
                sdrain()
            smk = mark("scalar")
            p.wait("vector", es["scalar"], smk)
            p.emit("vector", lambda e: e.tensor_scalar(
                dcol[:, 0, :], dcol[:, 0, :], 1.0 / D, None, ALU.mult))
            p.emit("vector", lambda e: e.tensor_scalar(
                dcol[:, 1, :], dcol[:, 1, :], 1.0 / D, None, ALU.mult))
            vdrain()
            p.emit("vector", lambda e: e.tensor_tensor(
                dcol[:, 2, :], dcol[:, 0, :], dcol[:, 0, :], ALU.mult))
            vdrain()
            p.emit("vector", lambda e: e.tensor_tensor(
                dcol[:, 1, :], dcol[:, 1, :], dcol[:, 2, :], ALU.subtract))
            vmk = mark("vector")
            p.wait("scalar", es["vector"], vmk)
            p.emit("scalar", lambda e: e.activation(
                dcol[:, 3, :], dcol[:, 1, :], AF.Ln, bias=eps_col[:]))
            sdrain()
            p.emit("scalar", lambda e: e.activation(
                dcol[:, 3, :], dcol[:, 3, :], AF.Exp, scale=-0.5))
            smk1 = mark("scalar")
            p.wait("vector", es["scalar"], smk1)
            p.emit("vector", lambda e: e.tensor_tensor(
                dcol[:, 2, :], dcol[:, 0, :], dcol[:, 3, :], ALU.mult),
                es["vector"], 1)
            vmk2 = p.cur(es["vector"])
            p.wait("scalar", es["vector"], vmk2)
            for r in range(4):
                p.emit("scalar", lambda e, r=r, bufi=bufi: e.activation(
                    hatf[:, r, :], hat2[:, bufi, r, :], AF.Copy,
                    scale=dcol[:, 3, r:r + 1]))
            smk2 = mark("scalar")
            p.wait("vector", es["scalar"], smk2)
            dwait("vector", st_dld.tok(bufi))
            dwait("vector", st_dg.tok(bufi))
            dwait("vector", st_out.tok(bufi))
            for r in range(4):
                p.emit("vector", lambda e, r=r: e.scalar_tensor_tensor(
                    cen[:, r, :], hatf[:, r, :], dcol[:, 2, r:r + 1],
                    gamrep[:, :], ALU.subtract, ALU.mult),
                    es["vector"] if r == 3 else None, 1)
            hread[g] = p.cur(es["vector"])
            vdrain()
            for r in range(4):
                p.emit("vector", lambda e, r=r: e.tensor_tensor(
                    cen[:, r, :], cen[:, r, :], betrep[:, :], ALU.add))
            vdrain()
            for r in range(4):
                p.emit("vector", lambda e, r=r, bufi=bufi: e.tensor_tensor(
                    outb[:, bufi, r, :], cen[:, r, :], x2[:, bufi, r, :], ALU.add))
            vdrain()
            for src2, wi in ((rowa2, 0), (rowb2, 1), (xn2, 2)):
                for r in range(4):
                    p.emit("vector", lambda e, src2=src2, wi=wi, r=r, bufi=bufi,
                           g=g: e.scalar_tensor_tensor(
                        outb[:, bufi, r, :], src2[:, bufi, r, :],
                        wcols[:, wi, 4 * g + r:4 * g + r + 1],
                        outb[:, bufi, r, :], ALU.mult, ALU.add))
                vdrain()
            dmk = mark("vector")
            p.wait("sync", es["vector"], dmk)
            st_out.issue("sync", lambda e, rows=rows, bufi=bufi: e.dma_start(
                out=out_h[rows, :].rearrange("(r p) d -> p r d", p=128),
                in_=outb[:, bufi, :, :]), bufi)
            dcomp[g] = ((es["vector"], dmk), (es["scalar"], smk2))
        dwait("sync", st_out.tok(0))
        dwait("sync", st_out.tok(1))

    f_init()
    scope_a0 = ExitStack()
    a_tok = f_a0(scope_a0)
    ag_emit("a", a_tok, ag1a_in, ag1a_out)
    barrier()
    scope_a0.close()
    scope_a1 = ExitStack()
    c_tok = f_a1(scope_a1)
    ag_emit("c", c_tok, ag1c_in, ag1c_out)
    barrier()
    scope_a1.close()
    scope1.close()
    scope_a2 = ExitStack()
    b_tok = f_a2(scope_a2)
    ag_emit("b", b_tok, ag1b_in, ag1b_out)
    barrier()
    scope_a2.close()
    scope_b = ExitStack()
    f_b(scope_b)
    barrier()
    scope_b.close()
    scope_ab.close()

    for gname, gap in (("g", g_sb), ("aloc", aloc_sb), ("omg", omg_sb),
                       ("gi_i", gi_i), ("gi_f", gi_f)):
        if gname in dbg:
            dbg_h[gname] = dram("dbg_" + gname, [H, TL], F32, "ExternalOutput")
            p.wait("sync", es["vector"])
            p.wait("sync", es["scalar"])
            st_dbg.issue("sync", lambda e, gap=gap, gname=gname: e.dma_start(
                out=dbg_h[gname], in_=gap))
    dbg2 = tuple(x for x in dbg if x in scratch_map)
    for name in dbg2:
        src = scratch_map[name]
        dbg_h[name] = dram("dbg_" + name, list(src.shape), src.tensor.dtype,
                           "ExternalOutput")
        st_dbgg.issue("gpsimd", lambda e, s=src, d=dbg_h[name]: e.dma_start(out=d, in_=s))
    if dbg:
        dwait("gpsimd", st_dbgg.tok())
        dwait("sync", st_dbg.tok())

    scope_cd = ExitStack()
    wcols = scope_cd.enter_context(nc.sbuf_tensor("wcols", [128, 4, NCH], F32))
    flrep = scope_cd.enter_context(nc.sbuf_tensor("flrep", [128, 512], I16))
    scope_c = ExitStack()
    f_c(scope_c, wcols, flrep)
    barrier()
    scope_c.close()
    scope_d = ExitStack()
    f_d(scope_d, wcols, flrep)
    barrier()
    scope_d.close()
    scope_cd.close()

    with nc.Block() as block:
        @block.sync
        def _(sync):
            for fn in p.q["sync"]:
                fn(sync)

        @block.tensor
        def _(tensor):
            for fn in p.q["tensor"]:
                fn(tensor)

        @block.vector
        def _(vector):
            for fn in p.q["vector"]:
                fn(vector)

        @block.scalar
        def _(scalar):
            for fn in p.q["scalar"]:
                fn(scalar)

        @block.gpsimd
        def _(gpsimd):
            for fn in p.q["gpsimd"]:
                fn(gpsimd)

    ctx.close()
    mybir.codegen_inst_isa_subclasses(nc)
    return nc


def make_consts():
    ident = np.eye(128, dtype=np.float32)
    trist = np.triu(np.ones((128, 128), np.float32), 1).astype(NPBF)
    base = np.tril(np.ones((CH, CH), np.float32)).T
    tri01 = np.ascontiguousarray(np.concatenate([base] * H, axis=1)).astype(NPBF)
    bdiag16 = np.zeros((H, H * CH), np.float32)
    negoh16 = np.zeros((H, H * CH), np.float32)
    for h in range(H):
        bdiag16[h, h * CH:(h + 1) * CH] = 1.0
        negoh16[h, h * CH:(h + 1) * CH] = -1.0
    bdiag32 = np.zeros((32, 32 * H), np.float32)
    for j in range(32):
        bdiag32[j, j * H:(j + 1) * H] = 1.0
    iotaj = (np.arange(256, dtype=np.float32) * 128.0).reshape(1, 256)
    pcol1 = (np.arange(128, dtype=np.float32) + 1.0).reshape(128, 1)
    return dict(ident=ident, tristb=trist, tri01=tri01,
                bdiag16=bdiag16.astype(NPBF), negoh16=negoh16.astype(NPBF),
                bdiag32=bdiag32, iotaj=iotaj, pcol1=pcol1)


def make_inmaps(inputs):
    x = np.ascontiguousarray(inputs["L0_attn_x"])
    consts = make_consts()
    wif = np.ascontiguousarray(np.concatenate(
        [inputs["L0_attn_w_i"], inputs["L0_attn_w_f"]], 1))
    bif = np.concatenate([inputs["L0_attn_b_i"], inputs["L0_attn_b_f"]]).reshape(2 * H, 1)
    rwt = np.ascontiguousarray(inputs["L0_ffn_router_w"].T)
    rb = np.ascontiguousarray(inputs["L0_ffn_router_b"].reshape(NE, 1))
    in_maps = []
    for c in range(NC):
        b = c // (NC // B)
        q = c % (NC // B)
        ohrep = np.zeros((128, NE), np.float32)
        valid = np.zeros((128, 1), np.float32)
        if c < 7:
            ohrep[:, c] = 1.0
            valid[:] = 1.0
        prevmask = np.zeros((NC, 1), np.float32)
        prevmask[(b * (NC // B)):c, 0] = 1.0
        prevmask16 = np.tile(prevmask.T, (H, 1))
        prevall = np.zeros((NC, 1), np.float32)
        prevall[:c, 0] = 1.0
        ec = min(c, 6)
        in_maps.append(dict(
            x=np.ascontiguousarray(x[b, q * TL:(q + 1) * TL]),
            wq=inputs["L0_attn_W_q"], wk=inputs["L0_attn_W_k"],
            wv=inputs["L0_attn_W_v"], wo=inputs["L0_attn_W_o"],
            bo=np.ascontiguousarray(inputs["L0_attn_b_o"].reshape(1, D)),
            wif=wif, bif=bif, rwt=rwt, rb=rb,
            gam=np.ascontiguousarray(inputs["L0_attn_gamma"].reshape(1, D)),
            bet=np.ascontiguousarray(inputs["L0_attn_beta"].reshape(1, D)),
            w1=np.ascontiguousarray(inputs["L0_ffn_experts_inter"][ec]),
            w2=np.ascontiguousarray(inputs["L0_ffn_experts_out"][ec]),
            ohrep=ohrep, valid=valid, prevmask=prevmask, prevall=prevall,
            prevmask16=prevmask16,
            **consts))
    return in_maps


def kernel(**inputs):
    in_maps = make_inmaps(inputs)
    nc = build_program()
    res = run_bass_kernel_spmd(nc, in_maps, core_ids=list(range(NC)))
    outs = [res.results[c]["out"] for c in range(NC)]
    return np.concatenate(outs, 0).reshape(B, T, D).astype(np.float32)


